# version 13
# baseline (speedup 1.0000x reference)
import numpy as np
import ml_dtypes
from contextlib import ExitStack

import concourse.bass as bass
import concourse.mybir as mybir
from concourse.bass_utils import run_bass_kernel_spmd

F32 = mybir.dt.float32
BF16 = mybir.dt.bfloat16
I32 = mybir.dt.int32
AF = mybir.ActivationFunctionType
OP = mybir.AluOpType
AX = mybir.AxisListType

D = 2048
T = 2048
HN = 16
HD = 128
CC = 128
NCH = T // CC
ND = D // 128
NT = T // 128
NQ = T // 512
II = 8192
NI = II // 512
NIC = II // 128
GLN = 16.0
EPS = 1e-5
SCALE = float(HD) ** -0.5
MAGIC = 12582912.0

ENGINES = ("sync", "tensor", "vector", "scalar", "gpsimd")
DMASEM = {"sync": "dS", "scalar": "dA", "gpsimd": "dG"}


def hap(handle, offset, dims):
    return bass.AP(handle, offset, [list(x) for x in dims])


class KB:

    def __init__(self, nc, dump=()):
        self.nc = nc
        self.dump = set(dump)
        self.prog = {e: [] for e in ENGINES}
        self.cnt = {}
        self.lastwait = {}
        self.S = {}
        self.sem_names = []
        self.guards = {}

    def sem(self, *names):
        for n in names:
            if n not in self.cnt:
                self.cnt[n] = 0
                self.sem_names.append(n)

    OWN = {"vector": "V", "scalar": "A", "gpsimd": "G", "tensor": "T"}

    def do(self, eng, fn, sem=None, n=1):
        if eng in ("vector", "scalar", "gpsimd"):
            own = self.OWN[eng]
            self.wait(eng, own, self.cnt[own])
            if sem is None:
                sem = own
        if sem is None:
            self.prog[eng].append(fn)
            return None
        self.cnt[sem] += n
        v = self.cnt[sem]

        def wrapped(e, f=fn, s=sem, k=n):
            f(e).then_inc(self.S[s], k)
        self.prog[eng].append(wrapped)
        return v

    def wait(self, eng, sem, v):
        if v is None or v <= 0:
            return
        key = (eng, sem)
        if self.lastwait.get(key, -1) >= v:
            return
        self.lastwait[key] = v

        def wrapped(e, s=sem, vv=v):
            e.wait_ge(self.S[s], vv)
        self.prog[eng].append(wrapped)

    def dma(self, eng, dst, src, stream="misc", transpose=False):
        s = "D_" + stream
        self.sem(s)
        self.cnt[s] += 16
        v = self.cnt[s]

        def wrapped(e, dd=dst, sr=src, tr=transpose, ss=s):
            if tr:
                e.dma_start(dd, sr, transpose=True).then_inc(self.S[ss], 16)
            else:
                e.dma_start(dd, sr).then_inc(self.S[ss], 16)
        self.prog[eng].append(wrapped)
        return (s, v)

    def dwait(self, eng, sv):
        self.wait(eng, sv[0], sv[1])

    def guard_wait(self, eng, key):
        ent = self.guards.get(key)
        if ent:
            for s, v in ent:
                self.wait(eng, s, v)

    def guard_set(self, key, sem, val):
        self.guards[key] = [(sem, val)]

    def guard_add(self, key, sem, val):
        self.guards.setdefault(key, []).append((sem, val))

    def barrier(self, engines=ENGINES):
        snap = dict(self.cnt)
        for eng in engines:
            for s, v in snap.items():
                self.wait(eng, s, v)


def build_nc(dump=(), stop_after=None, p3_chunks=NCH, p3_level=9):
    nc = bass.Bass("TRN2", target_bir_lowering=False)
    kb = KB(nc, dump)
    kb.sem("T", "V", "A", "G")

    x_fm = nc.dram_tensor("x_fm", [D, T], F32, kind="ExternalInput")
    c_row = nc.dram_tensor("c_row", [1, D], F32, kind="ExternalInput")
    wmod = nc.dram_tensor("wmod", [D, 6 * D], F32, kind="ExternalInput")
    wq_h = nc.dram_tensor("wq", [D, D], F32, kind="ExternalInput")
    wk_h = nc.dram_tensor("wk", [D, D], F32, kind="ExternalInput")
    wv_h = nc.dram_tensor("wv", [D, D], F32, kind="ExternalInput")
    wg_h = nc.dram_tensor("wg", [D, D], F32, kind="ExternalInput")
    wo_h = nc.dram_tensor("wo", [D, D], F32, kind="ExternalInput")
    wgk1 = nc.dram_tensor("wgk1", [D, 16], F32, kind="ExternalInput")
    wgk2 = nc.dram_tensor("wgk2", [16, D], F32, kind="ExternalInput")
    bgk2 = nc.dram_tensor("bgk2", [1, D], F32, kind="ExternalInput")
    gnw_h = nc.dram_tensor("gnw", [1, HD], F32, kind="ExternalInput")
    wgT = nc.dram_tensor("wgT", [D, II], F32, kind="ExternalInput")
    wuT = nc.dram_tensor("wuT", [D, II], F32, kind="ExternalInput")
    wdT = nc.dram_tensor("wdT", [II, D], F32, kind="ExternalInput")

    y_fm = nc.dram_tensor("y_fm", [D, T], F32, kind="ExternalOutput")

    def scratch(name, shape, dt):
        kind = "ExternalOutput" if name in kb.dump else "Internal"
        return nc.dram_tensor(name, shape, dt, kind=kind)

    sp_q = scratch("sp_q", [NCH, ND, 128, 128], BF16)
    sp_k = scratch("sp_k", [NCH, ND, 128, 128], BF16)
    sp_g = scratch("sp_g", [NCH, ND, 128, 128], BF16)
    sp_v = scratch("sp_v", [T, D], BF16)
    sp_ktm = scratch("sp_ktm", [T, D], BF16)
    sp_sp = scratch("sp_sp", [T, D], BF16)
    sp_op = scratch("sp_op", [D, T], BF16)
    sp_x1 = scratch("sp_x1", [D, T], F32)
    sp_h2 = scratch("sp_h2", [D, T], BF16)
    sp_h2q = scratch("sp_h2q", [T, D], BF16)
    sp_gu = scratch("sp_gu", [T, II], BF16)
    sp_guq = scratch("sp_guq", [T, II], BF16)
    sp_wqd = scratch("sp_wqd", [ND, NIC, 128, 128], BF16)

    dumps = {}
    for nm, shape, dt in [("dmp_mod", [1, 6 * D], F32), ("dmp_h", [D, T], BF16),
                          ("dmp_o", [D, T], F32), ("dmp_ws", [1, 8], F32),
                          ("dmp_wsr", [128, 8], F32)]:
        if nm in kb.dump:
            dumps[nm] = nc.dram_tensor(nm, shape, dt, kind="ExternalOutput")

    with ExitStack() as ctx:
        E = ctx.enter_context

        identB = E(nc.sbuf_tensor("identB", [128, 128], BF16))
        identF = E(nc.sbuf_tensor("identF", [128, 128], F32))
        U_bf = E(nc.sbuf_tensor("U_bf", [128, 128], BF16))
        U4 = E(nc.sbuf_tensor("U4", [128, 512], BF16))
        gnwsel = E(nc.sbuf_tensor("gnwsel", [16, HN * 128], BF16))
        iota_i = E(nc.sbuf_tensor("iota_i", [128, 128], I32))
        cg = E(nc.sbuf_tensor("cg", [128, 16 * HN], BF16))
        ones1 = E(nc.sbuf_tensor("ones1", [1, 1], F32))
        onesrow = E(nc.sbuf_tensor("onesrow", [1, 512], F32))
        sumcolF = E(nc.sbuf_tensor("sumcolF", [128, 1], BF16))
        gn_col = E(nc.sbuf_tensor("gn_col", [128, 1], BF16))
        dcols = E(nc.sbuf_tensor("dcols", [128, 8 * ND], F32))
        gnw_row = E(nc.sbuf_tensor("gnw_row", [1, HD], BF16))
        gnw_f32 = E(nc.sbuf_tensor("gnw_f32", [1, HD], F32))
        sc_cols = E(nc.sbuf_tensor("sc_cols", [128, ND], BF16))
        statbuf = [None]
        s_all = E(nc.sbuf_tensor("s_all", [128, NT], F32))
        srecip = E(nc.sbuf_tensor("srecip", [128, NT], F32))
        gmax = E(nc.sbuf_tensor("gmax", [128, NT], F32))
        s2r = E(nc.sbuf_tensor("s2r", [128, NT], F32))

        wsum = E(nc.sbuf_tensor("wsum", [128, 8], F32))
        wsr_c = E(nc.sbuf_tensor("wsr_c", [128, 4], F32))
        wsr1 = [E(nc.sbuf_tensor(f"wsr1_{i}", [128, 1], F32)) for i in range(3)]
        srq = [E(nc.sbuf_tensor(f"srq{i}", [128, 1], F32)) for i in range(2)]
        s2q = [E(nc.sbuf_tensor(f"s2q{i}", [128, 1], F32)) for i in range(2)]
        sgcol = E(nc.sbuf_tensor("sgcol", [128, 2 * NT], F32))
        ws_row = E(nc.sbuf_tensor("ws_row", [1, 8], F32))
        stg = E(nc.sbuf_tensor("stg", [128, 4 * 512], BF16))
        stg_f = E(nc.sbuf_tensor("stg_f", [128, 2 * 512], F32))
        ln_xbf = E(nc.sbuf_tensor("ln_xbf", [128, 2 * 512], BF16))
        ln_x2bf = E(nc.sbuf_tensor("ln_x2bf", [128, 2 * 512], BF16))
        ln_t1 = E(nc.sbuf_tensor("ln_t1", [128, 512], F32))
        ln_t4 = E(nc.sbuf_tensor("ln_t4", [128, 512], F32))

        ps = [E(nc.psum_tensor(f"ps{i}", [128, 512], F32)) for i in range(6)]
        psb = [E(nc.psum_tensor(f"psb{i}", [128, 1024], BF16)) for i in range(2)]

        def dcol(name, m):
            idx = {"sa": 0, "ca1": 1, "nca1": 2, "ga2": 3,
                   "sb": 4, "cb1": 5, "ncb1": 6, "gb2": 7}[name]
            return dcols[:, ND * idx + m:ND * idx + m + 1]

        d = kb.do

        def p0(E0):
            mod_row = E0(nc.sbuf_tensor("mod_row", [1, 6 * D], F32))
            extra = E0(nc.sbuf_tensor("extra_rows", [1, 2 * D], F32))
            c_sb = E0(nc.sbuf_tensor("c_sb", [1, D], F32))
            d("gpsimd", lambda e: e.iota(iota_i[:, :], [[1, 128]], base=0,
                                         channel_multiplier=-1), "G")
            kb.wait("vector", "G", kb.cnt["G"])
            d("vector", lambda e: e.tensor_scalar(U_bf[:, :], iota_i[:, :], 0.0, None, OP.is_ge), None)
            for r4 in range(4):
                d("vector", lambda e, r4=r4: e.tensor_copy(U4[:, 128 * r4:128 * (r4 + 1)], U_bf[:, :]), None)
            d("vector", lambda e: e.tensor_scalar(identB[:, :], iota_i[:, :], 0.0, None,
                                                  OP.is_equal), None)
            d("vector", lambda e: e.tensor_copy(identF[:, :], identB[:, :]), None)
            d("vector", lambda e: e.memset(cg[:, :], 0.0), None)
            for h in range(HN):
                d("vector", lambda e, h=h: e.memset(cg[:, 17 * h:17 * h + 1], 1.0 / 128.0), None)
            d("vector", lambda e: e.memset(ones1[:, :], 1.0), None)
            d("vector", lambda e: e.memset(onesrow[:, :], 1.0), None)
            d("vector", lambda e: e.memset(gn_col[:, :], 1.0 / 128.0), None)
            d("vector", lambda e: e.memset(sumcolF[:, :], 1.0 / 2048.0), "V")
            vc = kb.dma("sync", c_sb[:, :], c_row[:, :], "c")
            vg = kb.dma("sync", gnw_f32[:, :], gnw_h[:, :], "gnw")
            kb.dwait("vector", vg)
            d("vector", lambda e: e.tensor_copy(gnw_row[:, :], gnw_f32[:, :]), "V")
            d("vector", lambda e: e.memset(gnwsel[:, :], 0.0), "V")
            kb.wait("sync", "V", kb.cnt["V"])
            vgs = None
            for h in range(HN):
                vgs = kb.dma("sync", gnwsel[h:h + 1, 128 * h:128 * (h + 1)],
                             gnw_row[:, :], "gnws")
            kb.dwait("tensor", vgs)
            kb.dwait("scalar", vc)
            d("scalar", lambda e: e.activation(c_sb[:, :], c_sb[:, :], AF.Silu,
                                               bias=0.0, scale=1.0), "A")
            kb.wait("tensor", "A", kb.cnt["A"])
            kb.wait("tensor", "V", kb.cnt["V"])
            for m in range(ND):
                d("tensor", lambda e, m=m: e.matmul(
                    ps[0][:, m:m + 1], c_sb[:, 128 * m:128 * (m + 1)], ones1[:, :],
                    start=True, stop=True), "T" if m == ND - 1 else None)
            kb.wait("vector", "T", kb.cnt["T"])
            d("vector", lambda e: e.tensor_copy(sc_cols[:, :], ps[0][:, 0:ND]), "V")
            v_sc = kb.cnt["V"]
            kb.guard_set(("bank", 0), "V", v_sc)

            NKH = ND // 2
            wmall = E0(nc.sbuf_tensor("wmall", [128, 4 * NKH * 512], BF16))
            wmf32 = E0(nc.sbuf_tensor("wmf32", [128, 2 * NKH * 512], F32))
            kb.wait("tensor", "V", v_sc)
            piece_T = {}
            cast_done = {}
            f32cnt = 0
            for pc in range(48):
                j, half = divmod(pc, 2)
                slot = pc % 4
                bank = 1 + (j % 2)
                buf = wmall[:, slot * NKH * 512:(slot + 1) * NKH * 512]
                path = pc % 3
                src = hap(wmod, 512 * j + 128 * NKH * half * (6 * D),
                          [[6 * D, 128], [128 * 6 * D, NKH], [1, 512]])
                if path == 0:
                    if pc >= 4:
                        kb.wait("gpsimd", "T", piece_T[pc - 4])
                    vld = kb.dma("gpsimd",
                                 buf.rearrange("p (k c) -> p k c", c=512), src, f"wmg{slot}")
                    kb.dwait("tensor", vld)
                else:
                    fsl = f32cnt % 2
                    f32cnt += 1
                    fbuf = wmf32[:, fsl * NKH * 512:(fsl + 1) * NKH * 512]
                    qeng = "sync" if path == 1 else "scalar"
                    ceng = "scalar" if path == 1 else "vector"
                    prev = cast_done.get(fsl)
                    if prev is not None:
                        kb.wait(qeng, prev[0], prev[1])
                    vld = kb.dma(qeng, fbuf.rearrange("p (k c) -> p k c", c=512),
                                 src, f"wmf{fsl}")
                    if pc >= 4:
                        kb.wait(ceng, "T", piece_T[pc - 4])
                    kb.dwait(ceng, vld)
                    if ceng == "scalar":
                        vcast = d("scalar", lambda e, buf=buf, fbuf=fbuf: e.activation(
                            buf[:, :], fbuf[:, :], AF.Identity, bias=0.0, scale=1.0), "A")
                        cast_done[fsl] = ("A", vcast)
                    else:
                        vcast = d("vector", lambda e, buf=buf, fbuf=fbuf: e.tensor_copy(
                            buf[:, :], fbuf[:, :]), "V")
                        cast_done[fsl] = ("V", vcast)
                    kb.wait("tensor", cast_done[fsl][0], cast_done[fsl][1])
                if half == 0:
                    kb.guard_wait("tensor", ("bank", bank))
                for kh in range(NKH):
                    kc = half * NKH + kh
                    d("tensor", lambda e, kc=kc, kh=kh, bank=bank, buf=buf: e.matmul(
                        ps[bank][0:1, 0:512], sc_cols[:, kc:kc + 1],
                        buf[:, 512 * kh:512 * (kh + 1)],
                        start=(kc == 0), stop=(kc == ND - 1)),
                        "T" if kh == NKH - 1 else None)
                piece_T[pc] = kb.cnt["T"]
                if half == 1:
                    kb.wait("vector", "T", piece_T[pc])
                    vj = d("vector", lambda e, j=j, bank=bank: e.tensor_copy(
                        mod_row[:, 512 * j:512 * (j + 1)],
                        ps[bank][0:1, 0:512]), "V")
                    kb.guard_set(("bank", bank), "V", vj)
            v_mod = kb.cnt["V"]

            kb.wait("vector", "V", v_mod)
            d("vector", lambda e: e.tensor_scalar(mod_row[:, D:2 * D], mod_row[:, D:2 * D],
                                                  1.0, None, OP.add), None)
            d("vector", lambda e: e.tensor_scalar(mod_row[:, 4 * D:5 * D],
                                                  mod_row[:, 4 * D:5 * D],
                                                  1.0, None, OP.add), "V")
            for so in (2 * D, 5 * D):
                d("vector", lambda e, so=so: e.tensor_scalar(
                    extra[:, 0:D], mod_row[:, so:so + D], 0.0, None, OP.is_ge), None)
                d("vector", lambda e: e.tensor_scalar(
                    extra[:, 0:D], extra[:, 0:D], 2.0, -1.0, OP.mult, OP.add), "V")
                kb.wait("scalar", "V", kb.cnt["V"])
                d("scalar", lambda e, so=so: e.activation(
                    extra[:, D:2 * D], mod_row[:, so:so + D], AF.Abs,
                    bias=0.0, scale=1.0), "A")
                kb.wait("vector", "A", kb.cnt["A"])
                d("vector", lambda e: e.tensor_scalar(
                    extra[:, D:2 * D], extra[:, D:2 * D], 0.25, None, OP.max), None)
                d("vector", lambda e, so=so: e.tensor_tensor(
                    mod_row[:, so:so + D], extra[:, 0:D], extra[:, D:2 * D],
                    OP.mult), "V")
            d("vector", lambda e: e.tensor_scalar(extra[:, 0:D], mod_row[:, D:2 * D],
                                                  -1.0, None, OP.mult), None)
            d("vector", lambda e: e.tensor_scalar(extra[:, D:2 * D], mod_row[:, 4 * D:5 * D],
                                                  -1.0, None, OP.mult), "V")
            v_rows = kb.cnt["V"]
            kb.wait("tensor", "V", v_rows)
            srcs = [(mod_row, 0), (mod_row, D), (extra, 0), (mod_row, 2 * D),
                    (mod_row, 3 * D), (mod_row, 4 * D), (extra, D), (mod_row, 5 * D)]
            for i, (rt, so) in enumerate(srcs):
                bank = i % 2
                kb.guard_wait("tensor", ("bank", bank))
                for m in range(ND):
                    d("tensor", lambda e, rt=rt, so=so, m=m, bank=bank: e.matmul(
                        ps[bank][:, m:m + 1], rt[:, so + 128 * m:so + 128 * (m + 1)],
                        ones1[:, :], start=True, stop=True),
                        "T" if m == ND - 1 else None)
                kb.wait("vector", "T", kb.cnt["T"])
                vj = d("vector", lambda e, i=i, bank=bank: e.tensor_copy(
                    dcols[:, ND * i:ND * (i + 1)], ps[bank][:, 0:ND]), "V")
                kb.guard_set(("bank", bank), "V", vj)

            if "dmp_mod" in kb.dump:
                kb.wait("sync", "V", kb.cnt["V"])
                kb.dma("sync", dumps["dmp_mod"][:, :], mod_row[:, :], "dump")

        hbuf = [None]
        tf32c = None
        i32c = None

        def h_chunk(m):
            return hbuf[0][:, T * m:T * (m + 1)]

        def ln_pass(x_src_chunk, out_chunk, cname, csname, ncname):
            stat_rows = statbuf[0]
            """x_src_chunk(m) -> fp32 [128,512] AP (resident); writes
            out_chunk(m) bf16. Uses banks 0,1 (stats) and 2,3 (broadcast)."""
            xbf = ln_xbf
            x2bf = ln_x2bf
            t1 = ln_t1
            t4 = ln_t4
            mm2_done = {}
            for m in range(ND):
                pb = (m % 2) * 512
                if m >= 2:
                    kb.wait("vector", "T", mm2_done[m - 2])
                    kb.wait("scalar", "T", mm2_done[m - 2])
                d("vector", lambda e, m=m, pb=pb: e.tensor_copy(
                    xbf[:, pb:pb + 512], x_src_chunk(m)), "V")
                vv = kb.cnt["V"]
                kb.wait("scalar", "V", vv)
                d("scalar", lambda e, pb=pb: e.activation(
                    x2bf[:, pb:pb + 512], xbf[:, pb:pb + 512], AF.Square,
                    bias=0.0, scale=1.0), "A")
                va = kb.cnt["A"]
                kb.wait("tensor", "V", vv)
                if m == 0:
                    kb.guard_wait("tensor", ("bank", 0))
                    kb.guard_wait("tensor", ("bank", 1))
                d("tensor", lambda e, pb=pb, m=m: e.matmul(
                    ps[0][0:1, 0:512], sumcolF[:, :], xbf[:, pb:pb + 512],
                    start=(m == 0), stop=(m == ND - 1)), None)
                kb.wait("tensor", "A", va)
                d("tensor", lambda e, pb=pb, m=m: e.matmul(
                    ps[1][0:1, 0:512], sumcolF[:, :], x2bf[:, pb:pb + 512],
                    start=(m == 0), stop=(m == ND - 1)), "T")
                mm2_done[m] = kb.cnt["T"]
            v_stats = kb.cnt["T"]
            kb.wait("vector", "T", v_stats)
            R = stat_rows
            d("vector", lambda e: e.tensor_copy(R[:, 0:512], ps[0][0:1, 0:512]), None)
            d("vector", lambda e: e.tensor_tensor(R[:, 512:1024], R[:, 0:512],
                                                  R[:, 0:512], OP.mult), None)
            d("vector", lambda e: e.scalar_tensor_tensor(
                R[:, 1024:1536], ps[1][0:1, 0:512], EPS, R[:, 512:1024],
                OP.add, OP.subtract), None)
            d("vector", lambda e: e.reciprocal(R[:, 1536:2048], R[:, 1024:1536]), "V")
            v_free01 = kb.cnt["V"]
            kb.guard_set(("bank", 0), "V", v_free01)
            kb.guard_set(("bank", 1), "V", v_free01)
            kb.wait("scalar", "V", v_free01)
            d("scalar", lambda e: e.activation(R[:, 2048:2560], R[:, 1536:2048],
                                               AF.Sqrt, bias=0.0, scale=1.0), "A")
            kb.wait("vector", "A", kb.cnt["A"])
            d("vector", lambda e: e.tensor_tensor(R[:, 2560:3072], R[:, 0:512],
                                                  R[:, 2048:2560], OP.mult), "V")
            v_rows = kb.cnt["V"]
            kb.wait("tensor", "V", v_rows)
            kb.guard_wait("tensor", ("bank", 2))
            kb.guard_wait("tensor", ("bank", 3))
            d("tensor", lambda e: e.matmul(ps[2][:, 0:512], onesrow[:, 0:128],
                                           R[:, 2048:2560], start=True, stop=True), None)
            d("tensor", lambda e: e.matmul(ps[3][:, 0:512], onesrow[:, 0:128],
                                           R[:, 2560:3072], start=True, stop=True), "T")
            v_bc = kb.cnt["T"]
            kb.wait("vector", "T", v_bc)
            for m in range(ND):
                d("vector", lambda e, m=m: e.scalar_tensor_tensor(
                    t1[:, :], x_src_chunk(m), dcol(cname, m), ps[2][:, 0:512],
                    OP.mult, OP.mult), None)
                d("vector", lambda e, m=m: e.scalar_tensor_tensor(
                    t4[:, :], ps[3][:, 0:512], dcol(ncname, m), t1[:, :],
                    OP.mult, OP.add), None)
                d("vector", lambda e, m=m: e.tensor_scalar(
                    out_chunk(m), t4[:, :], dcol(csname, m), None, OP.add), "V")
            v_done = kb.cnt["V"]
            kb.guard_set(("bank", 2), "V", v_done)
            kb.guard_set(("bank", 3), "V", v_done)
            return v_done

        def p1(E1):
            xbuf = E1(nc.sbuf_tensor("xbuf", [128, 2 * ND * 512], F32))
            statbuf[0] = E1(nc.sbuf_tensor("stat_rows1", [1, 6 * 512], F32))

            for qq in range(NQ):
                bq = (qq % 2) * ND * 512
                kb.guard_wait("sync", ("xbuf", qq % 2))
                vx = kb.dma("sync",
                            xbuf[:, bq:bq + ND * 512].rearrange("p (m t) -> p m t", t=512),
                            hap(x_fm, 512 * qq, [[T, 128], [128 * T, ND], [1, 512]]),
                            f"x{qq % 2}")
                kb.dwait("vector", vx)
                kb.dwait("scalar", vx)

                def src(m, bq=bq):
                    return xbuf[:, bq + 512 * m:bq + 512 * (m + 1)]

                def out(m, qq=qq):
                    return h_chunk(m)[:, 512 * qq:512 * (qq + 1)]

                v_done = ln_pass(src, out, "ca1", "sa", "nca1")
                kb.guard_set(("xbuf", qq % 2), "V", v_done)
                kb.tick5w(4)

            if "dmp_h" in kb.dump:
                kb.wait("sync", "V", kb.cnt["V"])
                for m in range(ND):
                    kb.dma("sync", dumps["dmp_h"][128 * m:128 * (m + 1), :], h_chunk(m), "dump")

        def p2(E2):
            wbuf = E2(nc.sbuf_tensor("wbuf", [128, 2 * ND * 512], BF16))
            hw1 = E2(nc.sbuf_tensor("hw1", [17, T], BF16))
            wgk2a = E2(nc.sbuf_tensor("wgk2a", [17, D], BF16))
            w1b = E2(nc.sbuf_tensor("w1b", [128, ND * 16], BF16))

            def stream_proj(w_hbm, kind, sp_dst, scale_q=1.0, silu_evac=False):
                for quarter in range(4):
                    bq = (quarter % 2) * ND * 512
                    kb.guard_wait("gpsimd", ("wbuf", quarter % 2))
                    vw = kb.dma("gpsimd",
                                wbuf[:, bq:bq + ND * 512].rearrange("p (k c) -> p k c", c=512),
                                hap(w_hbm, 512 * quarter, [[D, 128], [128 * D, ND], [1, 512]]),
                                f"w{quarter % 2}")
                    kb.dwait("tensor", vw)
                    outer = range(4) if kind == "fm" else range(NT)
                    for oidx in outer:
                        inner = range(NQ) if kind == "fm" else (0,)
                        for qq in inner:
                            it = (oidx * NQ + qq) if kind == "fm" else oidx
                            bank = it % 2
                            sslot = it % 4
                            kb.guard_wait("tensor", ("bank", bank))
                            for k in range(ND):
                                if kind == "fm":
                                    lhsT = wbuf[:, bq + 512 * k + 128 * oidx:
                                                bq + 512 * k + 128 * (oidx + 1)]
                                    rhs = h_chunk(k)[:, 512 * qq:512 * (qq + 1)]
                                else:
                                    lhsT = h_chunk(k)[:, 128 * oidx:128 * (oidx + 1)]
                                    rhs = wbuf[:, bq + 512 * k:bq + 512 * (k + 1)]
                                d("tensor", lambda e, lhsT=lhsT, rhs=rhs, k=k, bank=bank: e.matmul(
                                    ps[bank][:, 0:512], lhsT, rhs,
                                    start=(k == 0), stop=(k == ND - 1)),
                                    "T" if k == ND - 1 else None)
                            vT = kb.cnt["T"]
                            dst = stg[:, 512 * sslot:512 * (sslot + 1)]
                            if silu_evac:
                                kb.wait("scalar", "T", vT)
                                kb.guard_wait("scalar", ("stg", sslot))
                                d("scalar", lambda e, bank=bank, dst=dst: e.activation(
                                    dst, ps[bank][:, 0:512], AF.Silu,
                                    bias=0.0, scale=1.0), "A")
                                sem_e, vE = "A", kb.cnt["A"]
                            else:
                                kb.wait("vector", "T", vT)
                                kb.guard_wait("vector", ("stg", sslot))
                                d("vector", lambda e, bank=bank, dst=dst, sc=scale_q: e.tensor_scalar(
                                    dst, ps[bank][:, 0:512], sc, None, OP.mult), "V")
                                sem_e, vE = "V", kb.cnt["V"]
                            kb.guard_set(("bank", bank), sem_e, vE)
                            kb.wait("sync", sem_e, vE)
                            if kind == "fm":
                                m_glob = quarter * 4 + oidx
                                vout = kb.dma("sync",
                                              hap(sp_dst, (4 * qq) * (ND * 128 * 128) + m_glob * (128 * 128),
                                                  [[128, 128], [ND * 128 * 128, 4], [1, 128]]),
                                              dst.rearrange("p (c t) -> p c t", t=128),
                                              f"stg{sslot}")
                            else:
                                vout = kb.dma("sync",
                                              hap(sp_dst, (128 * oidx) * D + 512 * quarter,
                                                  [[D, 128], [1, 512]]),
                                              dst, f"stg{sslot}")
                            kb.guard_set(("stg", sslot), *vout)
                        if (it % 4) == 3:
                            kb.tick5w(1)
                        if getattr(kb, "tick2", None) is not None:
                            kb.tick2(2)
                    kb.guard_set(("wbuf", quarter % 2), "T", kb.cnt["T"])

            stream_proj(wq_h, "fm", sp_q, scale_q=SCALE)
            stream_proj(wk_h, "fm", sp_k)
            stream_proj(wg_h, "fm", sp_g, silu_evac=True)

            vw1 = kb.dma("gpsimd", w1b[:, :].rearrange("p (k c) -> p k c", c=16),
                         hap(wgk1, 0, [[16, 128], [128 * 16, ND], [1, 16]]), "gk1")
            vk2 = kb.dma("gpsimd", wgk2a[0:16, :], wgk2[:, :], "gk2")
            vb2 = kb.dma("gpsimd", wgk2a[16:17, :], bgk2[:, :], "gk3")
            for _sl in range(4):
                kb.guard_wait("vector", ("stg", _sl))
            d("vector", lambda e: e.memset(stg[0:1, 0:T], 1.0), "V")
            kb.wait("sync", "V", kb.cnt["V"])
            v_hw1ones = kb.dma("sync", hw1[16:17, :], stg[0:1, 0:T], "hw1o")
            for _sl in range(4):
                kb.guard_set(("stg", _sl), *v_hw1ones)
            kb.dwait("tensor", v_hw1ones)
            kb.dwait("tensor", vw1)
            for qq in range(NQ):
                bank = qq % 2
                kb.guard_wait("tensor", ("bank", bank))
                for k in range(ND):
                    d("tensor", lambda e, k=k, qq=qq, bank=bank: e.matmul(
                        ps[bank][0:16, 0:512], w1b[:, 16 * k:16 * (k + 1)],
                        h_chunk(k)[:, 512 * qq:512 * (qq + 1)],
                        start=(k == 0), stop=(k == ND - 1)),
                        "T" if k == ND - 1 else None)
                kb.wait("vector", "T", kb.cnt["T"])
                vj = d("vector", lambda e, qq=qq, bank=bank: e.tensor_copy(
                    hw1[0:16, 512 * qq:512 * (qq + 1)], ps[bank][0:16, 0:512]), "V")
                kb.guard_set(("bank", bank), "V", vj)
            v_hw1done = kb.cnt["V"]

            def gk2_gen():
                yield
                for it in range(64):
                    tt, j = divmod(it, 4)
                    bank = 2 + (it % 2)
                    fs = it % 2
                    if it == 0:
                        kb.dwait("tensor", vb2)
                        kb.dwait("tensor", vk2)
                        kb.wait("tensor", "V", v_hw1done)
                    kb.guard_wait("tensor", ("bank", bank))
                    d("tensor", lambda e, tt=tt, j=j, bank=bank: e.matmul(
                        ps[bank][:, 0:512], hw1[:, 128 * tt:128 * (tt + 1)],
                        wgk2a[:, 512 * j:512 * (j + 1)], start=True, stop=True), "T")
                    vT = kb.cnt["T"]
                    kb.wait("scalar", "T", vT)
                    kb.guard_wait("scalar", ("stgf", fs))
                    d("scalar", lambda e, bank=bank, fs=fs: e.activation(
                        stg_f[:, 512 * fs:512 * (fs + 1)], ps[bank][:, 0:512],
                        AF.Exp, bias=0.0, scale=-1.0), "A")
                    va1 = kb.cnt["A"]
                    kb.guard_set(("bank", bank), "A", va1)
                    kb.guard_wait("scalar", ("lnx", fs))
                    d("scalar", lambda e, fs=fs: e.activation(
                        ln_xbf[:, 512 * fs:512 * (fs + 1)],
                        stg_f[:, 512 * fs:512 * (fs + 1)],
                        AF.Ln, bias=1.0, scale=1.0), "A")
                    vA = kb.cnt["A"]
                    kb.guard_set(("stgf", fs), "A", vA)
                    kb.wait("sync", "A", vA)
                    vout = kb.dma("sync",
                                  hap(sp_sp, (128 * tt) * D + 512 * j, [[D, 128], [1, 512]]),
                                  ln_xbf[:, 512 * fs:512 * (fs + 1)], f"sp2{fs}")
                    kb.guard_set(("lnx", fs), *vout)
                    yield

            g2k = gk2_gen()
            next(g2k)

            def tick2(n=1):
                for _ in range(n):
                    if next(g2k, "END") == "END":
                        return
            kb.tick2 = tick2
            stream_proj(wv_h, "tm", sp_v)
            stream_proj(wk_h, "tm", sp_ktm)
            kb.tick2 = None
            for _ in range(64):
                if next(g2k, "END") == "END":
                    break

        def p3(E3):
            qc = E3(nc.sbuf_tensor("qc", [128, 2 * D], BF16))
            kcb = E3(nc.sbuf_tensor("kcb", [128, 2 * D], BF16))
            vcb = E3(nc.sbuf_tensor("vcb", [128, 2 * D], BF16))
            ktc = E3(nc.sbuf_tensor("ktc", [128, 2 * D], BF16))
            spc = E3(nc.sbuf_tensor("spc", [128, 2 * D], BF16))
            en_tm = E3(nc.sbuf_tensor("en_tm", [128, 2 * D], BF16))
            spcum = E3(nc.sbuf_tensor("spcum", [128, 2 * D], BF16))
            epos_b = E3(nc.sbuf_tensor("epos_b", [128, 2 * D], BF16))
            eneg_b = E3(nc.sbuf_tensor("eneg_b", [128, 2 * D], BF16))
            ecols_b = E3(nc.sbuf_tensor("ecols_b", [128, 2 * HN], F32))
            qg_b = E3(nc.sbuf_tensor("qg_b", [128, 2 * D], BF16))
            s_tmp = E3(nc.sbuf_tensor("s_tmp", [128, 512], F32))
            kh_b = E3(nc.sbuf_tensor("kh_b", [128, 2 * D], BF16))
            kt_b = E3(nc.sbuf_tensor("kt_b", [128, 2 * D], BF16))
            a_bf = E3(nc.sbuf_tensor("a_bf", [128, 2 * 512], BF16))
            s_f32 = E3(nc.sbuf_tensor("s_f32", [128, HN * 128], F32))
            s_bf = E3(nc.sbuf_tensor("s_bf", [128, HN * 128], BF16))
            o_seg = E3(nc.sbuf_tensor("o_seg", [128, HN * 512], F32))
            g_seg = E3(nc.sbuf_tensor("g_seg", [128, HN * 512], BF16))
            o2_bf = E3(nc.sbuf_tensor("o2_bf", [128, 2 * 512], BF16))
            fe_rows = E3(nc.sbuf_tensor("fe_rows", [16, 2 * 512], F32))
            f_bf = E3(nc.sbuf_tensor("f_bf", [16, 512], BF16))
            t_tmp = E3(nc.sbuf_tensor("t_tmp", [128, 2 * 512], F32))
            op_stg = E3(nc.sbuf_tensor("op_stg", [128, 2 * 512], BF16))

            CH_STRIDE = ND * 128 * 128
            LD = {}
            FR = {}

            def loads(cc):
                sl = cc % 2
                kb.guard_wait("sync", ("qbuf", sl))
                v_q = kb.dma("sync", qc[:, sl * D:(sl + 1) * D].rearrange("p (m t) -> p m t", t=128),
                             hap(sp_q, cc * CH_STRIDE, [[128, 128], [128 * 128, ND], [1, 128]]),
                             f"q{sl}")
                kb.guard_wait("sync", ("kbuf", sl))
                v_k = kb.dma("sync", kcb[:, sl * D:(sl + 1) * D].rearrange("p (m t) -> p m t", t=128),
                             hap(sp_k, cc * CH_STRIDE, [[128, 128], [128 * 128, ND], [1, 128]]),
                             f"kk{sl}")
                kb.guard_wait("sync", ("vbuf", sl))
                v_v = kb.dma("sync", vcb[:, sl * D:(sl + 1) * D],
                             hap(sp_v, (128 * cc) * D, [[D, 128], [1, D]]), f"v{sl}")
                kb.guard_wait("sync", ("ktbuf", sl))
                v_kt = kb.dma("sync", ktc[:, sl * D:(sl + 1) * D],
                              hap(sp_ktm, (128 * cc) * D, [[D, 128], [1, D]]), f"kt{sl}")
                kb.guard_wait("sync", ("spbuf", sl))
                v_sp = kb.dma("sync", spc[:, sl * D:(sl + 1) * D],
                              hap(sp_sp, (128 * cc) * D, [[D, 128], [1, D]]), f"sp{sl}")
                LD[cc] = dict(q=v_q, k=v_k, v=v_v, kt=v_kt, sp=v_sp)

            def load_gseg(seg):
                kb.guard_wait("sync", ("gseg",))
                for hh2 in range(HN):
                    v_g = kb.dma("sync",
                                 g_seg[:, 512 * hh2:512 * (hh2 + 1)]
                                 .rearrange("p (c t) -> p c t", t=128),
                                 hap(sp_g, (4 * seg) * CH_STRIDE + hh2 * (128 * 128),
                                     [[128, 128], [CH_STRIDE, 4], [1, 128]]),
                                 "gseg")
                kb.guards[("gseg_ld", seg)] = [v_g]

            def make_fillers(cc):
                sl = cc % 2
                FR[cc] = {}

                def u_piece(j4):
                    if j4 == 0:
                        kb.dwait("tensor", LD[cc]["sp"])
                    kb.guard_wait("tensor", ("bank", 5))
                    vT = d("tensor", lambda e, sl=sl, j4=j4: e.matmul(
                        ps[5][:, 0:512], U_bf[:, :],
                        spc[:, sl * D + 512 * j4:sl * D + 512 * (j4 + 1)],
                        start=True, stop=True), "T")
                    kb.wait("scalar", "T", vT)
                    if j4 == 0:
                        kb.guard_wait("scalar", ("entm", sl))
                        kb.guard_wait("scalar", ("spcum", sl))
                    d("scalar", lambda e, sl=sl, j4=j4: e.activation(
                        en_tm[:, sl * D + 512 * j4:sl * D + 512 * (j4 + 1)],
                        ps[5][:, 0:512], AF.Exp, bias=0.0, scale=1.0 / GLN), "A")
                    va = d("scalar", lambda e, sl=sl, j4=j4: e.activation(
                        spcum[:, sl * D + 512 * j4:sl * D + 512 * (j4 + 1)],
                        ps[5][:, 0:512], AF.Identity, bias=0.0, scale=1.0), "A")
                    kb.guard_set(("bank", 5), "A", va)
                    if j4 == 3:
                        kb.guard_set(("spbuf", sl), "T", kb.cnt["T"])
                        FR[cc]["spcum"] = va
                        FR[cc]["entm"] = va

                def tr_piece(g):
                    pb = psb[g]
                    kb.wait("tensor", "A", FR[cc]["spcum"])
                    kb.guard_wait("tensor", ("psb", g))
                    for h8 in range(8):
                        hh = g * 8 + h8
                        d("tensor", lambda e, sl=sl, hh=hh, h8=h8, pb=pb: e.transpose(
                            pb[:, 128 * h8:128 * (h8 + 1)],
                            spcum[:, sl * D + 128 * hh:sl * D + 128 * (hh + 1)],
                            identB[:, :]),
                            "T" if h8 == 7 else None)
                    vtr = kb.cnt["T"]
                    kb.wait("scalar", "T", vtr)
                    kb.guard_wait("scalar", ("eall", g, sl))
                    d("scalar", lambda e, g=g, pb=pb, sl=sl: e.activation(
                        epos_b[:, sl * D + 1024 * g:sl * D + 1024 * (g + 1)], pb[:, 0:1024],
                        AF.Exp, bias=0.0, scale=-1.0 / GLN), "A")
                    vexp = d("scalar", lambda e, g=g, pb=pb, sl=sl: e.activation(
                        eneg_b[:, sl * D + 1024 * g:sl * D + 1024 * (g + 1)], pb[:, 0:1024],
                        AF.Exp, bias=0.0, scale=1.0 / GLN), "A")
                    kb.guard_set(("psb", g), "A", vexp)
                    if g == 1:
                        kb.guard_set(("spcum", sl), "T", vtr)
                        FR[cc]["exps"] = vexp

                def ec_qg():
                    kb.wait("vector", "A", FR[cc]["exps"])
                    kb.guard_wait("vector", ("ecols", sl))
                    d("vector", lambda e, sl=sl: e.tensor_copy(
                        ecols_b[:, sl * HN:(sl + 1) * HN],
                        hap(epos_b, sl * D + 127, [[2 * D, 128], [128, HN]])), "V")
                    kb.dwait("vector", LD[cc]["q"])
                    kb.guard_wait("vector", ("qgall", sl))
                    d("vector", lambda e, sl=sl: e.tensor_tensor(
                        qg_b[:, sl * D:(sl + 1) * D], qc[:, sl * D:(sl + 1) * D],
                        epos_b[:, sl * D:(sl + 1) * D], OP.mult), "V")
                    kb.guard_set(("qbuf", sl), "V", kb.cnt["V"])

                def kh_f():
                    kb.dwait("vector", LD[cc]["k"])
                    kb.guard_wait("vector", ("khall", sl))
                    d("vector", lambda e, sl=sl: e.tensor_tensor(
                        kh_b[:, sl * D:(sl + 1) * D], kcb[:, sl * D:(sl + 1) * D],
                        eneg_b[:, sl * D:(sl + 1) * D], OP.mult), "V")
                    FR[cc]["gates"] = kb.cnt["V"]
                    kb.guard_set(("kbuf", sl), "V", FR[cc]["gates"])
                    kb.guard_set(("eall", 0, sl), "V", FR[cc]["gates"])
                    kb.guard_set(("eall", 1, sl), "V", FR[cc]["gates"])

                def kt_f():
                    kb.dwait("vector", LD[cc]["kt"])
                    kb.wait("vector", "A", FR[cc]["entm"])
                    kb.guard_wait("vector", ("ktall", sl))
                    v_ktall = d("vector", lambda e, sl=sl: e.tensor_tensor(
                        kt_b[:, sl * D:(sl + 1) * D], ktc[:, sl * D:(sl + 1) * D],
                        en_tm[:, sl * D:(sl + 1) * D], OP.mult), "V")
                    FR[cc]["ktall"] = v_ktall
                    kb.guard_set(("entm", sl), "V", v_ktall)
                    kb.guard_set(("ktbuf", sl), "V", v_ktall)

                return [lambda j4=j4: u_piece(j4) for j4 in range(4)] + \
                       [lambda: tr_piece(0), lambda: tr_piece(1),
                        ec_qg, kh_f, kt_f]

            def pass2(cc, fillers):
                sl = cc % 2
                fillers = list(fillers)

                def fill(n=1):
                    for _ in range(n):
                        if fillers:
                            fillers.pop(0)()

                def s_evac(hg):
                    vsmm = kb.cnt["T"]
                    kb.wait("vector", "T", vsmm)
                    for h4 in range(4):
                        kb.guard_wait("vector", ("sf32", hg * 4 + h4))
                    if cc > 0:
                        d("vector", lambda e, hg=hg: e.tensor_tensor(
                            s_tmp[:, :], ps[4][:, 0:512],
                            s_f32[:, 512 * hg:512 * (hg + 1)], OP.add), "V")
                        for h4 in range(4):
                            hh = hg * 4 + h4
                            d("vector", lambda e, hh=hh, h4=h4, sl=sl: e.tensor_scalar(
                                s_f32[:, 128 * hh:128 * (hh + 1)],
                                s_tmp[:, 128 * h4:128 * (h4 + 1)],
                                ecols_b[:, sl * HN + hh:sl * HN + hh + 1], None, OP.mult), "V")
                    else:
                        for h4 in range(4):
                            hh = hg * 4 + h4
                            d("vector", lambda e, hh=hh, h4=h4, sl=sl: e.tensor_scalar(
                                s_f32[:, 128 * hh:128 * (hh + 1)],
                                ps[4][:, 128 * h4:128 * (h4 + 1)],
                                ecols_b[:, sl * HN + hh:sl * HN + hh + 1], None, OP.mult), "V")
                    vsev = kb.cnt["V"]
                    kb.guard_set(("bank", 4), "V", vsev)
                    kb.wait("gpsimd", "V", vsev)
                    vcast = d("gpsimd", lambda e, hg=hg: e.tensor_copy(
                        s_bf[:, 512 * hg:512 * (hg + 1)],
                        s_f32[:, 512 * hg:512 * (hg + 1)]), "G")
                    for h4 in range(4):
                        kb.guard_set(("S", hg * 4 + h4), "G", vcast)
                        kb.guards[("sf32", hg * 4 + h4)] = [("G", vcast)]

                kb.wait("tensor", "V", FR[cc]["gates"])
                kb.wait("tensor", "V", FR[cc]["ktall"])
                kb.dwait("tensor", LD[cc]["v"])

                def a_mms(hg):
                    abank = hg % 2
                    kb.guard_wait("tensor", ("bank", abank))
                    for h4 in range(4):
                        hof = 128 * (hg * 4 + h4)
                        d("tensor", lambda e, hof=hof, abank=abank, h4=h4, sl=sl: e.matmul(
                            ps[abank][:, 128 * h4:128 * (h4 + 1)],
                            kh_b[:, sl * D + hof:sl * D + hof + 128],
                            qg_b[:, sl * D + hof:sl * D + hof + 128],
                            start=True, stop=True),
                            "T" if h4 == 3 else None)
                    return kb.cnt["T"]

                vamm_d = {0: a_mms(0), 1: a_mms(1)}
                prev_sev = None
                for hg in range(4):
                    abank = hg % 2
                    ob = 2 + hg % 2
                    vamm = vamm_d[hg]
                    kb.wait("vector", "T", vamm)
                    kb.guard_wait("vector", ("abf", abank))
                    vabf = d("vector", lambda e, hg=hg, abank=abank: e.tensor_tensor(
                        a_bf[:, 512 * (hg % 2):512 * (hg % 2 + 1)],
                        ps[abank][:, 0:512], U4[:, :], OP.mult), "V")
                    kb.guard_set(("bank", abank), "V", vabf)
                    if prev_sev is not None:
                        prev_sev()
                        prev_sev = None
                    fill()

                    kb.wait("tensor", "V", vabf)
                    kb.guard_wait("tensor", ("bank", ob))
                    for h4 in range(4):
                        hh = hg * 4 + h4
                        hof = 128 * hh
                        if cc > 0:
                            kb.guard_wait("tensor", ("S", hh))
                            d("tensor", lambda e, ob=ob, hof=hof, h4=h4, sl=sl: e.matmul(
                                ps[ob][:, 128 * h4:128 * (h4 + 1)], s_bf[:, hof:hof + 128],
                                qg_b[:, sl * D + hof:sl * D + hof + 128],
                                start=True, stop=False), None)
                        d("tensor", lambda e, ob=ob, sl=sl, hof=hof, hg=hg, h4=h4, cc=cc: e.matmul(
                            ps[ob][:, 128 * h4:128 * (h4 + 1)],
                            vcb[:, sl * D + hof:sl * D + hof + 128],
                            a_bf[:, 512 * (hg % 2) + 128 * h4:512 * (hg % 2) + 128 * (h4 + 1)],
                            start=(cc == 0), stop=True),
                            "T" if h4 == 3 else None)
                    vomm = kb.cnt["T"]
                    kb.guard_set(("abf", abank), "T", vomm)
                    kb.wait("scalar", "T", vomm)
                    for h4 in range(4):
                        kb.guard_wait("scalar", ("oseg", hg * 4 + h4, cc % 4))
                    voev = d("scalar", lambda e, ob=ob, hg=hg, cc=cc: e.activation(
                        hap(o_seg, 512 * (4 * hg) + 128 * (cc % 4),
                            [[HN * 512, 128], [512, 4], [1, 128]]),
                        ps[ob][:, 0:512], AF.Identity, bias=0.0, scale=1.0), "A")
                    kb.guard_set(("bank", ob), "A", voev)
                    for h4 in range(4):
                        kb.guards[("olast", hg * 4 + h4)] = [("A", voev)]
                    fill()

                    kb.guard_wait("tensor", ("bank", 4))
                    for h4 in range(4):
                        hof = 128 * (hg * 4 + h4)
                        d("tensor", lambda e, sl=sl, hof=hof, h4=h4: e.matmul(
                            ps[4][:, 128 * h4:128 * (h4 + 1)],
                            kt_b[:, sl * D + hof:sl * D + hof + 128],
                            vcb[:, sl * D + hof:sl * D + hof + 128],
                            start=True, stop=True),
                            "T" if h4 == 3 else None)
                    if hg + 2 < 4:
                        vamm_d[hg + 2] = a_mms(hg + 2)
                    prev_sev = (lambda hg=hg: s_evac(hg))
                prev_sev()
                fill(len(fillers))
                vT_end = kb.cnt["T"]
                kb.guard_set(("vbuf", sl), "T", vT_end)
                kb.guard_set(("ktall", sl), "T", vT_end)
                kb.guard_set(("qgall", sl), "T", vT_end)
                kb.guard_set(("khall", sl), "T", vT_end)
                kb.guard_set(("ecols", sl), "V", kb.cnt["V"])

            def tail(cc):
                sl = cc % 2
                seg = cc // 4
                kb.dwait("scalar", kb.guards[("gseg_ld", seg)][0])
                kb.dwait("vector", kb.guards[("gseg_ld", seg)][0])
                kb.guard_wait("tensor", ("bank", 4))
                for hh in range(HN):
                    hp = hh % 2
                    kb.wait("scalar", "A", kb.guards[("olast", hh)][0][1])
                    kb.guard_wait("scalar", ("o2", hp))
                    vsq = d("scalar", lambda e, hh=hh, hp=hp: e.activation(
                        o2_bf[:, 512 * hp:512 * (hp + 1)],
                        o_seg[:, 512 * hh:512 * (hh + 1)],
                        AF.Square, bias=0.0, scale=1.0), "A")
                    kb.wait("tensor", "A", vsq)
                    vmq = d("tensor", lambda e, hh=hh, hp=hp: e.matmul(
                        ps[4][0:16, 0:512], cg[:, 16 * hh:16 * (hh + 1)],
                        o2_bf[:, 512 * hp:512 * (hp + 1)],
                        start=(hh == 0), stop=(hh == HN - 1)), "T")
                    kb.guard_set(("o2", hp), "T", vmq)
                vmq = kb.cnt["T"]
                kb.wait("vector", "T", vmq)
                kb.guard_wait("vector", ("frows",))
                d("vector", lambda e: e.tensor_scalar(
                    fe_rows[:, 0:512], ps[4][0:16, 0:512], EPS, None, OP.add), "V")
                vrc = d("vector", lambda e: e.reciprocal(
                    fe_rows[:, 512:1024], fe_rows[:, 0:512]), "V")
                kb.guard_set(("bank", 4), "V", vrc)
                kb.wait("scalar", "V", vrc)
                kb.guard_wait("scalar", ("fbf",))
                vfr = d("scalar", lambda e: e.activation(
                    f_bf[:, :], fe_rows[:, 512:1024], AF.Sqrt,
                    bias=0.0, scale=1.0), "A")
                kb.guards[("frows",)] = [("A", vfr)]
                kb.wait("tensor", "A", vfr)
                for hh in range(HN):
                    hp = hh % 2
                    b2b = hp
                    kb.guard_wait("tensor", ("bank", b2b))
                    vb2m = d("tensor", lambda e, hh=hh, b2b=b2b: e.matmul(
                        ps[b2b][:, 0:512], gnwsel[:, 128 * hh:128 * (hh + 1)],
                        f_bf[:, :], start=True, stop=True), "T")
                    kb.guards[("fbf",)] = [("T", vb2m)]
                    kb.wait("vector", "T", vb2m)
                    kb.guard_wait("vector", ("ttmp", hp))
                    d("vector", lambda e, hh=hh, hp=hp, b2b=b2b: e.tensor_tensor(
                        t_tmp[:, 512 * hp:512 * (hp + 1)],
                        o_seg[:, 512 * hh:512 * (hh + 1)],
                        ps[b2b][:, 0:512], OP.mult), "V")
                    kb.guard_wait("vector", ("opstg", hp))
                    vop = d("vector", lambda e, hh=hh, hp=hp: e.tensor_tensor(
                        op_stg[:, 512 * hp:512 * (hp + 1)],
                        t_tmp[:, 512 * hp:512 * (hp + 1)],
                        g_seg[:, 512 * hh:512 * (hh + 1)], OP.mult), "V")
                    kb.guard_set(("bank", b2b), "V", vop)
                    kb.guard_set(("ttmp", hp), "V", vop)
                    for j in range(4):
                        kb.guard_set(("oseg", hh, j), "V", vop)
                    kb.wait("sync", "V", vop)
                    vsp = kb.dma("sync",
                                 hap(sp_op, (128 * hh) * T + 512 * seg,
                                     [[T, 128], [1, 512]]),
                                 op_stg[:, 512 * hp:512 * (hp + 1)], f"op{hp}")
                    kb.guard_set(("opstg", hp), *vsp)
                    if "dmp_o" in kb.dump:
                        vdo = kb.dma("sync",
                                     hap(dumps["dmp_o"], (128 * hh) * T + 512 * seg,
                                         [[T, 128], [1, 512]]),
                                     o_seg[:, 512 * hh:512 * (hh + 1)], f"do{hh}")
                        for j in range(4):
                            kb.guard_add(("oseg", hh, j), *vdo)
                kb.guard_set(("gseg",), "V", kb.cnt["V"])

            load_gseg(0)
            loads(0)
            for f in make_fillers(0):
                f()
            for cc in range(p3_chunks):
                nxt = cc + 1
                fillers = []
                if nxt < p3_chunks:
                    loads(nxt)
                    fillers = make_fillers(nxt)
                pass2(cc, fillers)
                if cc % 4 == 3:
                    tail(cc)
                if nxt < p3_chunks and nxt % 4 == 0:
                    load_gseg(nxt // 4)

        def p4(E4):
            wobuf = E4(nc.sbuf_tensor("wobuf", [128, ND * D], BF16))
            statbuf[0] = E4(nc.sbuf_tensor("stat_rows4", [1, 6 * 512], F32))
            opb = E4(nc.sbuf_tensor("opb", [128, 2 * ND * 512], BF16))
            xbuf4 = E4(nc.sbuf_tensor("xbuf4", [128, ND * 512], F32))
            x1q = E4(nc.sbuf_tensor("x1q", [128, ND * 512], F32))

            vwo = None
            for quarter in range(4):
                vwo = kb.dma("gpsimd",
                             wobuf[:, :].rearrange("p (k c) -> p k c", c=D)
                             [:, :, 512 * quarter:512 * (quarter + 1)],
                             hap(wo_h, 512 * quarter, [[D, 128], [128 * D, ND], [1, 512]]),
                             f"wo{quarter}")
                kb.dwait("tensor", vwo)

            for qq in range(NQ):
                osl = qq % 2
                kb.guard_wait("sync", ("opb", osl))
                vop = kb.dma("sync",
                             opb[:, osl * ND * 512:(osl + 1) * ND * 512]
                             .rearrange("p (k t) -> p k t", t=512),
                             hap(sp_op, 512 * qq, [[T, 128], [128 * T, ND], [1, 512]]),
                             f"ob{osl}")
                kb.guard_wait("sync", ("xbuf4",))
                vx = kb.dma("sync",
                            xbuf4[:, :].rearrange("p (m t) -> p m t", t=512),
                            hap(x_fm, 512 * qq, [[T, 128], [128 * T, ND], [1, 512]]),
                            "x4")
                kb.dwait("tensor", vop)
                kb.dwait("vector", vx)
                for m in range(ND):
                    bank = m % 2
                    kb.guard_wait("tensor", ("bank", bank))
                    for k in range(ND):
                        d("tensor", lambda e, m=m, k=k, bank=bank, osl=osl: e.matmul(
                            ps[bank][:, 0:512],
                            wobuf[:, k * D + 128 * m:k * D + 128 * (m + 1)],
                            opb[:, osl * ND * 512 + 512 * k:osl * ND * 512 + 512 * (k + 1)],
                            start=(k == 0), stop=(k == ND - 1)),
                            "T" if k == ND - 1 else None)
                    vT = kb.cnt["T"]
                    kb.wait("vector", "T", vT)
                    kb.guard_wait("vector", ("x1q", m % 4))
                    vx1 = d("vector", lambda e, m=m, bank=bank: e.scalar_tensor_tensor(
                        x1q[:, 512 * m:512 * (m + 1)], ps[bank][:, 0:512],
                        dcol("ga2", m), xbuf4[:, 512 * m:512 * (m + 1)],
                        OP.mult, OP.add), "V")
                    kb.guard_set(("bank", bank), "V", vx1)
                    kb.wait("sync", "V", vx1)
                    vx1s = kb.dma("sync",
                                  hap(sp_x1, (128 * m) * T + 512 * qq, [[T, 128], [1, 512]]),
                                  x1q[:, 512 * m:512 * (m + 1)], f"x1s{m % 4}")
                kb.guard_set(("opb", osl), "T", kb.cnt["T"])
                kb.guard_set(("xbuf4",), "V", kb.cnt["V"])
                for j in range(4):
                    kb.guard_set(("x1q", j), "D_x1s" + str(j), kb.cnt.get("D_x1s" + str(j), 0))

                def src(m):
                    return x1q[:, 512 * m:512 * (m + 1)]

                mm2_done = {}
                xbf, x2bf, t1, t4 = ln_xbf, ln_x2bf, ln_t1, ln_t4
                for m in range(ND):
                    pb = (m % 2) * 512
                    if m >= 2:
                        kb.wait("vector", "T", mm2_done[m - 2])
                        kb.wait("scalar", "T", mm2_done[m - 2])
                    d("vector", lambda e, m=m, pb=pb: e.tensor_copy(
                        xbf[:, pb:pb + 512], src(m)), "V")
                    vv = kb.cnt["V"]
                    kb.wait("scalar", "V", vv)
                    d("scalar", lambda e, pb=pb: e.activation(
                        x2bf[:, pb:pb + 512], xbf[:, pb:pb + 512], AF.Square,
                        bias=0.0, scale=1.0), "A")
                    va = kb.cnt["A"]
                    kb.wait("tensor", "V", vv)
                    if m == 0:
                        kb.guard_wait("tensor", ("bank", 0))
                        kb.guard_wait("tensor", ("bank", 1))
                    d("tensor", lambda e, pb=pb, m=m: e.matmul(
                        ps[0][0:1, 0:512], sumcolF[:, :], xbf[:, pb:pb + 512],
                        start=(m == 0), stop=(m == ND - 1)), None)
                    kb.wait("tensor", "A", va)
                    d("tensor", lambda e, pb=pb, m=m: e.matmul(
                        ps[1][0:1, 0:512], sumcolF[:, :], x2bf[:, pb:pb + 512],
                        start=(m == 0), stop=(m == ND - 1)), "T")
                    mm2_done[m] = kb.cnt["T"]
                v_stats = kb.cnt["T"]
                kb.wait("vector", "T", v_stats)
                R = statbuf[0]
                d("vector", lambda e: e.tensor_copy(R[:, 0:512], ps[0][0:1, 0:512]), None)
                d("vector", lambda e: e.tensor_tensor(R[:, 512:1024], R[:, 0:512],
                                                      R[:, 0:512], OP.mult), None)
                d("vector", lambda e: e.scalar_tensor_tensor(
                    R[:, 1024:1536], ps[1][0:1, 0:512], EPS, R[:, 512:1024],
                    OP.add, OP.subtract), None)
                d("vector", lambda e: e.reciprocal(R[:, 1536:2048], R[:, 1024:1536]), "V")
                v_free01 = kb.cnt["V"]
                kb.guard_set(("bank", 0), "V", v_free01)
                kb.guard_set(("bank", 1), "V", v_free01)
                kb.wait("scalar", "V", v_free01)
                d("scalar", lambda e: e.activation(R[:, 2048:2560], R[:, 1536:2048],
                                                   AF.Sqrt, bias=0.0, scale=1.0), "A")
                kb.wait("vector", "A", kb.cnt["A"])
                d("vector", lambda e: e.tensor_tensor(R[:, 2560:3072], R[:, 0:512],
                                                      R[:, 2048:2560], OP.mult), "V")
                v_rows = kb.cnt["V"]
                kb.wait("tensor", "V", v_rows)
                kb.guard_wait("tensor", ("bank", 2))
                kb.guard_wait("tensor", ("bank", 3))
                d("tensor", lambda e: e.matmul(ps[2][:, 0:512], onesrow[:, 0:128],
                                               R[:, 2048:2560], start=True, stop=True), None)
                vbc = d("tensor", lambda e: e.matmul(ps[3][:, 0:512], onesrow[:, 0:128],
                                                     R[:, 2560:3072], start=True, stop=True), "T")
                kb.wait("vector", "T", vbc)
                for m in range(ND):
                    sslot = m % 4
                    d("vector", lambda e, m=m: e.scalar_tensor_tensor(
                        t1[:, :], src(m), dcol("cb1", m), ps[2][:, 0:512],
                        OP.mult, OP.mult), None)
                    d("vector", lambda e, m=m: e.scalar_tensor_tensor(
                        t4[:, :], ps[3][:, 0:512], dcol("ncb1", m), t1[:, :],
                        OP.mult, OP.add), None)
                    kb.guard_wait("vector", ("stg", sslot))
                    vh2 = d("vector", lambda e, m=m, sslot=sslot: e.tensor_scalar(
                        stg[:, 512 * sslot:512 * (sslot + 1)], t4[:, :],
                        dcol("sb", m), None, OP.add), "V")
                    kb.wait("sync", "V", vh2)
                    vsp2 = kb.dma("sync",
                                  hap(sp_h2, (128 * m) * T + 512 * qq, [[T, 128], [1, 512]]),
                                  stg[:, 512 * sslot:512 * (sslot + 1)], f"stg{sslot}")
                    kb.guard_set(("stg", sslot), *vsp2)
                v_done = kb.cnt["V"]
                kb.guard_set(("bank", 2), "V", v_done)
                kb.guard_set(("bank", 3), "V", v_done)
                kb.guard_set(("xbuf4",), "V", v_done)

        def p5(E5, h2qfm):
            nonlocal tf32c, i32c
            h2tm = E5(nc.sbuf_tensor("h2tm", [128, 2 * D], BF16))
            i32c = E5(nc.sbuf_tensor("i32c", [128, 1024], I32))
            tf32c = E5(nc.sbuf_tensor("tf32c", [128, 1024], F32))
            h2qs = E5(nc.sbuf_tensor("h2qs", [128, 2 * D], BF16))
            mxc = E5(nc.sbuf_tensor("mxc", [128, 2], F32))
            fb_last = {}

            def tile_loads(tt):
                sl = tt % 2
                kb.guard_wait("sync", ("h2tm", sl))
                vls = []
                for j in range(4):
                    vls.append(kb.dma("sync", h2tm[:, sl * D + 512 * j:sl * D + 512 * (j + 1)],
                                      hap(sp_h2, (512 * j) * T + 128 * tt, [[T, 512], [1, 128]]),
                                      f"xb{j}{sl}", transpose=True))
                return vls

            LD5 = {0: tile_loads(0)}
            for tt in range(NT):
                sl = tt % 2
                if tt + 1 < NT:
                    LD5[tt + 1] = tile_loads(tt + 1)
                for v_ in LD5[tt]:
                    kb.dwait("vector", v_)
                d("vector", lambda e, sl=sl: e.tensor_reduce(
                    mxc[:, sl:sl + 1], h2tm[:, sl * D:(sl + 1) * D], AX.X, OP.max,
                    apply_absolute_value=True), "V")
                d("vector", lambda e, sl=sl, tt=tt: e.tensor_scalar(
                    s_all[:, tt:tt + 1], mxc[:, sl:sl + 1], 1.0 / 127.0, EPS,
                    OP.mult, OP.add), "V")
                d("vector", lambda e, tt=tt: e.reciprocal(
                    srq[tt % 2][:, :], s_all[:, tt:tt + 1]), "V")
                kb.guard_wait("vector", ("h2qs", sl))
                for half in range(2):
                    d("vector", lambda e, sl=sl, tt=tt, half=half: e.tensor_scalar(
                        tf32c[:, :],
                        h2tm[:, sl * D + 1024 * half:sl * D + 1024 * (half + 1)],
                        srq[tt % 2][:, :], MAGIC, OP.mult, OP.add), "V")
                    d("vector", lambda e: e.tensor_scalar(
                        i32c[:, :], tf32c[:, :],
                        -MAGIC, None, OP.add), "V")
                    d("vector", lambda e, sl=sl, half=half: e.tensor_scalar(
                        h2qs[:, sl * D + 1024 * half:sl * D + 1024 * (half + 1)],
                        i32c[:, :],
                        -129.0, 129.0, OP.max, OP.min), "V")
                vq = kb.cnt["V"]
                kb.guard_set(("h2tm", sl), "V", vq)
                kb.wait("sync", "V", vq)
                vsq = kb.dma("sync", hap(sp_h2q, (128 * tt) * D, [[D, 128], [1, D]]),
                             h2qs[:, sl * D:(sl + 1) * D], f"h2q{sl}")
                kb.guard_set(("h2qs", sl), *vsq)
                if tt % 4 == 3:
                    j = tt // 4
                    kb.wait("sync", "D_h2q0", kb.cnt.get("D_h2q0", 0))
                    kb.wait("sync", "D_h2q1", kb.cnt.get("D_h2q1", 0))
                    for m in range(ND):
                        vfb = kb.dma("sync",
                                     h2qfm[:, m * T + 512 * j:m * T + 512 * (j + 1)],
                                     hap(sp_h2q, (512 * j) * D + 128 * m, [[D, 512], [1, 128]]),
                                     f"fb{m % 4}", transpose=True)
                        fb_last[m % 4] = vfb
                    kb.guards[("h2qfm", j)] = list(fb_last.values())
                kb.tick6(1)
            kb.guards[("h2qfm",)] = list(fb_last.values())

        def p5w_gen(E5w):
            wsb = E5w(nc.sbuf_tensor("wsb", [128, 2 * 4096], F32))
            acc = E5w(nc.sbuf_tensor("acc", [128, 4], F32))
            onescolF = E5w(nc.sbuf_tensor("onescolF", [128, 1], F32))
            d("vector", lambda e: e.memset(onescolF[:, :], 1.0), "V")
            d("vector", lambda e: e.memset(wsum[:, :], 0.0), "V")
            d("vector", lambda e: e.memset(ws_row[:, :], 0.0), "V")
            d("vector", lambda e: e.memset(wsr_c[:, :], 0.0), "V")
            yield
            for widx, wt in enumerate((wgT, wuT, wdT)):
                NP = (D * II) // (128 * 4096)
                for pc in range(NP):
                    sl = pc % 2
                    kb.guard_wait("scalar", ("wsb", sl))
                    vw = kb.dma("scalar", wsb[:, sl * 4096:(sl + 1) * 4096],
                                hap(wt, pc * 128 * 4096, [[4096, 128], [1, 4096]]),
                                f"ws{sl}")
                    kb.dwait("vector", vw)
                    kb.guard_wait("vector", ("acc", sl))
                    va = d("vector", lambda e, sl=sl: e.tensor_reduce(
                        acc[:, sl:sl + 1], wsb[:, sl * 4096:(sl + 1) * 4096],
                        AX.X, OP.add, apply_absolute_value=True), "V")
                    kb.guard_set(("wsb", sl), "V", va)
                    vacc = d("vector", lambda e, widx=widx, sl=sl: e.tensor_tensor(
                        wsum[:, widx:widx + 1], wsum[:, widx:widx + 1],
                        acc[:, sl:sl + 1], OP.add), "V")
                    kb.guard_set(("acc", sl), "V", vacc)
                    yield
                vws = kb.cnt["V"]
                kb.wait("tensor", "V", vws)
                kb.guard_wait("tensor", ("bank", 5))
                vmm = d("tensor", lambda e, widx=widx: e.matmul(
                    ps[5][0:1, 0:1], wsum[:, widx:widx + 1], onescolF[:, :],
                    start=True, stop=True), "T")
                kb.wait("vector", "T", vmm)
                d("vector", lambda e, widx=widx: e.tensor_scalar(
                    ws_row[:, widx:widx + 1], ps[5][0:1, 0:1],
                    1.0 / float(D * II), EPS, OP.mult, OP.add), "V")
                d("vector", lambda e, widx=widx: e.reciprocal(
                    ws_row[:, 4 + widx:5 + widx], ws_row[:, widx:widx + 1]), "V")
                vrw = kb.cnt["V"]
                kb.guard_set(("bank", 5), "V", vrw)
                kb.wait("tensor", "V", vrw)
                kb.guard_wait("tensor", ("bank", 4))
                d("tensor", lambda e, widx=widx: e.matmul(
                    ps[4][:, 0:1], onesrow[:, 0:128], ws_row[:, widx:widx + 1],
                    start=True, stop=True), None)
                vb = d("tensor", lambda e, widx=widx: e.matmul(
                    ps[4][:, 1:2], onesrow[:, 0:128], ws_row[:, 4 + widx:5 + widx],
                    start=True, stop=True, skip_group_check=True), "T")
                kb.wait("vector", "T", vb)
                d("vector", lambda e, widx=widx: e.tensor_copy(
                    wsr_c[:, widx:widx + 1], ps[4][:, 1:2]), "V")
                d("vector", lambda e, widx=widx: e.tensor_copy(
                    wsr1[widx][:, :], ps[4][:, 1:2]), "V")
                vcc2 = d("vector", lambda e, widx=widx: e.tensor_copy(
                    sgcol[:, 32 + widx:33 + widx] if False else wsum[:, 4 + widx:5 + widx],
                    ps[4][:, 0:1]), "V")
                kb.guard_set(("bank", 4), "V", vcc2)

        def p5wd():
            if "dmp_ws" in kb.dump:
                kb.wait("sync", "V", kb.cnt["V"])
                kb.dma("sync", dumps["dmp_ws"][:, :], ws_row[:, :], "dws")
            if "dmp_wsr" in kb.dump:
                kb.wait("sync", "V", kb.cnt["V"])
                kb.dma("sync", dumps["dmp_wsr"][:, 0:4], wsr_c[:, :], "dwsr")
                kb.dma("sync", dumps["dmp_wsr"][:, 4:8], wsum[:, 4:8], "dwsr2")

        def p65_gen(E65):
            wdf = E65(nc.sbuf_tensor("wdf", [128, 2048], F32))
            wqs = E65(nc.sbuf_tensor("wqs", [128, 2048], BF16))
            yield
            for pg in range(NIC):
                sl = pg % 2
                kb.guard_wait("gpsimd", ("wdf",))
                vw = kb.dma("gpsimd", wdf[:, :],
                            hap(wdT, (128 * pg) * D, [[D, 128], [1, D]]),
                            f"wd{sl}")
                kb.dwait("vector", vw)
                kb.guard_wait("vector", ("wqs",))
                for hf in range(2):
                    kb.guard_wait("vector", ("i32c",))
                    d("vector", lambda e, sl=sl, hf=hf: e.tensor_scalar(
                        tf32c[:, :], wdf[:, 1024 * hf:1024 * (hf + 1)],
                        wsr1[2][:, :], MAGIC, OP.mult, OP.add), "V")
                    d("vector", lambda e: e.tensor_scalar(
                        i32c[:, :], tf32c[:, :], -MAGIC, None, OP.add), "V")
                    vq = d("vector", lambda e, hf=hf: e.tensor_scalar(
                        wqs[:, 1024 * hf:1024 * (hf + 1)], i32c[:, :],
                        -1.0, 1.0, OP.max, OP.min), "V")
                    kb.guards[("i32c",)] = [("V", vq)]
                kb.guard_set(("wdf",), "V", vq)
                kb.wait("sync", "V", vq)
                vsp = kb.dma("sync",
                             hap(sp_wqd, pg * (128 * 128),
                                 [[128, 128], [NIC * 16384, ND], [1, 128]]),
                             wqs[:, :].rearrange("p (m c) -> p m c", c=128),
                             f"wqs{sl}")
                kb.guard_set(("wqs",), *vsp)
                yield

        def make_p6pre(E):
            wf32 = E(nc.sbuf_tensor("wf32", [128, 2048], F32))
            wqgbf = E(nc.sbuf_tensor("wqgbf", [128, 2 * ND * 512], BF16))
            wqubf = E(nc.sbuf_tensor("wqubf", [128, 2 * ND * 512], BF16))
            vq_ic = {}
            v_sg = {}

            def emit_piece(ic, pidx):
                wsel, qp = divmod(pidx, 4)
                wt = wgT if wsel == 0 else wuT
                wqb = wqgbf if wsel == 0 else wqubf
                icsl = ic % 2
                kb.guard_wait("gpsimd", ("wf32",))
                vw = kb.dma("gpsimd", wf32[:, :].rearrange("p (k c) -> p k c", c=128),
                            hap(wt, 512 * ic + 128 * qp,
                                [[II, 128], [128 * II, ND], [1, 128]]),
                            f"wf{pidx % 2}")
                kb.dwait("vector", vw)
                if pidx == 0:
                    kb.guard_wait("vector", ("wq", icsl))
                for hf in range(2):
                    kb.guard_wait("vector", ("i32c",))
                    d("vector", lambda e, wsel=wsel, hf=hf: e.tensor_scalar(
                        tf32c[:, :], wf32[:, 1024 * hf:1024 * (hf + 1)],
                        wsr1[wsel][:, :], MAGIC, OP.mult, OP.add), "V")
                    d("vector", lambda e: e.tensor_scalar(
                        i32c[:, :], tf32c[:, :], -MAGIC, None, OP.add), "V")
                    vq = d("vector", lambda e, wqb=wqb, qp=qp, icsl=icsl, hf=hf: e.tensor_scalar(
                        wqb[:, icsl * ND * 512 + hf * 8 * 512:icsl * ND * 512 + hf * 8 * 512 + 8 * 512]
                        .rearrange("p (k c) -> p k c", c=512)[:, :, 128 * qp:128 * (qp + 1)],
                        i32c[:, :].rearrange("p (k c) -> p k c", c=128),
                        -1.0, 1.0, OP.max, OP.min), "V")
                    kb.guards[("i32c",)] = [("V", vq)]
                kb.guard_set(("wf32",), "V", vq)
                return vq

            def gen():
                d("vector", lambda e: e.memset(gmax[:, :], 0.0), "V")
                yield
                for k in range(NT):
                    if k < 8:
                        vq_ic[0] = emit_piece(0, k)
                    d("vector", lambda e, k=k: e.tensor_tensor(
                        sgcol[:, k:k + 1], s_all[:, k:k + 1],
                        wsum[:, 4:5], OP.mult), "V")
                    d("vector", lambda e, k=k: e.tensor_tensor(
                        sgcol[:, NT + k:NT + k + 1], s_all[:, k:k + 1],
                        wsum[:, 5:6], OP.mult), "V")
                    v_sg[k] = kb.cnt["V"]
                    yield

            return gen(), emit_piece, vq_ic, v_sg, wqgbf, wqubf

        def p6(E6, h2qfm, p6x):
            emit_piece, vq_ic, v_sg, wqgbf, wqubf = p6x
            sgb = E6(nc.sbuf_tensor("sgb", [128, 2 * 512], BF16))
            gub = E6(nc.sbuf_tensor("gub", [128, 2 * 512], BF16))
            mx2 = E6(nc.sbuf_tensor("mx2", [128, 2], F32))
            g65 = p65_gen(E6)
            next(g65)

            def tick65(n=1):
                for _ in range(n):
                    if next(g65, "END") == "END":
                        return

            for ic in range(NI):
                icsl = ic % 2
                kb.wait("tensor", "V", vq_ic[ic])
                for tt in range(NT):
                    gbank = tt % 2
                    ubank = 2 + tt % 2
                    kb.guard_wait("tensor", ("h2qfm", tt // 4))
                    kb.guard_wait("tensor", ("bank", gbank))
                    for k in range(ND):
                        d("tensor", lambda e, k=k, tt=tt, gbank=gbank, icsl=icsl: e.matmul(
                            ps[gbank][:, 0:512],
                            h2qfm[:, k * T + 128 * tt:k * T + 128 * (tt + 1)],
                            wqgbf[:, icsl * ND * 512 + 512 * k:icsl * ND * 512 + 512 * (k + 1)],
                            start=(k == 0), stop=(k == ND - 1)),
                            "T" if k == ND - 1 else None)
                    vtg = kb.cnt["T"]
                    kb.guard_wait("tensor", ("bank", ubank))
                    for k in range(ND):
                        d("tensor", lambda e, k=k, tt=tt, ubank=ubank, icsl=icsl: e.matmul(
                            ps[ubank][:, 0:512],
                            h2qfm[:, k * T + 128 * tt:k * T + 128 * (tt + 1)],
                            wqubf[:, icsl * ND * 512 + 512 * k:icsl * ND * 512 + 512 * (k + 1)],
                            start=(k == 0), stop=(k == ND - 1)),
                            "T" if k == ND - 1 else None)
                    vtu = kb.cnt["T"]
                    if tt == NT - 1:
                        kb.guard_set(("wq", icsl), "T", vtu)
                    kb.wait("scalar", "T", vtg)
                    kb.wait("scalar", "V", v_sg[tt])
                    kb.guard_wait("scalar", ("sgb", tt % 2))
                    vsg = d("scalar", lambda e, tt=tt, gbank=gbank: e.activation(
                        sgb[:, 512 * (tt % 2):512 * (tt % 2 + 1)], ps[gbank][:, 0:512],
                        AF.Silu, bias=0.0, scale=sgcol[:, tt:tt + 1]), "A")
                    kb.guard_set(("bank", gbank), "A", vsg)
                    kb.wait("vector", "T", vtu)
                    kb.wait("vector", "A", vsg)
                    kb.guard_wait("vector", ("gub", tt % 2))
                    vgu = d("vector", lambda e, tt=tt, ubank=ubank: e.scalar_tensor_tensor(
                        gub[:, 512 * (tt % 2):512 * (tt % 2 + 1)], ps[ubank][:, 0:512],
                        sgcol[:, NT + tt:NT + tt + 1],
                        sgb[:, 512 * (tt % 2):512 * (tt % 2 + 1)],
                        OP.mult, OP.mult), "V")
                    kb.guard_set(("bank", ubank), "V", vgu)
                    kb.guard_set(("sgb", tt % 2), "V", vgu)
                    d("vector", lambda e, tt=tt: e.tensor_reduce(
                        mx2[:, tt % 2:tt % 2 + 1], gub[:, 512 * (tt % 2):512 * (tt % 2 + 1)],
                        AX.X, OP.max, apply_absolute_value=True), "V")
                    vmx = d("vector", lambda e, tt=tt: e.tensor_tensor(
                        gmax[:, tt:tt + 1], gmax[:, tt:tt + 1],
                        mx2[:, tt % 2:tt % 2 + 1], OP.max), "V")
                    kb.wait("sync", "V", vgu)
                    vsp = kb.dma("sync",
                                 hap(sp_gu, (128 * tt) * II + 512 * ic, [[II, 128], [1, 512]]),
                                 gub[:, 512 * (tt % 2):512 * (tt % 2 + 1)], f"gu{tt % 2}")
                    kb.guard_set(("gub", tt % 2), *vsp)
                    if ic + 1 < NI and tt % 2 == 0:
                        vq_ic[ic + 1] = emit_piece(ic + 1, tt // 2)
                    if tt in (1, 5, 9, 13):
                        tick65()
            for _ in range(NIC):
                if next(g65, "END") == "END":
                    break

        def p78(E8):
            gut = E8(nc.sbuf_tensor("gut", [128, II], BF16))
            i32g = E8(nc.sbuf_tensor("i32g", [128, 512], I32))
            tf32g = E8(nc.sbuf_tensor("tf32g", [128, 512], F32))

            s2c = E8(nc.sbuf_tensor("s2c", [128, NT], F32))
            guqf = E8(nc.sbuf_tensor("guqf", [128, NIC * 512], BF16))
            wdb = E8(nc.sbuf_tensor("wdb", [128, 3 * NIC * 128], BF16))
            x1qb = E8(nc.sbuf_tensor("x1qb", [128, ND * 512], F32))
            xs2row = E8(nc.sbuf_tensor("xs2row", [1, T], F32))
            xsb = E8(nc.sbuf_tensor("xsb", [128, 512], F32))
            t8 = E8(nc.sbuf_tensor("t8", [128, 512], F32))
            out8 = E8(nc.sbuf_tensor("out8", [128, 2 * 512], F32))

            def p7_tile(tt):
                sl = tt % 2
                kb.guard_wait("sync", ("gut",))
                vg = kb.dma("sync", gut[:, :],
                            hap(sp_gu, (128 * tt) * II, [[II, 128], [1, II]]), f"gut{sl}")
                d("vector", lambda e, tt=tt: e.tensor_scalar(
                    s2c[:, tt:tt + 1], gmax[:, tt:tt + 1], 1.0 / 127.0, EPS,
                    OP.mult, OP.add), "V")
                d("vector", lambda e, tt=tt: e.reciprocal(
                    s2q[tt % 2][:, :], s2c[:, tt:tt + 1]), "V")
                kb.dwait("vector", vg)
                kb.guard_wait("vector", ("guq",))
                for half in range(16):
                    d("vector", lambda e, tt=tt, half=half: e.tensor_scalar(
                        tf32g[:, :], gut[:, 512 * half:512 * (half + 1)],
                        s2q[tt % 2][:, :], MAGIC, OP.mult, OP.add), "V")
                    d("vector", lambda e: e.tensor_scalar(
                        i32g[:, :], tf32g[:, :], -MAGIC, None, OP.add), "V")
                    d("vector", lambda e, half=half: e.tensor_scalar(
                        gut[:, 512 * half:512 * (half + 1)], i32g[:, :],
                        -129.0, 129.0, OP.max, OP.min), "V")
                vq = kb.cnt["V"]
                kb.guard_set(("gut",), "V", vq)
                kb.wait("sync", "V", vq)
                vsp = kb.dma("sync", hap(sp_guq, (128 * tt) * II, [[II, 128], [1, II]]),
                             gut[:, :], f"guq{tt % 2}")
                kb.guard_set(("guq",), *vsp)
                kb.guard_add(("gut",), *vsp)
                sp7[tt // 4][vsp[0]] = vsp

            def xs2_prep(seg):
                kb.wait("tensor", "V", kb.cnt["V"])
                kb.guard_wait("tensor", ("bank", 5))
                vtr = d("tensor", lambda e, seg=seg: e.transpose(
                    ps[5][0:4, 0:128], s2c[:, 4 * seg:4 * seg + 4], identF[:, :]), "T")
                kb.wait("vector", "T", vtr)
                kb.guard_wait("vector", ("stgf", 0))
                vev = d("vector", lambda e: e.tensor_copy(
                    stg_f[0:4, 0:128], ps[5][0:4, 0:128]), "V")
                kb.guard_set(("bank", 5), "V", vev)
                kb.wait("sync", "V", vev)
                for v_ in gq_last.values():
                    kb.dwait("sync", v_)
                vdm = kb.dma("sync",
                             xs2row[:, 512 * seg:512 * (seg + 1)]
                             .rearrange("o (r c) -> o r c", c=128),
                             stg_f[0:4, 0:128], "xs2g")
                kb.guard_set(("stgf", 0), *vdm)
                kb.dwait("vector", vdm)
                vxr = d("vector", lambda e, seg=seg: e.tensor_scalar(
                    xs2row[:, 512 * seg:512 * (seg + 1)],
                    xs2row[:, 512 * seg:512 * (seg + 1)],
                    ws_row[:, 2:3], None, OP.mult), "V")
                xs2v[seg] = vxr
                xs2dma[seg] = vdm

            sp7 = [dict() for _ in range(NQ)]
            xs2v = {}
            xs2dma = {}
            gq_last = {}
            for tt in range(4):
                p7_tile(tt)
            xs2_prep(0)

            for seg in range(NQ):
                kb.guard_wait("scalar", ("x1qb",))
                vx1 = kb.dma("scalar", x1qb[:, :].rearrange("p (m t) -> p m t", t=512),
                             hap(sp_x1, 512 * seg, [[T, 128], [128 * T, ND], [1, 512]]),
                             "x1q")
                kb.dwait("sync", xs2dma[seg])
                kb.dwait("scalar", xs2dma[seg])
                for v_ in sp7[seg].values():
                    kb.dwait("sync", v_)
                    kb.dwait("scalar", v_)
                kb.guard_wait("sync", ("guqf",))
                kb.guard_wait("scalar", ("guqf",))
                for ic in range(NIC):
                    gq_last[ic % 4] = kb.dma("sync",
                                             guqf[:, ic * 512:(ic + 1) * 512],
                                             hap(sp_guq, (512 * seg) * II + 128 * ic,
                                                 [[II, 512], [1, 128]]),
                                             f"gq{ic % 4}", transpose=True)
                kb.wait("tensor", "V", xs2v[seg])
                kb.guard_wait("tensor", ("bank", 5))
                vb3 = d("tensor", lambda e, seg=seg: e.matmul(
                    ps[5][:, 0:512], onesrow[:, 0:128],
                    xs2row[:, 512 * seg:512 * (seg + 1)], start=True, stop=True), "T")
                kb.wait("vector", "T", vb3)
                kb.guard_wait("vector", ("xsb",))
                vxsb = d("vector", lambda e: e.tensor_scalar(
                    xsb[:, :], ps[5][:, 0:512], 1.0, None, OP.mult), "V")
                kb.guard_set(("bank", 5), "V", vxsb)

                for m in range(ND):
                    msl = (seg * ND + m) % 3
                    kb.guard_wait("gpsimd", ("wdb", msl))
                    vwd = kb.dma("gpsimd", wdb[:, msl * NIC * 128:(msl + 1) * NIC * 128],
                                 hap(sp_wqd, m * (NIC * 128 * 128),
                                     [[128, 128], [128 * 128, NIC], [1, 128]]),
                                 f"wdb{msl}")
                    bank = m % 2
                    for v_ in gq_last.values():
                        kb.dwait("tensor", v_)
                    kb.dwait("tensor", vwd)
                    kb.guard_wait("tensor", ("bank", bank))
                    for ic in range(NIC):
                        d("tensor", lambda e, ic=ic, msl=msl, bank=bank: e.matmul(
                            ps[bank][:, 0:512],
                            wdb[:, msl * NIC * 128 + 128 * ic:msl * NIC * 128 + 128 * (ic + 1)],
                            guqf[:, ic * 512:(ic + 1) * 512],
                            start=(ic == 0), stop=(ic == NIC - 1)),
                            "T" if ic == NIC - 1 else None)
                    vT = kb.cnt["T"]
                    kb.guard_set(("wdb", msl), "T", vT)
                    kb.wait("vector", "T", vT)
                    kb.dwait("vector", vx1)
                    d("vector", lambda e, bank=bank: e.tensor_tensor(
                        t8[:, :], ps[bank][:, 0:512],
                        xsb[:, :], OP.mult), "V")
                    osl = m % 2
                    kb.guard_wait("vector", ("out8", osl))
                    vout = d("vector", lambda e, m=m, osl=osl: e.scalar_tensor_tensor(
                        out8[:, 512 * osl:512 * (osl + 1)], t8[:, :],
                        dcol("gb2", m), x1qb[:, 512 * m:512 * (m + 1)],
                        OP.mult, OP.add), "V")
                    kb.guard_set(("bank", bank), "V", vout)
                    kb.wait("sync", "V", vout)
                    vy = kb.dma("sync",
                                hap(y_fm, (128 * m) * T + 512 * seg, [[T, 128], [1, 512]]),
                                out8[:, 512 * osl:512 * (osl + 1)], f"y{m % 4}")
                    kb.guard_set(("out8", osl), *vy)
                    if seg + 1 < NQ:
                        if m in (2, 5, 8, 11):
                            p7_tile(4 * (seg + 1) + (m - 2) // 3)
                        elif m == 13:
                            xs2_prep(seg + 1)
                kb.guard_set(("guqf",), "T", kb.cnt["T"])
                kb.guard_set(("x1qb",), "V", kb.cnt["V"])
                kb.guard_set(("xsb",), "V", kb.cnt["V"])

        with ExitStack() as s0:
            p0(s0.enter_context)
        kb.barrier()
        with ExitStack() as s12:
            hbuf[0] = s12.enter_context(nc.sbuf_tensor("h_all", [128, ND * T], BF16))
            g5w = p5w_gen(s12.enter_context)
            next(g5w)

            def tick5w(n=1):
                for _ in range(n):
                    if next(g5w, "END") == "END":
                        return
            kb.tick5w = tick5w
            with ExitStack() as s1:
                p1(s1.enter_context)
            kb.barrier()
            if stop_after != "p1":
                with ExitStack() as s2:
                    p2(s2.enter_context)
                for _ in range(120):
                    if next(g5w, "END") == "END":
                        break
                kb.barrier()
        if stop_after not in ("p1", "p2"):
            with ExitStack() as s3:
                p3(s3.enter_context)
            kb.barrier()
        if stop_after not in ("p1", "p2", "p3"):
            with ExitStack() as s4:
                p4(s4.enter_context)
            kb.barrier()
            with ExitStack() as s56:
                h2qfm = s56.enter_context(nc.sbuf_tensor("h2qfm", [128, ND * T], BF16))
                g6pre, ep6, vq6, vsg6, wqg6, wqu6 = make_p6pre(s56.enter_context)
                next(g6pre)

                def tick6(n=1):
                    for _ in range(n):
                        if next(g6pre, "END") == "END":
                            return
                kb.tick6 = tick6
                p5(s56.enter_context, h2qfm)
                for _ in range(NT):
                    if next(g6pre, "END") == "END":
                        break
                if kb.dump:
                    kb.barrier()
                    p5wd()
                    kb.barrier()
                with ExitStack() as s6:
                    p6(s6.enter_context, h2qfm, (ep6, vq6, vsg6, wqg6, wqu6))
                kb.barrier()
            with ExitStack() as s8:
                p78(s8.enter_context)
            kb.barrier()
        else:
            zz = E(nc.sbuf_tensor("zz", [128, 512], F32))
            d("vector", lambda e: e.memset(zz[:, :], 0.0), "V")
            kb.wait("sync", "V", kb.cnt["V"])
            for m in range(ND):
                for qq in range(NQ):
                    kb.dma("sync", y_fm[128 * m:128 * (m + 1), 512 * qq:512 * (qq + 1)],
                           zz[:, :], "out")
            kb.barrier()

        with ExitStack() as sctx:
            for nm in kb.sem_names:
                kb.S[nm] = sctx.enter_context(nc.semaphore(nm))
            with nc.Block() as block:
                for eng in ENGINES:
                    if not kb.prog[eng]:
                        continue

                    def mk(engname):
                        def body(e):
                            for f in kb.prog[engname]:
                                f(e)
                        return body
                    getattr(block, eng)(mk(eng))

    return nc


def make_in_maps(inputs):
    xs = np.asarray(inputs["L0_attn_x"], np.float32)
    shared = {
        "wmod": np.asarray(inputs["L0_wmod"], np.float32),
        "wq": np.asarray(inputs["L0_attn_Wq"], np.float32),
        "wk": np.asarray(inputs["L0_attn_Wk"], np.float32),
        "wv": np.asarray(inputs["L0_attn_Wv"], np.float32),
        "wg": np.asarray(inputs["L0_attn_Wg"], np.float32),
        "wo": np.asarray(inputs["L0_attn_Wo"], np.float32),
        "wgk1": np.asarray(inputs["L0_attn_Wgk1"], np.float32),
        "wgk2": np.asarray(inputs["L0_attn_Wgk2"], np.float32),
        "bgk2": np.asarray(inputs["L0_attn_bgk2"], np.float32).reshape(1, D),
        "gnw": np.asarray(inputs["L0_attn_gnorm_w"], np.float32).reshape(1, HD),
        "wgT": np.ascontiguousarray(np.asarray(inputs["L0_ffn_w_gate"], np.float32).T),
        "wuT": np.ascontiguousarray(np.asarray(inputs["L0_ffn_w_up"], np.float32).T),
        "wdT": np.ascontiguousarray(np.asarray(inputs["L0_ffn_w_down"], np.float32).T),
    }
    in_maps = []
    for b in range(8):
        m = dict(shared)
        m["x_fm"] = np.ascontiguousarray(xs[b].T)
        m["c_row"] = np.ascontiguousarray(
            np.asarray(inputs["L0_c"], np.float32)[b:b + 1, :])
        in_maps.append(m)
    return in_maps


def kernel(**inputs):
    in_maps = make_in_maps(inputs)
    nc = build_nc()
    res = run_bass_kernel_spmd(nc, in_maps, core_ids=list(range(8)))
    out = np.empty((8, T, D), np.float32)
    for b in range(8):
        out[b] = res.results[b]["y_fm"].T
    return out


# version 14
# speedup vs baseline: 1.0203x; 1.0203x over previous
import numpy as np
import ml_dtypes
from contextlib import ExitStack

import concourse.bass as bass
import concourse.mybir as mybir
from concourse.bass_utils import run_bass_kernel_spmd

F32 = mybir.dt.float32
BF16 = mybir.dt.bfloat16
I32 = mybir.dt.int32
AF = mybir.ActivationFunctionType
OP = mybir.AluOpType
AX = mybir.AxisListType

D = 2048
T = 2048
HN = 16
HD = 128
CC = 128
NCH = T // CC
ND = D // 128
NT = T // 128
NQ = T // 512
II = 8192
NI = II // 512
NIC = II // 128
GLN = 16.0
EPS = 1e-5
SCALE = float(HD) ** -0.5
MAGIC = 12582912.0

ENGINES = ("sync", "tensor", "vector", "scalar", "gpsimd")
DMASEM = {"sync": "dS", "scalar": "dA", "gpsimd": "dG"}


def hap(handle, offset, dims):
    return bass.AP(handle, offset, [list(x) for x in dims])


class KB:

    def __init__(self, nc, dump=()):
        self.nc = nc
        self.dump = set(dump)
        self.prog = {e: [] for e in ENGINES}
        self.cnt = {}
        self.lastwait = {}
        self.S = {}
        self.sem_names = []
        self.guards = {}

    def sem(self, *names):
        for n in names:
            if n not in self.cnt:
                self.cnt[n] = 0
                self.sem_names.append(n)

    OWN = {"vector": "V", "scalar": "A", "gpsimd": "G", "tensor": "T"}

    def do(self, eng, fn, sem=None, n=1):
        if eng in ("vector", "scalar", "gpsimd"):
            own = self.OWN[eng]
            self.wait(eng, own, self.cnt[own])
            if sem is None:
                sem = own
        if sem is None:
            self.prog[eng].append(fn)
            return None
        self.cnt[sem] += n
        v = self.cnt[sem]

        def wrapped(e, f=fn, s=sem, k=n):
            f(e).then_inc(self.S[s], k)
        self.prog[eng].append(wrapped)
        return v

    def wait(self, eng, sem, v):
        if v is None or v <= 0:
            return
        key = (eng, sem)
        if self.lastwait.get(key, -1) >= v:
            return
        self.lastwait[key] = v

        def wrapped(e, s=sem, vv=v):
            e.wait_ge(self.S[s], vv)
        self.prog[eng].append(wrapped)

    def dma(self, eng, dst, src, stream="misc", transpose=False):
        s = "D_" + stream
        self.sem(s)
        self.cnt[s] += 16
        v = self.cnt[s]

        def wrapped(e, dd=dst, sr=src, tr=transpose, ss=s):
            if tr:
                e.dma_start(dd, sr, transpose=True).then_inc(self.S[ss], 16)
            else:
                e.dma_start(dd, sr).then_inc(self.S[ss], 16)
        self.prog[eng].append(wrapped)
        return (s, v)

    def dwait(self, eng, sv):
        self.wait(eng, sv[0], sv[1])

    def guard_wait(self, eng, key):
        ent = self.guards.get(key)
        if ent:
            for s, v in ent:
                self.wait(eng, s, v)

    def guard_set(self, key, sem, val):
        self.guards[key] = [(sem, val)]

    def guard_add(self, key, sem, val):
        self.guards.setdefault(key, []).append((sem, val))

    def barrier(self, engines=ENGINES):
        snap = dict(self.cnt)
        for eng in engines:
            for s, v in snap.items():
                self.wait(eng, s, v)


def build_nc(dump=(), stop_after=None, p3_chunks=NCH, p3_level=9):
    nc = bass.Bass("TRN2", target_bir_lowering=False)
    kb = KB(nc, dump)
    kb.sem("T", "V", "A", "G")

    x_fm = nc.dram_tensor("x_fm", [D, T], F32, kind="ExternalInput")
    c_row = nc.dram_tensor("c_row", [1, D], F32, kind="ExternalInput")
    wmod = nc.dram_tensor("wmod", [D, 6 * D], F32, kind="ExternalInput")
    wq_h = nc.dram_tensor("wq", [D, D], F32, kind="ExternalInput")
    wk_h = nc.dram_tensor("wk", [D, D], F32, kind="ExternalInput")
    wv_h = nc.dram_tensor("wv", [D, D], F32, kind="ExternalInput")
    wg_h = nc.dram_tensor("wg", [D, D], F32, kind="ExternalInput")
    wo_h = nc.dram_tensor("wo", [D, D], F32, kind="ExternalInput")
    wgk1 = nc.dram_tensor("wgk1", [D, 16], F32, kind="ExternalInput")
    wgk2 = nc.dram_tensor("wgk2", [16, D], F32, kind="ExternalInput")
    bgk2 = nc.dram_tensor("bgk2", [1, D], F32, kind="ExternalInput")
    gnw_h = nc.dram_tensor("gnw", [1, HD], F32, kind="ExternalInput")
    wgT = nc.dram_tensor("wgT", [D, II], F32, kind="ExternalInput")
    wuT = nc.dram_tensor("wuT", [D, II], F32, kind="ExternalInput")
    wdT = nc.dram_tensor("wdT", [II, D], F32, kind="ExternalInput")

    y_fm = nc.dram_tensor("y_fm", [D, T], F32, kind="ExternalOutput")

    def scratch(name, shape, dt):
        kind = "ExternalOutput" if name in kb.dump else "Internal"
        return nc.dram_tensor(name, shape, dt, kind=kind)

    sp_q = scratch("sp_q", [NCH, ND, 128, 128], BF16)
    sp_k = scratch("sp_k", [NCH, ND, 128, 128], BF16)
    sp_g = scratch("sp_g", [NCH, ND, 128, 128], BF16)
    sp_v = scratch("sp_v", [T, D], BF16)
    sp_ktm = scratch("sp_ktm", [T, D], BF16)
    sp_sp = scratch("sp_sp", [T, D], BF16)
    sp_op = scratch("sp_op", [D, T], BF16)
    sp_x1 = scratch("sp_x1", [D, T], F32)
    sp_h2 = scratch("sp_h2", [D, T], BF16)
    sp_h2q = scratch("sp_h2q", [T, D], BF16)
    sp_gu = scratch("sp_gu", [T, II], BF16)
    sp_guq = scratch("sp_guq", [T, II], BF16)
    sp_wqd = scratch("sp_wqd", [ND, NIC, 128, 128], BF16)

    dumps = {}
    for nm, shape, dt in [("dmp_mod", [1, 6 * D], F32), ("dmp_h", [D, T], BF16),
                          ("dmp_o", [D, T], F32), ("dmp_ws", [1, 8], F32),
                          ("dmp_wsr", [128, 8], F32)]:
        if nm in kb.dump:
            dumps[nm] = nc.dram_tensor(nm, shape, dt, kind="ExternalOutput")

    with ExitStack() as ctx:
        E = ctx.enter_context

        identB = E(nc.sbuf_tensor("identB", [128, 128], BF16))
        identF = E(nc.sbuf_tensor("identF", [128, 128], F32))
        U_bf = E(nc.sbuf_tensor("U_bf", [128, 128], BF16))
        U4 = E(nc.sbuf_tensor("U4", [128, 512], BF16))
        gnwsel = E(nc.sbuf_tensor("gnwsel", [16, HN * 128], BF16))
        iota_i = E(nc.sbuf_tensor("iota_i", [128, 128], I32))
        cg = E(nc.sbuf_tensor("cg", [128, 16 * HN], BF16))
        ones1 = E(nc.sbuf_tensor("ones1", [1, 1], F32))
        onesrow = E(nc.sbuf_tensor("onesrow", [1, 512], F32))
        sumcolF = E(nc.sbuf_tensor("sumcolF", [128, 1], BF16))
        gn_col = E(nc.sbuf_tensor("gn_col", [128, 1], BF16))
        dcols = E(nc.sbuf_tensor("dcols", [128, 8 * ND], F32))
        gnw_row = E(nc.sbuf_tensor("gnw_row", [1, HD], BF16))
        gnw_f32 = E(nc.sbuf_tensor("gnw_f32", [1, HD], F32))
        sc_cols = E(nc.sbuf_tensor("sc_cols", [128, ND], BF16))
        statbuf = [None]
        s_all = E(nc.sbuf_tensor("s_all", [128, NT], F32))
        srecip = E(nc.sbuf_tensor("srecip", [128, NT], F32))
        gmax = E(nc.sbuf_tensor("gmax", [128, NT], F32))
        s2r = E(nc.sbuf_tensor("s2r", [128, NT], F32))

        wsum = E(nc.sbuf_tensor("wsum", [128, 8], F32))
        wsr_c = E(nc.sbuf_tensor("wsr_c", [128, 4], F32))
        wsr1 = [E(nc.sbuf_tensor(f"wsr1_{i}", [128, 1], F32)) for i in range(3)]
        srq = [E(nc.sbuf_tensor(f"srq{i}", [128, 1], F32)) for i in range(2)]
        s2q = [E(nc.sbuf_tensor(f"s2q{i}", [128, 1], F32)) for i in range(2)]
        sgcol = E(nc.sbuf_tensor("sgcol", [128, 2 * NT], F32))
        ws_row = E(nc.sbuf_tensor("ws_row", [1, 8], F32))
        stg = E(nc.sbuf_tensor("stg", [128, 4 * 512], BF16))
        stg_f = E(nc.sbuf_tensor("stg_f", [128, 2 * 512], F32))
        ln_xbf = E(nc.sbuf_tensor("ln_xbf", [128, 2 * 512], BF16))
        ln_x2bf = E(nc.sbuf_tensor("ln_x2bf", [128, 2 * 512], BF16))
        ln_t1 = E(nc.sbuf_tensor("ln_t1", [128, 512], F32))
        ln_t4 = E(nc.sbuf_tensor("ln_t4", [128, 512], F32))

        ps = [E(nc.psum_tensor(f"ps{i}", [128, 512], F32)) for i in range(6)]
        psb = [E(nc.psum_tensor(f"psb{i}", [128, 1024], BF16)) for i in range(2)]

        def dcol(name, m):
            idx = {"sa": 0, "ca1": 1, "nca1": 2, "ga2": 3,
                   "sb": 4, "cb1": 5, "ncb1": 6, "gb2": 7}[name]
            return dcols[:, ND * idx + m:ND * idx + m + 1]

        d = kb.do

        def p0(E0):
            mod_row = E0(nc.sbuf_tensor("mod_row", [1, 6 * D], F32))
            extra = E0(nc.sbuf_tensor("extra_rows", [1, 2 * D], F32))
            c_sb = E0(nc.sbuf_tensor("c_sb", [1, D], F32))
            d("gpsimd", lambda e: e.iota(iota_i[:, :], [[1, 128]], base=0,
                                         channel_multiplier=-1), "G")
            kb.wait("vector", "G", kb.cnt["G"])
            d("vector", lambda e: e.tensor_scalar(U_bf[:, :], iota_i[:, :], 0.0, None, OP.is_ge), None)
            for r4 in range(4):
                d("vector", lambda e, r4=r4: e.tensor_copy(U4[:, 128 * r4:128 * (r4 + 1)], U_bf[:, :]), None)
            d("vector", lambda e: e.tensor_scalar(identB[:, :], iota_i[:, :], 0.0, None,
                                                  OP.is_equal), None)
            d("vector", lambda e: e.tensor_copy(identF[:, :], identB[:, :]), None)
            d("vector", lambda e: e.memset(cg[:, :], 0.0), None)
            for h in range(HN):
                d("vector", lambda e, h=h: e.memset(cg[:, 17 * h:17 * h + 1], 1.0 / 128.0), None)
            d("vector", lambda e: e.memset(ones1[:, :], 1.0), None)
            d("vector", lambda e: e.memset(onesrow[:, :], 1.0), None)
            d("vector", lambda e: e.memset(gn_col[:, :], 1.0 / 128.0), None)
            d("vector", lambda e: e.memset(sumcolF[:, :], 1.0 / 2048.0), "V")
            vc = kb.dma("sync", c_sb[:, :], c_row[:, :], "c")
            vg = kb.dma("sync", gnw_f32[:, :], gnw_h[:, :], "gnw")
            kb.dwait("vector", vg)
            d("vector", lambda e: e.tensor_copy(gnw_row[:, :], gnw_f32[:, :]), "V")
            d("vector", lambda e: e.memset(gnwsel[:, :], 0.0), "V")
            kb.wait("sync", "V", kb.cnt["V"])
            vgs = None
            for h in range(HN):
                vgs = kb.dma("sync", gnwsel[h:h + 1, 128 * h:128 * (h + 1)],
                             gnw_row[:, :], "gnws")
            kb.dwait("tensor", vgs)
            kb.dwait("scalar", vc)
            d("scalar", lambda e: e.activation(c_sb[:, :], c_sb[:, :], AF.Silu,
                                               bias=0.0, scale=1.0), "A")
            kb.wait("tensor", "A", kb.cnt["A"])
            kb.wait("tensor", "V", kb.cnt["V"])
            for m in range(ND):
                d("tensor", lambda e, m=m: e.matmul(
                    ps[0][:, m:m + 1], c_sb[:, 128 * m:128 * (m + 1)], ones1[:, :],
                    start=True, stop=True), "T" if m == ND - 1 else None)
            kb.wait("vector", "T", kb.cnt["T"])
            d("vector", lambda e: e.tensor_copy(sc_cols[:, :], ps[0][:, 0:ND]), "V")
            v_sc = kb.cnt["V"]
            kb.guard_set(("bank", 0), "V", v_sc)

            NKH = ND // 2
            wmall = E0(nc.sbuf_tensor("wmall", [128, 4 * NKH * 512], BF16))
            wmf32 = E0(nc.sbuf_tensor("wmf32", [128, 2 * NKH * 512], F32))
            kb.wait("tensor", "V", v_sc)
            piece_T = {}
            cast_done = {}
            f32cnt = 0
            for pc in range(48):
                j, half = divmod(pc, 2)
                slot = pc % 4
                bank = 1 + (j % 2)
                buf = wmall[:, slot * NKH * 512:(slot + 1) * NKH * 512]
                path = pc % 3
                src = hap(wmod, 512 * j + 128 * NKH * half * (6 * D),
                          [[6 * D, 128], [128 * 6 * D, NKH], [1, 512]])
                if path == 0:
                    if pc >= 4:
                        kb.wait("gpsimd", "T", piece_T[pc - 4])
                    vld = kb.dma("gpsimd",
                                 buf.rearrange("p (k c) -> p k c", c=512), src, f"wmg{slot}")
                    kb.dwait("tensor", vld)
                else:
                    fsl = f32cnt % 2
                    f32cnt += 1
                    fbuf = wmf32[:, fsl * NKH * 512:(fsl + 1) * NKH * 512]
                    qeng = "sync" if path == 1 else "scalar"
                    ceng = "scalar" if path == 1 else "vector"
                    prev = cast_done.get(fsl)
                    if prev is not None:
                        kb.wait(qeng, prev[0], prev[1])
                    vld = kb.dma(qeng, fbuf.rearrange("p (k c) -> p k c", c=512),
                                 src, f"wmf{fsl}")
                    if pc >= 4:
                        kb.wait(ceng, "T", piece_T[pc - 4])
                    kb.dwait(ceng, vld)
                    if ceng == "scalar":
                        vcast = d("scalar", lambda e, buf=buf, fbuf=fbuf: e.activation(
                            buf[:, :], fbuf[:, :], AF.Identity, bias=0.0, scale=1.0), "A")
                        cast_done[fsl] = ("A", vcast)
                    else:
                        vcast = d("vector", lambda e, buf=buf, fbuf=fbuf: e.tensor_copy(
                            buf[:, :], fbuf[:, :]), "V")
                        cast_done[fsl] = ("V", vcast)
                    kb.wait("tensor", cast_done[fsl][0], cast_done[fsl][1])
                if half == 0:
                    kb.guard_wait("tensor", ("bank", bank))
                for kh in range(NKH):
                    kc = half * NKH + kh
                    d("tensor", lambda e, kc=kc, kh=kh, bank=bank, buf=buf: e.matmul(
                        ps[bank][0:1, 0:512], sc_cols[:, kc:kc + 1],
                        buf[:, 512 * kh:512 * (kh + 1)],
                        start=(kc == 0), stop=(kc == ND - 1)),
                        "T" if kh == NKH - 1 else None)
                piece_T[pc] = kb.cnt["T"]
                if half == 1:
                    kb.wait("vector", "T", piece_T[pc])
                    vj = d("vector", lambda e, j=j, bank=bank: e.tensor_copy(
                        mod_row[:, 512 * j:512 * (j + 1)],
                        ps[bank][0:1, 0:512]), "V")
                    kb.guard_set(("bank", bank), "V", vj)
            v_mod = kb.cnt["V"]

            kb.wait("vector", "V", v_mod)
            d("vector", lambda e: e.tensor_scalar(mod_row[:, D:2 * D], mod_row[:, D:2 * D],
                                                  1.0, None, OP.add), None)
            d("vector", lambda e: e.tensor_scalar(mod_row[:, 4 * D:5 * D],
                                                  mod_row[:, 4 * D:5 * D],
                                                  1.0, None, OP.add), "V")
            for so in (2 * D, 5 * D):
                d("vector", lambda e, so=so: e.tensor_scalar(
                    extra[:, 0:D], mod_row[:, so:so + D], 0.0, None, OP.is_ge), None)
                d("vector", lambda e: e.tensor_scalar(
                    extra[:, 0:D], extra[:, 0:D], 2.0, -1.0, OP.mult, OP.add), "V")
                kb.wait("scalar", "V", kb.cnt["V"])
                d("scalar", lambda e, so=so: e.activation(
                    extra[:, D:2 * D], mod_row[:, so:so + D], AF.Abs,
                    bias=0.0, scale=1.0), "A")
                kb.wait("vector", "A", kb.cnt["A"])
                d("vector", lambda e: e.tensor_scalar(
                    extra[:, D:2 * D], extra[:, D:2 * D], 0.25, None, OP.max), None)
                d("vector", lambda e, so=so: e.tensor_tensor(
                    mod_row[:, so:so + D], extra[:, 0:D], extra[:, D:2 * D],
                    OP.mult), "V")
            d("vector", lambda e: e.tensor_scalar(extra[:, 0:D], mod_row[:, D:2 * D],
                                                  -1.0, None, OP.mult), None)
            d("vector", lambda e: e.tensor_scalar(extra[:, D:2 * D], mod_row[:, 4 * D:5 * D],
                                                  -1.0, None, OP.mult), "V")
            v_rows = kb.cnt["V"]
            kb.wait("tensor", "V", v_rows)
            srcs = [(mod_row, 0), (mod_row, D), (extra, 0), (mod_row, 2 * D),
                    (mod_row, 3 * D), (mod_row, 4 * D), (extra, D), (mod_row, 5 * D)]
            for i, (rt, so) in enumerate(srcs):
                bank = i % 2
                kb.guard_wait("tensor", ("bank", bank))
                for m in range(ND):
                    d("tensor", lambda e, rt=rt, so=so, m=m, bank=bank: e.matmul(
                        ps[bank][:, m:m + 1], rt[:, so + 128 * m:so + 128 * (m + 1)],
                        ones1[:, :], start=True, stop=True),
                        "T" if m == ND - 1 else None)
                kb.wait("vector", "T", kb.cnt["T"])
                vj = d("vector", lambda e, i=i, bank=bank: e.tensor_copy(
                    dcols[:, ND * i:ND * (i + 1)], ps[bank][:, 0:ND]), "V")
                kb.guard_set(("bank", bank), "V", vj)

            if "dmp_mod" in kb.dump:
                kb.wait("sync", "V", kb.cnt["V"])
                kb.dma("sync", dumps["dmp_mod"][:, :], mod_row[:, :], "dump")

        hbuf = [None]
        tf32c = None
        i32c = None

        def h_chunk(m):
            return hbuf[0][:, T * m:T * (m + 1)]

        def ln_pass(x_src_chunk, out_chunk, cname, csname, ncname):
            stat_rows = statbuf[0]
            """x_src_chunk(m) -> fp32 [128,512] AP (resident); writes
            out_chunk(m) bf16. Uses banks 0,1 (stats) and 2,3 (broadcast)."""
            xbf = ln_xbf
            x2bf = ln_x2bf
            t1 = ln_t1
            t4 = ln_t4
            mm2_done = {}
            for m in range(ND):
                pb = (m % 2) * 512
                if m >= 2:
                    kb.wait("vector", "T", mm2_done[m - 2])
                    kb.wait("scalar", "T", mm2_done[m - 2])
                d("vector", lambda e, m=m, pb=pb: e.tensor_copy(
                    xbf[:, pb:pb + 512], x_src_chunk(m)), "V")
                vv = kb.cnt["V"]
                kb.wait("scalar", "V", vv)
                d("scalar", lambda e, pb=pb: e.activation(
                    x2bf[:, pb:pb + 512], xbf[:, pb:pb + 512], AF.Square,
                    bias=0.0, scale=1.0), "A")
                va = kb.cnt["A"]
                kb.wait("tensor", "V", vv)
                if m == 0:
                    kb.guard_wait("tensor", ("bank", 0))
                    kb.guard_wait("tensor", ("bank", 1))
                d("tensor", lambda e, pb=pb, m=m: e.matmul(
                    ps[0][0:1, 0:512], sumcolF[:, :], xbf[:, pb:pb + 512],
                    start=(m == 0), stop=(m == ND - 1)), None)
                kb.wait("tensor", "A", va)
                d("tensor", lambda e, pb=pb, m=m: e.matmul(
                    ps[1][0:1, 0:512], sumcolF[:, :], x2bf[:, pb:pb + 512],
                    start=(m == 0), stop=(m == ND - 1)), "T")
                mm2_done[m] = kb.cnt["T"]
            v_stats = kb.cnt["T"]
            kb.wait("vector", "T", v_stats)
            R = stat_rows
            d("vector", lambda e: e.tensor_copy(R[:, 0:512], ps[0][0:1, 0:512]), None)
            d("vector", lambda e: e.tensor_tensor(R[:, 512:1024], R[:, 0:512],
                                                  R[:, 0:512], OP.mult), None)
            d("vector", lambda e: e.scalar_tensor_tensor(
                R[:, 1024:1536], ps[1][0:1, 0:512], EPS, R[:, 512:1024],
                OP.add, OP.subtract), None)
            d("vector", lambda e: e.reciprocal(R[:, 1536:2048], R[:, 1024:1536]), "V")
            v_free01 = kb.cnt["V"]
            kb.guard_set(("bank", 0), "V", v_free01)
            kb.guard_set(("bank", 1), "V", v_free01)
            kb.wait("scalar", "V", v_free01)
            d("scalar", lambda e: e.activation(R[:, 2048:2560], R[:, 1536:2048],
                                               AF.Sqrt, bias=0.0, scale=1.0), "A")
            kb.wait("vector", "A", kb.cnt["A"])
            d("vector", lambda e: e.tensor_tensor(R[:, 2560:3072], R[:, 0:512],
                                                  R[:, 2048:2560], OP.mult), "V")
            v_rows = kb.cnt["V"]
            kb.wait("tensor", "V", v_rows)
            kb.guard_wait("tensor", ("bank", 2))
            kb.guard_wait("tensor", ("bank", 3))
            d("tensor", lambda e: e.matmul(ps[2][:, 0:512], onesrow[:, 0:128],
                                           R[:, 2048:2560], start=True, stop=True), None)
            d("tensor", lambda e: e.matmul(ps[3][:, 0:512], onesrow[:, 0:128],
                                           R[:, 2560:3072], start=True, stop=True), "T")
            v_bc = kb.cnt["T"]
            kb.wait("vector", "T", v_bc)
            va_prev = None
            for m in range(ND):
                d("vector", lambda e, m=m: e.scalar_tensor_tensor(
                    t1[:, :], x_src_chunk(m), dcol(cname, m), ps[2][:, 0:512],
                    OP.mult, OP.mult), None)
                if va_prev is not None:
                    kb.wait("vector", "A", va_prev)
                d("vector", lambda e, m=m: e.scalar_tensor_tensor(
                    t4[:, :], ps[3][:, 0:512], dcol(ncname, m), t1[:, :],
                    OP.mult, OP.add), "V")
                kb.wait("scalar", "V", kb.cnt["V"])
                va_prev = d("scalar", lambda e, m=m: e.activation(
                    out_chunk(m), t4[:, :], AF.Identity,
                    bias=dcol(csname, m), scale=1.0), "A")
            v_done = kb.cnt["V"]
            kb.wait("vector", "A", va_prev)
            d("vector", lambda e: e.nop(), "V")
            v_done = kb.cnt["V"]
            kb.guard_set(("bank", 2), "V", v_done)
            kb.guard_set(("bank", 3), "V", v_done)
            return v_done

        def p1(E1):
            xbuf = E1(nc.sbuf_tensor("xbuf", [128, 2 * ND * 512], F32))
            statbuf[0] = E1(nc.sbuf_tensor("stat_rows1", [1, 6 * 512], F32))

            for qq in range(NQ):
                bq = (qq % 2) * ND * 512
                kb.guard_wait("sync", ("xbuf", qq % 2))
                vx = kb.dma("sync",
                            xbuf[:, bq:bq + ND * 512].rearrange("p (m t) -> p m t", t=512),
                            hap(x_fm, 512 * qq, [[T, 128], [128 * T, ND], [1, 512]]),
                            f"x{qq % 2}")
                kb.dwait("vector", vx)
                kb.dwait("scalar", vx)

                def src(m, bq=bq):
                    return xbuf[:, bq + 512 * m:bq + 512 * (m + 1)]

                def out(m, qq=qq):
                    return h_chunk(m)[:, 512 * qq:512 * (qq + 1)]

                v_done = ln_pass(src, out, "ca1", "sa", "nca1")
                kb.guard_set(("xbuf", qq % 2), "V", v_done)
                kb.tick5w(4)

            if "dmp_h" in kb.dump:
                kb.wait("sync", "V", kb.cnt["V"])
                for m in range(ND):
                    kb.dma("sync", dumps["dmp_h"][128 * m:128 * (m + 1), :], h_chunk(m), "dump")

        def p2(E2):
            wbuf = E2(nc.sbuf_tensor("wbuf", [128, 2 * ND * 512], BF16))
            hw1 = E2(nc.sbuf_tensor("hw1", [17, T], BF16))
            wgk2a = E2(nc.sbuf_tensor("wgk2a", [17, D], BF16))
            w1b = E2(nc.sbuf_tensor("w1b", [128, ND * 16], BF16))

            def stream_proj(w_hbm, kind, sp_dst, scale_q=1.0, silu_evac=False):
                for quarter in range(4):
                    bq = (quarter % 2) * ND * 512
                    kb.guard_wait("gpsimd", ("wbuf", quarter % 2))
                    vw = kb.dma("gpsimd",
                                wbuf[:, bq:bq + ND * 512].rearrange("p (k c) -> p k c", c=512),
                                hap(w_hbm, 512 * quarter, [[D, 128], [128 * D, ND], [1, 512]]),
                                f"w{quarter % 2}")
                    kb.dwait("tensor", vw)
                    outer = range(4) if kind == "fm" else range(NT)
                    for oidx in outer:
                        inner = range(NQ) if kind == "fm" else (0,)
                        for qq in inner:
                            it = (oidx * NQ + qq) if kind == "fm" else oidx
                            bank = it % 2
                            sslot = it % 4
                            kb.guard_wait("tensor", ("bank", bank))
                            for k in range(ND):
                                if kind == "fm":
                                    lhsT = wbuf[:, bq + 512 * k + 128 * oidx:
                                                bq + 512 * k + 128 * (oidx + 1)]
                                    rhs = h_chunk(k)[:, 512 * qq:512 * (qq + 1)]
                                else:
                                    lhsT = h_chunk(k)[:, 128 * oidx:128 * (oidx + 1)]
                                    rhs = wbuf[:, bq + 512 * k:bq + 512 * (k + 1)]
                                d("tensor", lambda e, lhsT=lhsT, rhs=rhs, k=k, bank=bank: e.matmul(
                                    ps[bank][:, 0:512], lhsT, rhs,
                                    start=(k == 0), stop=(k == ND - 1)),
                                    "T" if k == ND - 1 else None)
                            vT = kb.cnt["T"]
                            dst = stg[:, 512 * sslot:512 * (sslot + 1)]
                            if silu_evac:
                                kb.wait("scalar", "T", vT)
                                kb.guard_wait("scalar", ("stg", sslot))
                                d("scalar", lambda e, bank=bank, dst=dst: e.activation(
                                    dst, ps[bank][:, 0:512], AF.Silu,
                                    bias=0.0, scale=1.0), "A")
                                sem_e, vE = "A", kb.cnt["A"]
                            else:
                                kb.wait("vector", "T", vT)
                                kb.guard_wait("vector", ("stg", sslot))
                                d("vector", lambda e, bank=bank, dst=dst, sc=scale_q: e.tensor_scalar(
                                    dst, ps[bank][:, 0:512], sc, None, OP.mult), "V")
                                sem_e, vE = "V", kb.cnt["V"]
                            kb.guard_set(("bank", bank), sem_e, vE)
                            kb.wait("sync", sem_e, vE)
                            if kind == "fm":
                                m_glob = quarter * 4 + oidx
                                vout = kb.dma("sync",
                                              hap(sp_dst, (4 * qq) * (ND * 128 * 128) + m_glob * (128 * 128),
                                                  [[128, 128], [ND * 128 * 128, 4], [1, 128]]),
                                              dst.rearrange("p (c t) -> p c t", t=128),
                                              f"stg{sslot}")
                            else:
                                vout = kb.dma("sync",
                                              hap(sp_dst, (128 * oidx) * D + 512 * quarter,
                                                  [[D, 128], [1, 512]]),
                                              dst, f"stg{sslot}")
                            kb.guard_set(("stg", sslot), *vout)
                        if (it % 4) == 3:
                            kb.tick5w(1)
                        if getattr(kb, "tick2", None) is not None:
                            kb.tick2(2)
                    kb.guard_set(("wbuf", quarter % 2), "T", kb.cnt["T"])

            stream_proj(wq_h, "fm", sp_q, scale_q=SCALE)
            stream_proj(wk_h, "fm", sp_k)
            stream_proj(wg_h, "fm", sp_g, silu_evac=True)

            vw1 = kb.dma("gpsimd", w1b[:, :].rearrange("p (k c) -> p k c", c=16),
                         hap(wgk1, 0, [[16, 128], [128 * 16, ND], [1, 16]]), "gk1")
            vk2 = kb.dma("gpsimd", wgk2a[0:16, :], wgk2[:, :], "gk2")
            vb2 = kb.dma("gpsimd", wgk2a[16:17, :], bgk2[:, :], "gk3")
            for _sl in range(4):
                kb.guard_wait("vector", ("stg", _sl))
            d("vector", lambda e: e.memset(stg[0:1, 0:T], 1.0), "V")
            kb.wait("sync", "V", kb.cnt["V"])
            v_hw1ones = kb.dma("sync", hw1[16:17, :], stg[0:1, 0:T], "hw1o")
            for _sl in range(4):
                kb.guard_set(("stg", _sl), *v_hw1ones)
            kb.dwait("tensor", v_hw1ones)
            kb.dwait("tensor", vw1)
            for qq in range(NQ):
                bank = qq % 2
                kb.guard_wait("tensor", ("bank", bank))
                for k in range(ND):
                    d("tensor", lambda e, k=k, qq=qq, bank=bank: e.matmul(
                        ps[bank][0:16, 0:512], w1b[:, 16 * k:16 * (k + 1)],
                        h_chunk(k)[:, 512 * qq:512 * (qq + 1)],
                        start=(k == 0), stop=(k == ND - 1)),
                        "T" if k == ND - 1 else None)
                kb.wait("vector", "T", kb.cnt["T"])
                vj = d("vector", lambda e, qq=qq, bank=bank: e.tensor_copy(
                    hw1[0:16, 512 * qq:512 * (qq + 1)], ps[bank][0:16, 0:512]), "V")
                kb.guard_set(("bank", bank), "V", vj)
            v_hw1done = kb.cnt["V"]

            def gk2_gen():
                yield
                for it in range(64):
                    tt, j = divmod(it, 4)
                    bank = 2 + (it % 2)
                    fs = it % 2
                    if it == 0:
                        kb.dwait("tensor", vb2)
                        kb.dwait("tensor", vk2)
                        kb.wait("tensor", "V", v_hw1done)
                    kb.guard_wait("tensor", ("bank", bank))
                    d("tensor", lambda e, tt=tt, j=j, bank=bank: e.matmul(
                        ps[bank][:, 0:512], hw1[:, 128 * tt:128 * (tt + 1)],
                        wgk2a[:, 512 * j:512 * (j + 1)], start=True, stop=True), "T")
                    vT = kb.cnt["T"]
                    kb.wait("scalar", "T", vT)
                    kb.guard_wait("scalar", ("stgf", fs))
                    d("scalar", lambda e, bank=bank, fs=fs: e.activation(
                        stg_f[:, 512 * fs:512 * (fs + 1)], ps[bank][:, 0:512],
                        AF.Exp, bias=0.0, scale=-1.0), "A")
                    va1 = kb.cnt["A"]
                    kb.guard_set(("bank", bank), "A", va1)
                    kb.guard_wait("scalar", ("lnx", fs))
                    d("scalar", lambda e, fs=fs: e.activation(
                        ln_xbf[:, 512 * fs:512 * (fs + 1)],
                        stg_f[:, 512 * fs:512 * (fs + 1)],
                        AF.Ln, bias=1.0, scale=1.0), "A")
                    vA = kb.cnt["A"]
                    kb.guard_set(("stgf", fs), "A", vA)
                    kb.wait("sync", "A", vA)
                    vout = kb.dma("sync",
                                  hap(sp_sp, (128 * tt) * D + 512 * j, [[D, 128], [1, 512]]),
                                  ln_xbf[:, 512 * fs:512 * (fs + 1)], f"sp2{fs}")
                    kb.guard_set(("lnx", fs), *vout)
                    yield

            g2k = gk2_gen()
            next(g2k)

            def tick2(n=1):
                for _ in range(n):
                    if next(g2k, "END") == "END":
                        return
            kb.tick2 = tick2
            stream_proj(wv_h, "tm", sp_v)
            stream_proj(wk_h, "tm", sp_ktm)
            kb.tick2 = None
            for _ in range(64):
                if next(g2k, "END") == "END":
                    break

        def p3(E3):
            qc = E3(nc.sbuf_tensor("qc", [128, 2 * D], BF16))
            kcb = E3(nc.sbuf_tensor("kcb", [128, 2 * D], BF16))
            vcb = E3(nc.sbuf_tensor("vcb", [128, 2 * D], BF16))
            ktc = E3(nc.sbuf_tensor("ktc", [128, 2 * D], BF16))
            spc = E3(nc.sbuf_tensor("spc", [128, 2 * D], BF16))
            en_tm = E3(nc.sbuf_tensor("en_tm", [128, 2 * D], BF16))
            spcum = E3(nc.sbuf_tensor("spcum", [128, 2 * D], BF16))
            epos_b = E3(nc.sbuf_tensor("epos_b", [128, 2 * D], BF16))
            eneg_b = E3(nc.sbuf_tensor("eneg_b", [128, 2 * D], BF16))
            ecols_b = E3(nc.sbuf_tensor("ecols_b", [128, 2 * HN], F32))
            qg_b = E3(nc.sbuf_tensor("qg_b", [128, 2 * D], BF16))
            s_tmp = E3(nc.sbuf_tensor("s_tmp", [128, 512], F32))
            kh_b = E3(nc.sbuf_tensor("kh_b", [128, 2 * D], BF16))
            kt_b = E3(nc.sbuf_tensor("kt_b", [128, 2 * D], BF16))
            a_bf = E3(nc.sbuf_tensor("a_bf", [128, 2 * 512], BF16))
            s_f32 = E3(nc.sbuf_tensor("s_f32", [128, HN * 128], F32))
            s_bf = E3(nc.sbuf_tensor("s_bf", [128, HN * 128], BF16))
            o_seg = E3(nc.sbuf_tensor("o_seg", [128, HN * 512], F32))
            g_seg = E3(nc.sbuf_tensor("g_seg", [128, HN * 512], BF16))
            o2_bf = E3(nc.sbuf_tensor("o2_bf", [128, 2 * 512], BF16))
            fe_rows = E3(nc.sbuf_tensor("fe_rows", [16, 2 * 512], F32))
            f_bf = E3(nc.sbuf_tensor("f_bf", [16, 512], BF16))
            t_tmp = E3(nc.sbuf_tensor("t_tmp", [128, 2 * 512], F32))
            op_stg = E3(nc.sbuf_tensor("op_stg", [128, 2 * 512], BF16))

            CH_STRIDE = ND * 128 * 128
            LD = {}
            FR = {}

            def loads(cc):
                sl = cc % 2
                kb.guard_wait("sync", ("qbuf", sl))
                v_q = kb.dma("sync", qc[:, sl * D:(sl + 1) * D].rearrange("p (m t) -> p m t", t=128),
                             hap(sp_q, cc * CH_STRIDE, [[128, 128], [128 * 128, ND], [1, 128]]),
                             f"q{sl}")
                kb.guard_wait("sync", ("kbuf", sl))
                v_k = kb.dma("sync", kcb[:, sl * D:(sl + 1) * D].rearrange("p (m t) -> p m t", t=128),
                             hap(sp_k, cc * CH_STRIDE, [[128, 128], [128 * 128, ND], [1, 128]]),
                             f"kk{sl}")
                kb.guard_wait("sync", ("vbuf", sl))
                v_v = kb.dma("sync", vcb[:, sl * D:(sl + 1) * D],
                             hap(sp_v, (128 * cc) * D, [[D, 128], [1, D]]), f"v{sl}")
                kb.guard_wait("sync", ("ktbuf", sl))
                v_kt = kb.dma("sync", ktc[:, sl * D:(sl + 1) * D],
                              hap(sp_ktm, (128 * cc) * D, [[D, 128], [1, D]]), f"kt{sl}")
                kb.guard_wait("sync", ("spbuf", sl))
                v_sp = kb.dma("sync", spc[:, sl * D:(sl + 1) * D],
                              hap(sp_sp, (128 * cc) * D, [[D, 128], [1, D]]), f"sp{sl}")
                LD[cc] = dict(q=v_q, k=v_k, v=v_v, kt=v_kt, sp=v_sp)

            def load_gseg(seg):
                kb.guard_wait("sync", ("gseg",))
                for hh2 in range(HN):
                    v_g = kb.dma("sync",
                                 g_seg[:, 512 * hh2:512 * (hh2 + 1)]
                                 .rearrange("p (c t) -> p c t", t=128),
                                 hap(sp_g, (4 * seg) * CH_STRIDE + hh2 * (128 * 128),
                                     [[128, 128], [CH_STRIDE, 4], [1, 128]]),
                                 "gseg")
                kb.guards[("gseg_ld", seg)] = [v_g]

            def make_fillers(cc):
                sl = cc % 2
                FR[cc] = {}

                def u_piece(j4):
                    if j4 == 0:
                        kb.dwait("tensor", LD[cc]["sp"])
                    kb.guard_wait("tensor", ("bank", 5))
                    vT = d("tensor", lambda e, sl=sl, j4=j4: e.matmul(
                        ps[5][:, 0:512], U_bf[:, :],
                        spc[:, sl * D + 512 * j4:sl * D + 512 * (j4 + 1)],
                        start=True, stop=True), "T")
                    kb.wait("scalar", "T", vT)
                    if j4 == 0:
                        kb.guard_wait("scalar", ("entm", sl))
                        kb.guard_wait("scalar", ("spcum", sl))
                    d("scalar", lambda e, sl=sl, j4=j4: e.activation(
                        en_tm[:, sl * D + 512 * j4:sl * D + 512 * (j4 + 1)],
                        ps[5][:, 0:512], AF.Exp, bias=0.0, scale=1.0 / GLN), "A")
                    va = d("scalar", lambda e, sl=sl, j4=j4: e.activation(
                        spcum[:, sl * D + 512 * j4:sl * D + 512 * (j4 + 1)],
                        ps[5][:, 0:512], AF.Identity, bias=0.0, scale=1.0), "A")
                    kb.guard_set(("bank", 5), "A", va)
                    if j4 == 3:
                        kb.guard_set(("spbuf", sl), "T", kb.cnt["T"])
                        FR[cc]["spcum"] = va
                        FR[cc]["entm"] = va

                def tr_piece(g):
                    pb = psb[g]
                    kb.wait("tensor", "A", FR[cc]["spcum"])
                    kb.guard_wait("tensor", ("psb", g))
                    for h8 in range(8):
                        hh = g * 8 + h8
                        d("tensor", lambda e, sl=sl, hh=hh, h8=h8, pb=pb: e.transpose(
                            pb[:, 128 * h8:128 * (h8 + 1)],
                            spcum[:, sl * D + 128 * hh:sl * D + 128 * (hh + 1)],
                            identB[:, :]),
                            "T" if h8 == 7 else None)
                    vtr = kb.cnt["T"]
                    kb.wait("scalar", "T", vtr)
                    kb.guard_wait("scalar", ("eall", g, sl))
                    d("scalar", lambda e, g=g, pb=pb, sl=sl: e.activation(
                        epos_b[:, sl * D + 1024 * g:sl * D + 1024 * (g + 1)], pb[:, 0:1024],
                        AF.Exp, bias=0.0, scale=-1.0 / GLN), "A")
                    vexp = d("scalar", lambda e, g=g, pb=pb, sl=sl: e.activation(
                        eneg_b[:, sl * D + 1024 * g:sl * D + 1024 * (g + 1)], pb[:, 0:1024],
                        AF.Exp, bias=0.0, scale=1.0 / GLN), "A")
                    kb.guard_set(("psb", g), "A", vexp)
                    if g == 1:
                        kb.guard_set(("spcum", sl), "T", vtr)
                        FR[cc]["exps"] = vexp

                def ec_qg():
                    kb.wait("vector", "A", FR[cc]["exps"])
                    kb.guard_wait("vector", ("ecols", sl))
                    d("vector", lambda e, sl=sl: e.tensor_copy(
                        ecols_b[:, sl * HN:(sl + 1) * HN],
                        hap(epos_b, sl * D + 127, [[2 * D, 128], [128, HN]])), "V")
                    kb.dwait("vector", LD[cc]["q"])
                    kb.guard_wait("vector", ("qgall", sl))
                    d("vector", lambda e, sl=sl: e.tensor_tensor(
                        qg_b[:, sl * D:(sl + 1) * D], qc[:, sl * D:(sl + 1) * D],
                        epos_b[:, sl * D:(sl + 1) * D], OP.mult), "V")
                    kb.guard_set(("qbuf", sl), "V", kb.cnt["V"])

                def kh_f():
                    kb.dwait("vector", LD[cc]["k"])
                    kb.guard_wait("vector", ("khall", sl))
                    d("vector", lambda e, sl=sl: e.tensor_tensor(
                        kh_b[:, sl * D:(sl + 1) * D], kcb[:, sl * D:(sl + 1) * D],
                        eneg_b[:, sl * D:(sl + 1) * D], OP.mult), "V")
                    FR[cc]["gates"] = kb.cnt["V"]
                    kb.guard_set(("kbuf", sl), "V", FR[cc]["gates"])
                    kb.guard_set(("eall", 0, sl), "V", FR[cc]["gates"])
                    kb.guard_set(("eall", 1, sl), "V", FR[cc]["gates"])

                def kt_f():
                    kb.dwait("vector", LD[cc]["kt"])
                    kb.wait("vector", "A", FR[cc]["entm"])
                    kb.guard_wait("vector", ("ktall", sl))
                    v_ktall = d("vector", lambda e, sl=sl: e.tensor_tensor(
                        kt_b[:, sl * D:(sl + 1) * D], ktc[:, sl * D:(sl + 1) * D],
                        en_tm[:, sl * D:(sl + 1) * D], OP.mult), "V")
                    FR[cc]["ktall"] = v_ktall
                    kb.guard_set(("entm", sl), "V", v_ktall)
                    kb.guard_set(("ktbuf", sl), "V", v_ktall)

                return [lambda j4=j4: u_piece(j4) for j4 in range(4)] + \
                       [lambda: tr_piece(0), lambda: tr_piece(1),
                        ec_qg, kh_f, kt_f]

            def pass2(cc, fillers):
                sl = cc % 2
                fillers = list(fillers)

                def fill(n=1):
                    for _ in range(n):
                        if fillers:
                            fillers.pop(0)()

                def s_evac(hg):
                    vsmm = kb.cnt["T"]
                    kb.wait("vector", "T", vsmm)
                    for h4 in range(4):
                        kb.guard_wait("vector", ("sf32", hg * 4 + h4))
                    if cc > 0:
                        d("vector", lambda e, hg=hg: e.tensor_tensor(
                            s_tmp[:, :], ps[4][:, 0:512],
                            s_f32[:, 512 * hg:512 * (hg + 1)], OP.add), "V")
                        for h4 in range(4):
                            hh = hg * 4 + h4
                            d("vector", lambda e, hh=hh, h4=h4, sl=sl: e.tensor_scalar(
                                s_f32[:, 128 * hh:128 * (hh + 1)],
                                s_tmp[:, 128 * h4:128 * (h4 + 1)],
                                ecols_b[:, sl * HN + hh:sl * HN + hh + 1], None, OP.mult), "V")
                    else:
                        for h4 in range(4):
                            hh = hg * 4 + h4
                            d("vector", lambda e, hh=hh, h4=h4, sl=sl: e.tensor_scalar(
                                s_f32[:, 128 * hh:128 * (hh + 1)],
                                ps[4][:, 128 * h4:128 * (h4 + 1)],
                                ecols_b[:, sl * HN + hh:sl * HN + hh + 1], None, OP.mult), "V")
                    vsev = kb.cnt["V"]
                    kb.guard_set(("bank", 4), "V", vsev)
                    kb.wait("gpsimd", "V", vsev)
                    vcast = d("gpsimd", lambda e, hg=hg: e.tensor_copy(
                        s_bf[:, 512 * hg:512 * (hg + 1)],
                        s_f32[:, 512 * hg:512 * (hg + 1)]), "G")
                    for h4 in range(4):
                        kb.guard_set(("S", hg * 4 + h4), "G", vcast)
                        kb.guards[("sf32", hg * 4 + h4)] = [("G", vcast)]

                kb.wait("tensor", "V", FR[cc]["gates"])
                kb.wait("tensor", "V", FR[cc]["ktall"])
                kb.dwait("tensor", LD[cc]["v"])

                def a_mms(hg):
                    abank = hg % 2
                    kb.guard_wait("tensor", ("bank", abank))
                    for h4 in range(4):
                        hof = 128 * (hg * 4 + h4)
                        d("tensor", lambda e, hof=hof, abank=abank, h4=h4, sl=sl: e.matmul(
                            ps[abank][:, 128 * h4:128 * (h4 + 1)],
                            kh_b[:, sl * D + hof:sl * D + hof + 128],
                            qg_b[:, sl * D + hof:sl * D + hof + 128],
                            start=True, stop=True),
                            "T" if h4 == 3 else None)
                    return kb.cnt["T"]

                vamm_d = {0: a_mms(0), 1: a_mms(1)}
                prev_sev = None
                for hg in range(4):
                    abank = hg % 2
                    ob = 2 + hg % 2
                    vamm = vamm_d[hg]
                    kb.wait("vector", "T", vamm)
                    kb.guard_wait("vector", ("abf", abank))
                    vabf = d("vector", lambda e, hg=hg, abank=abank: e.tensor_tensor(
                        a_bf[:, 512 * (hg % 2):512 * (hg % 2 + 1)],
                        ps[abank][:, 0:512], U4[:, :], OP.mult), "V")
                    kb.guard_set(("bank", abank), "V", vabf)
                    if prev_sev is not None:
                        prev_sev()
                        prev_sev = None
                    fill()

                    kb.wait("tensor", "V", vabf)
                    kb.guard_wait("tensor", ("bank", ob))
                    for h4 in range(4):
                        hh = hg * 4 + h4
                        hof = 128 * hh
                        if cc > 0:
                            kb.guard_wait("tensor", ("S", hh))
                            d("tensor", lambda e, ob=ob, hof=hof, h4=h4, sl=sl: e.matmul(
                                ps[ob][:, 128 * h4:128 * (h4 + 1)], s_bf[:, hof:hof + 128],
                                qg_b[:, sl * D + hof:sl * D + hof + 128],
                                start=True, stop=False), None)
                        d("tensor", lambda e, ob=ob, sl=sl, hof=hof, hg=hg, h4=h4, cc=cc: e.matmul(
                            ps[ob][:, 128 * h4:128 * (h4 + 1)],
                            vcb[:, sl * D + hof:sl * D + hof + 128],
                            a_bf[:, 512 * (hg % 2) + 128 * h4:512 * (hg % 2) + 128 * (h4 + 1)],
                            start=(cc == 0), stop=True),
                            "T" if h4 == 3 else None)
                    vomm = kb.cnt["T"]
                    kb.guard_set(("abf", abank), "T", vomm)
                    kb.wait("scalar", "T", vomm)
                    for h4 in range(4):
                        kb.guard_wait("scalar", ("oseg", hg * 4 + h4, cc % 4))
                    voev = d("scalar", lambda e, ob=ob, hg=hg, cc=cc: e.activation(
                        hap(o_seg, 512 * (4 * hg) + 128 * (cc % 4),
                            [[HN * 512, 128], [512, 4], [1, 128]]),
                        ps[ob][:, 0:512], AF.Identity, bias=0.0, scale=1.0), "A")
                    kb.guard_set(("bank", ob), "A", voev)
                    for h4 in range(4):
                        kb.guards[("olast", hg * 4 + h4)] = [("A", voev)]
                    fill()

                    kb.guard_wait("tensor", ("bank", 4))
                    for h4 in range(4):
                        hof = 128 * (hg * 4 + h4)
                        d("tensor", lambda e, sl=sl, hof=hof, h4=h4: e.matmul(
                            ps[4][:, 128 * h4:128 * (h4 + 1)],
                            kt_b[:, sl * D + hof:sl * D + hof + 128],
                            vcb[:, sl * D + hof:sl * D + hof + 128],
                            start=True, stop=True),
                            "T" if h4 == 3 else None)
                    if hg + 2 < 4:
                        vamm_d[hg + 2] = a_mms(hg + 2)
                    prev_sev = (lambda hg=hg: s_evac(hg))
                prev_sev()
                fill(len(fillers))
                vT_end = kb.cnt["T"]
                kb.guard_set(("vbuf", sl), "T", vT_end)
                kb.guard_set(("ktall", sl), "T", vT_end)
                kb.guard_set(("qgall", sl), "T", vT_end)
                kb.guard_set(("khall", sl), "T", vT_end)
                kb.guard_set(("ecols", sl), "V", kb.cnt["V"])

            def tail(cc):
                sl = cc % 2
                seg = cc // 4
                kb.dwait("scalar", kb.guards[("gseg_ld", seg)][0])
                kb.dwait("vector", kb.guards[("gseg_ld", seg)][0])
                kb.guard_wait("tensor", ("bank", 4))
                for hh in range(HN):
                    hp = hh % 2
                    kb.wait("scalar", "A", kb.guards[("olast", hh)][0][1])
                    kb.guard_wait("scalar", ("o2", hp))
                    vsq = d("scalar", lambda e, hh=hh, hp=hp: e.activation(
                        o2_bf[:, 512 * hp:512 * (hp + 1)],
                        o_seg[:, 512 * hh:512 * (hh + 1)],
                        AF.Square, bias=0.0, scale=1.0), "A")
                    kb.wait("tensor", "A", vsq)
                    vmq = d("tensor", lambda e, hh=hh, hp=hp: e.matmul(
                        ps[4][0:16, 0:512], cg[:, 16 * hh:16 * (hh + 1)],
                        o2_bf[:, 512 * hp:512 * (hp + 1)],
                        start=(hh == 0), stop=(hh == HN - 1)), "T")
                    kb.guard_set(("o2", hp), "T", vmq)
                vmq = kb.cnt["T"]
                kb.wait("vector", "T", vmq)
                kb.guard_wait("vector", ("frows",))
                d("vector", lambda e: e.tensor_scalar(
                    fe_rows[:, 0:512], ps[4][0:16, 0:512], EPS, None, OP.add), "V")
                vrc = d("vector", lambda e: e.reciprocal(
                    fe_rows[:, 512:1024], fe_rows[:, 0:512]), "V")
                kb.guard_set(("bank", 4), "V", vrc)
                kb.wait("scalar", "V", vrc)
                kb.guard_wait("scalar", ("fbf",))
                vfr = d("scalar", lambda e: e.activation(
                    f_bf[:, :], fe_rows[:, 512:1024], AF.Sqrt,
                    bias=0.0, scale=1.0), "A")
                kb.guards[("frows",)] = [("A", vfr)]
                kb.wait("tensor", "A", vfr)
                for hh in range(HN):
                    hp = hh % 2
                    b2b = hp
                    kb.guard_wait("tensor", ("bank", b2b))
                    vb2m = d("tensor", lambda e, hh=hh, b2b=b2b: e.matmul(
                        ps[b2b][:, 0:512], gnwsel[:, 128 * hh:128 * (hh + 1)],
                        f_bf[:, :], start=True, stop=True), "T")
                    kb.guards[("fbf",)] = [("T", vb2m)]
                    kb.wait("vector", "T", vb2m)
                    kb.guard_wait("vector", ("ttmp", hp))
                    d("vector", lambda e, hh=hh, hp=hp, b2b=b2b: e.tensor_tensor(
                        t_tmp[:, 512 * hp:512 * (hp + 1)],
                        o_seg[:, 512 * hh:512 * (hh + 1)],
                        ps[b2b][:, 0:512], OP.mult), "V")
                    kb.guard_wait("vector", ("opstg", hp))
                    vop = d("vector", lambda e, hh=hh, hp=hp: e.tensor_tensor(
                        op_stg[:, 512 * hp:512 * (hp + 1)],
                        t_tmp[:, 512 * hp:512 * (hp + 1)],
                        g_seg[:, 512 * hh:512 * (hh + 1)], OP.mult), "V")
                    kb.guard_set(("bank", b2b), "V", vop)
                    kb.guard_set(("ttmp", hp), "V", vop)
                    for j in range(4):
                        kb.guard_set(("oseg", hh, j), "V", vop)
                    kb.wait("sync", "V", vop)
                    vsp = kb.dma("sync",
                                 hap(sp_op, (128 * hh) * T + 512 * seg,
                                     [[T, 128], [1, 512]]),
                                 op_stg[:, 512 * hp:512 * (hp + 1)], f"op{hp}")
                    kb.guard_set(("opstg", hp), *vsp)
                    if "dmp_o" in kb.dump:
                        vdo = kb.dma("sync",
                                     hap(dumps["dmp_o"], (128 * hh) * T + 512 * seg,
                                         [[T, 128], [1, 512]]),
                                     o_seg[:, 512 * hh:512 * (hh + 1)], f"do{hh}")
                        for j in range(4):
                            kb.guard_add(("oseg", hh, j), *vdo)
                kb.guard_set(("gseg",), "V", kb.cnt["V"])

            load_gseg(0)
            loads(0)
            for f in make_fillers(0):
                f()
            for cc in range(p3_chunks):
                nxt = cc + 1
                fillers = []
                if nxt < p3_chunks:
                    loads(nxt)
                    fillers = make_fillers(nxt)
                pass2(cc, fillers)
                if cc % 4 == 3:
                    tail(cc)
                if nxt < p3_chunks and nxt % 4 == 0:
                    load_gseg(nxt // 4)

        def p4(E4):
            wobuf = E4(nc.sbuf_tensor("wobuf", [128, ND * D], BF16))
            statbuf[0] = E4(nc.sbuf_tensor("stat_rows4", [1, 6 * 512], F32))
            opb = E4(nc.sbuf_tensor("opb", [128, 2 * ND * 512], BF16))
            xbuf4 = E4(nc.sbuf_tensor("xbuf4", [128, ND * 512], F32))
            x1q = E4(nc.sbuf_tensor("x1q", [128, ND * 512], F32))

            vwo = None
            for quarter in range(4):
                vwo = kb.dma("gpsimd",
                             wobuf[:, :].rearrange("p (k c) -> p k c", c=D)
                             [:, :, 512 * quarter:512 * (quarter + 1)],
                             hap(wo_h, 512 * quarter, [[D, 128], [128 * D, ND], [1, 512]]),
                             f"wo{quarter}")
                kb.dwait("tensor", vwo)

            for qq in range(NQ):
                osl = qq % 2
                kb.guard_wait("sync", ("opb", osl))
                vop = kb.dma("sync",
                             opb[:, osl * ND * 512:(osl + 1) * ND * 512]
                             .rearrange("p (k t) -> p k t", t=512),
                             hap(sp_op, 512 * qq, [[T, 128], [128 * T, ND], [1, 512]]),
                             f"ob{osl}")
                kb.guard_wait("sync", ("xbuf4",))
                vx = kb.dma("sync",
                            xbuf4[:, :].rearrange("p (m t) -> p m t", t=512),
                            hap(x_fm, 512 * qq, [[T, 128], [128 * T, ND], [1, 512]]),
                            "x4")
                kb.dwait("tensor", vop)
                kb.dwait("vector", vx)
                for m in range(ND):
                    bank = m % 2
                    kb.guard_wait("tensor", ("bank", bank))
                    for k in range(ND):
                        d("tensor", lambda e, m=m, k=k, bank=bank, osl=osl: e.matmul(
                            ps[bank][:, 0:512],
                            wobuf[:, k * D + 128 * m:k * D + 128 * (m + 1)],
                            opb[:, osl * ND * 512 + 512 * k:osl * ND * 512 + 512 * (k + 1)],
                            start=(k == 0), stop=(k == ND - 1)),
                            "T" if k == ND - 1 else None)
                    vT = kb.cnt["T"]
                    kb.wait("vector", "T", vT)
                    kb.guard_wait("vector", ("x1q", m % 4))
                    vx1 = d("vector", lambda e, m=m, bank=bank: e.scalar_tensor_tensor(
                        x1q[:, 512 * m:512 * (m + 1)], ps[bank][:, 0:512],
                        dcol("ga2", m), xbuf4[:, 512 * m:512 * (m + 1)],
                        OP.mult, OP.add), "V")
                    kb.guard_set(("bank", bank), "V", vx1)
                    kb.wait("sync", "V", vx1)
                    vx1s = kb.dma("sync",
                                  hap(sp_x1, (128 * m) * T + 512 * qq, [[T, 128], [1, 512]]),
                                  x1q[:, 512 * m:512 * (m + 1)], f"x1s{m % 4}")
                kb.guard_set(("opb", osl), "T", kb.cnt["T"])
                kb.guard_set(("xbuf4",), "V", kb.cnt["V"])
                for j in range(4):
                    kb.guard_set(("x1q", j), "D_x1s" + str(j), kb.cnt.get("D_x1s" + str(j), 0))

                def src(m):
                    return x1q[:, 512 * m:512 * (m + 1)]

                mm2_done = {}
                xbf, x2bf, t1, t4 = ln_xbf, ln_x2bf, ln_t1, ln_t4
                for m in range(ND):
                    pb = (m % 2) * 512
                    if m >= 2:
                        kb.wait("vector", "T", mm2_done[m - 2])
                        kb.wait("scalar", "T", mm2_done[m - 2])
                    d("vector", lambda e, m=m, pb=pb: e.tensor_copy(
                        xbf[:, pb:pb + 512], src(m)), "V")
                    vv = kb.cnt["V"]
                    kb.wait("scalar", "V", vv)
                    d("scalar", lambda e, pb=pb: e.activation(
                        x2bf[:, pb:pb + 512], xbf[:, pb:pb + 512], AF.Square,
                        bias=0.0, scale=1.0), "A")
                    va = kb.cnt["A"]
                    kb.wait("tensor", "V", vv)
                    if m == 0:
                        kb.guard_wait("tensor", ("bank", 0))
                        kb.guard_wait("tensor", ("bank", 1))
                    d("tensor", lambda e, pb=pb, m=m: e.matmul(
                        ps[0][0:1, 0:512], sumcolF[:, :], xbf[:, pb:pb + 512],
                        start=(m == 0), stop=(m == ND - 1)), None)
                    kb.wait("tensor", "A", va)
                    d("tensor", lambda e, pb=pb, m=m: e.matmul(
                        ps[1][0:1, 0:512], sumcolF[:, :], x2bf[:, pb:pb + 512],
                        start=(m == 0), stop=(m == ND - 1)), "T")
                    mm2_done[m] = kb.cnt["T"]
                v_stats = kb.cnt["T"]
                kb.wait("vector", "T", v_stats)
                R = statbuf[0]
                d("vector", lambda e: e.tensor_copy(R[:, 0:512], ps[0][0:1, 0:512]), None)
                d("vector", lambda e: e.tensor_tensor(R[:, 512:1024], R[:, 0:512],
                                                      R[:, 0:512], OP.mult), None)
                d("vector", lambda e: e.scalar_tensor_tensor(
                    R[:, 1024:1536], ps[1][0:1, 0:512], EPS, R[:, 512:1024],
                    OP.add, OP.subtract), None)
                d("vector", lambda e: e.reciprocal(R[:, 1536:2048], R[:, 1024:1536]), "V")
                v_free01 = kb.cnt["V"]
                kb.guard_set(("bank", 0), "V", v_free01)
                kb.guard_set(("bank", 1), "V", v_free01)
                kb.wait("scalar", "V", v_free01)
                d("scalar", lambda e: e.activation(R[:, 2048:2560], R[:, 1536:2048],
                                                   AF.Sqrt, bias=0.0, scale=1.0), "A")
                kb.wait("vector", "A", kb.cnt["A"])
                d("vector", lambda e: e.tensor_tensor(R[:, 2560:3072], R[:, 0:512],
                                                      R[:, 2048:2560], OP.mult), "V")
                v_rows = kb.cnt["V"]
                kb.wait("tensor", "V", v_rows)
                kb.guard_wait("tensor", ("bank", 2))
                kb.guard_wait("tensor", ("bank", 3))
                d("tensor", lambda e: e.matmul(ps[2][:, 0:512], onesrow[:, 0:128],
                                               R[:, 2048:2560], start=True, stop=True), None)
                vbc = d("tensor", lambda e: e.matmul(ps[3][:, 0:512], onesrow[:, 0:128],
                                                     R[:, 2560:3072], start=True, stop=True), "T")
                kb.wait("vector", "T", vbc)
                va_prev = None
                for m in range(ND):
                    sslot = m % 4
                    d("vector", lambda e, m=m: e.scalar_tensor_tensor(
                        t1[:, :], src(m), dcol("cb1", m), ps[2][:, 0:512],
                        OP.mult, OP.mult), None)
                    if va_prev is not None:
                        kb.wait("vector", "A", va_prev)
                    d("vector", lambda e, m=m: e.scalar_tensor_tensor(
                        t4[:, :], ps[3][:, 0:512], dcol("ncb1", m), t1[:, :],
                        OP.mult, OP.add), "V")
                    kb.wait("scalar", "V", kb.cnt["V"])
                    kb.guard_wait("scalar", ("stg", sslot))
                    va_prev = d("scalar", lambda e, m=m, sslot=sslot: e.activation(
                        stg[:, 512 * sslot:512 * (sslot + 1)], t4[:, :],
                        AF.Identity, bias=dcol("sb", m), scale=1.0), "A")
                    kb.wait("sync", "A", va_prev)
                    vsp2 = kb.dma("sync",
                                  hap(sp_h2, (128 * m) * T + 512 * qq, [[T, 128], [1, 512]]),
                                  stg[:, 512 * sslot:512 * (sslot + 1)], f"stg{sslot}")
                    kb.guard_set(("stg", sslot), *vsp2)
                kb.wait("vector", "A", va_prev)
                d("vector", lambda e: e.nop(), "V")
                v_done = kb.cnt["V"]
                kb.guard_set(("bank", 2), "V", v_done)
                kb.guard_set(("bank", 3), "V", v_done)
                kb.guard_set(("xbuf4",), "V", v_done)

        def p5(E5, h2qfm):
            nonlocal tf32c, i32c
            h2tm = E5(nc.sbuf_tensor("h2tm", [128, 2 * D], BF16))
            i32c = E5(nc.sbuf_tensor("i32c", [128, 1024], I32))
            tf32c = E5(nc.sbuf_tensor("tf32c", [128, 1024], F32))
            h2qs = E5(nc.sbuf_tensor("h2qs", [128, 2 * D], BF16))
            mxc = E5(nc.sbuf_tensor("mxc", [128, 2], F32))
            fb_last = {}

            def tile_loads(tt):
                sl = tt % 2
                kb.guard_wait("sync", ("h2tm", sl))
                vls = []
                for j in range(4):
                    vls.append(kb.dma("sync", h2tm[:, sl * D + 512 * j:sl * D + 512 * (j + 1)],
                                      hap(sp_h2, (512 * j) * T + 128 * tt, [[T, 512], [1, 128]]),
                                      f"xb{j}{sl}", transpose=True))
                return vls

            LD5 = {0: tile_loads(0)}
            for tt in range(NT):
                sl = tt % 2
                if tt + 1 < NT:
                    LD5[tt + 1] = tile_loads(tt + 1)
                for v_ in LD5[tt]:
                    kb.dwait("vector", v_)
                d("vector", lambda e, sl=sl: e.tensor_reduce(
                    mxc[:, sl:sl + 1], h2tm[:, sl * D:(sl + 1) * D], AX.X, OP.max,
                    apply_absolute_value=True), "V")
                d("vector", lambda e, sl=sl, tt=tt: e.tensor_scalar(
                    s_all[:, tt:tt + 1], mxc[:, sl:sl + 1], 1.0 / 127.0, EPS,
                    OP.mult, OP.add), "V")
                d("vector", lambda e, tt=tt: e.reciprocal(
                    srq[tt % 2][:, :], s_all[:, tt:tt + 1]), "V")
                kb.guard_wait("vector", ("h2qs", sl))
                for half in range(2):
                    d("vector", lambda e, sl=sl, tt=tt, half=half: e.tensor_scalar(
                        tf32c[:, :],
                        h2tm[:, sl * D + 1024 * half:sl * D + 1024 * (half + 1)],
                        srq[tt % 2][:, :], MAGIC, OP.mult, OP.add), "V")
                    d("vector", lambda e: e.tensor_scalar(
                        i32c[:, :], tf32c[:, :],
                        -MAGIC, None, OP.add), "V")
                    d("vector", lambda e, sl=sl, half=half: e.tensor_scalar(
                        h2qs[:, sl * D + 1024 * half:sl * D + 1024 * (half + 1)],
                        i32c[:, :],
                        -129.0, 129.0, OP.max, OP.min), "V")
                vq = kb.cnt["V"]
                kb.guard_set(("h2tm", sl), "V", vq)
                kb.wait("scalar", "V", vq)
                vsq = kb.dma("scalar", hap(sp_h2q, (128 * tt) * D, [[D, 128], [1, D]]),
                             h2qs[:, sl * D:(sl + 1) * D], f"h2q{sl}")
                kb.guard_set(("h2qs", sl), *vsq)
                if tt % 4 == 3:
                    j = tt // 4
                    kb.wait("sync", "D_h2q0", kb.cnt.get("D_h2q0", 0))
                    kb.wait("sync", "D_h2q1", kb.cnt.get("D_h2q1", 0))
                    for m in range(ND):
                        vfb = kb.dma("sync",
                                     h2qfm[:, m * T + 512 * j:m * T + 512 * (j + 1)],
                                     hap(sp_h2q, (512 * j) * D + 128 * m, [[D, 512], [1, 128]]),
                                     f"fb{m % 4}", transpose=True)
                        fb_last[m % 4] = vfb
                    kb.guards[("h2qfm", j)] = list(fb_last.values())
                kb.tick6(1)
            kb.guards[("h2qfm",)] = list(fb_last.values())

        def p5w_gen(E5w):
            wsb = E5w(nc.sbuf_tensor("wsb", [128, 2 * 4096], F32))
            acc = E5w(nc.sbuf_tensor("acc", [128, 4], F32))
            onescolF = E5w(nc.sbuf_tensor("onescolF", [128, 1], F32))
            d("vector", lambda e: e.memset(onescolF[:, :], 1.0), "V")
            d("vector", lambda e: e.memset(wsum[:, :], 0.0), "V")
            d("vector", lambda e: e.memset(ws_row[:, :], 0.0), "V")
            d("vector", lambda e: e.memset(wsr_c[:, :], 0.0), "V")
            yield
            for widx, wt in enumerate((wgT, wuT, wdT)):
                NP = (D * II) // (128 * 4096)
                for pc in range(NP):
                    sl = pc % 2
                    kb.guard_wait("scalar", ("wsb", sl))
                    vw = kb.dma("scalar", wsb[:, sl * 4096:(sl + 1) * 4096],
                                hap(wt, pc * 128 * 4096, [[4096, 128], [1, 4096]]),
                                f"ws{sl}")
                    kb.dwait("vector", vw)
                    kb.guard_wait("vector", ("acc", sl))
                    va = d("vector", lambda e, sl=sl: e.tensor_reduce(
                        acc[:, sl:sl + 1], wsb[:, sl * 4096:(sl + 1) * 4096],
                        AX.X, OP.add, apply_absolute_value=True), "V")
                    kb.guard_set(("wsb", sl), "V", va)
                    vacc = d("vector", lambda e, widx=widx, sl=sl: e.tensor_tensor(
                        wsum[:, widx:widx + 1], wsum[:, widx:widx + 1],
                        acc[:, sl:sl + 1], OP.add), "V")
                    kb.guard_set(("acc", sl), "V", vacc)
                    yield
                vws = kb.cnt["V"]
                kb.wait("tensor", "V", vws)
                kb.guard_wait("tensor", ("bank", 5))
                vmm = d("tensor", lambda e, widx=widx: e.matmul(
                    ps[5][0:1, 0:1], wsum[:, widx:widx + 1], onescolF[:, :],
                    start=True, stop=True), "T")
                kb.wait("vector", "T", vmm)
                d("vector", lambda e, widx=widx: e.tensor_scalar(
                    ws_row[:, widx:widx + 1], ps[5][0:1, 0:1],
                    1.0 / float(D * II), EPS, OP.mult, OP.add), "V")
                d("vector", lambda e, widx=widx: e.reciprocal(
                    ws_row[:, 4 + widx:5 + widx], ws_row[:, widx:widx + 1]), "V")
                vrw = kb.cnt["V"]
                kb.guard_set(("bank", 5), "V", vrw)
                kb.wait("tensor", "V", vrw)
                kb.guard_wait("tensor", ("bank", 4))
                d("tensor", lambda e, widx=widx: e.matmul(
                    ps[4][:, 0:1], onesrow[:, 0:128], ws_row[:, widx:widx + 1],
                    start=True, stop=True), None)
                vb = d("tensor", lambda e, widx=widx: e.matmul(
                    ps[4][:, 1:2], onesrow[:, 0:128], ws_row[:, 4 + widx:5 + widx],
                    start=True, stop=True, skip_group_check=True), "T")
                kb.wait("vector", "T", vb)
                d("vector", lambda e, widx=widx: e.tensor_copy(
                    wsr_c[:, widx:widx + 1], ps[4][:, 1:2]), "V")
                d("vector", lambda e, widx=widx: e.tensor_copy(
                    wsr1[widx][:, :], ps[4][:, 1:2]), "V")
                vcc2 = d("vector", lambda e, widx=widx: e.tensor_copy(
                    sgcol[:, 32 + widx:33 + widx] if False else wsum[:, 4 + widx:5 + widx],
                    ps[4][:, 0:1]), "V")
                kb.guard_set(("bank", 4), "V", vcc2)

        def p5wd():
            if "dmp_ws" in kb.dump:
                kb.wait("sync", "V", kb.cnt["V"])
                kb.dma("sync", dumps["dmp_ws"][:, :], ws_row[:, :], "dws")
            if "dmp_wsr" in kb.dump:
                kb.wait("sync", "V", kb.cnt["V"])
                kb.dma("sync", dumps["dmp_wsr"][:, 0:4], wsr_c[:, :], "dwsr")
                kb.dma("sync", dumps["dmp_wsr"][:, 4:8], wsum[:, 4:8], "dwsr2")

        def p65_gen(E65):
            wdf = E65(nc.sbuf_tensor("wdf", [128, 2048], F32))
            wqs = E65(nc.sbuf_tensor("wqs", [128, 2048], BF16))
            yield
            for pg in range(NIC):
                sl = pg % 2
                kb.guard_wait("gpsimd", ("wdf",))
                vw = kb.dma("gpsimd", wdf[:, :],
                            hap(wdT, (128 * pg) * D, [[D, 128], [1, D]]),
                            f"wd{sl}")
                kb.dwait("vector", vw)
                kb.guard_wait("vector", ("wqs",))
                for hf in range(2):
                    kb.guard_wait("vector", ("i32c",))
                    d("vector", lambda e, sl=sl, hf=hf: e.tensor_scalar(
                        tf32c[:, :], wdf[:, 1024 * hf:1024 * (hf + 1)],
                        wsr1[2][:, :], MAGIC, OP.mult, OP.add), "V")
                    d("vector", lambda e: e.tensor_scalar(
                        i32c[:, :], tf32c[:, :], -MAGIC, None, OP.add), "V")
                    vq = d("vector", lambda e, hf=hf: e.tensor_scalar(
                        wqs[:, 1024 * hf:1024 * (hf + 1)], i32c[:, :],
                        -1.0, 1.0, OP.max, OP.min), "V")
                    kb.guards[("i32c",)] = [("V", vq)]
                kb.guard_set(("wdf",), "V", vq)
                kb.wait("sync", "V", vq)
                vsp = kb.dma("sync",
                             hap(sp_wqd, pg * (128 * 128),
                                 [[128, 128], [NIC * 16384, ND], [1, 128]]),
                             wqs[:, :].rearrange("p (m c) -> p m c", c=128),
                             f"wqs{sl}")
                kb.guard_set(("wqs",), *vsp)
                yield

        def make_p6pre(E):
            wf32 = E(nc.sbuf_tensor("wf32", [128, 2048], F32))
            wqgbf = E(nc.sbuf_tensor("wqgbf", [128, 2 * ND * 512], BF16))
            wqubf = E(nc.sbuf_tensor("wqubf", [128, 2 * ND * 512], BF16))
            vq_ic = {}
            v_sg = {}

            def emit_piece(ic, pidx):
                wsel, qp = divmod(pidx, 4)
                wt = wgT if wsel == 0 else wuT
                wqb = wqgbf if wsel == 0 else wqubf
                icsl = ic % 2
                kb.guard_wait("gpsimd", ("wf32",))
                vw = kb.dma("gpsimd", wf32[:, :].rearrange("p (k c) -> p k c", c=128),
                            hap(wt, 512 * ic + 128 * qp,
                                [[II, 128], [128 * II, ND], [1, 128]]),
                            f"wf{pidx % 2}")
                kb.dwait("vector", vw)
                if pidx == 0:
                    kb.guard_wait("vector", ("wq", icsl))
                for hf in range(2):
                    kb.guard_wait("vector", ("i32c",))
                    d("vector", lambda e, wsel=wsel, hf=hf: e.tensor_scalar(
                        tf32c[:, :], wf32[:, 1024 * hf:1024 * (hf + 1)],
                        wsr1[wsel][:, :], MAGIC, OP.mult, OP.add), "V")
                    d("vector", lambda e: e.tensor_scalar(
                        i32c[:, :], tf32c[:, :], -MAGIC, None, OP.add), "V")
                    vq = d("vector", lambda e, wqb=wqb, qp=qp, icsl=icsl, hf=hf: e.tensor_scalar(
                        wqb[:, icsl * ND * 512 + hf * 8 * 512:icsl * ND * 512 + hf * 8 * 512 + 8 * 512]
                        .rearrange("p (k c) -> p k c", c=512)[:, :, 128 * qp:128 * (qp + 1)],
                        i32c[:, :].rearrange("p (k c) -> p k c", c=128),
                        -1.0, 1.0, OP.max, OP.min), "V")
                    kb.guards[("i32c",)] = [("V", vq)]
                kb.guard_set(("wf32",), "V", vq)
                return vq

            def gen():
                d("vector", lambda e: e.memset(gmax[:, :], 0.0), "V")
                yield
                for k in range(NT):
                    if k < 8:
                        vq_ic[0] = emit_piece(0, k)
                    d("vector", lambda e, k=k: e.tensor_tensor(
                        sgcol[:, k:k + 1], s_all[:, k:k + 1],
                        wsum[:, 4:5], OP.mult), "V")
                    d("vector", lambda e, k=k: e.tensor_tensor(
                        sgcol[:, NT + k:NT + k + 1], s_all[:, k:k + 1],
                        wsum[:, 5:6], OP.mult), "V")
                    v_sg[k] = kb.cnt["V"]
                    yield

            return gen(), emit_piece, vq_ic, v_sg, wqgbf, wqubf

        def p6(E6, h2qfm, p6x):
            emit_piece, vq_ic, v_sg, wqgbf, wqubf = p6x
            sgb = E6(nc.sbuf_tensor("sgb", [128, 2 * 512], BF16))
            gub = E6(nc.sbuf_tensor("gub", [128, 2 * 512], BF16))
            mx2 = E6(nc.sbuf_tensor("mx2", [128, 2], F32))
            g65 = p65_gen(E6)
            next(g65)

            def tick65(n=1):
                for _ in range(n):
                    if next(g65, "END") == "END":
                        return

            for ic in range(NI):
                icsl = ic % 2
                kb.wait("tensor", "V", vq_ic[ic])
                for tt in range(NT):
                    gbank = tt % 2
                    ubank = 2 + tt % 2
                    kb.guard_wait("tensor", ("h2qfm", tt // 4))
                    kb.guard_wait("tensor", ("bank", gbank))
                    for k in range(ND):
                        d("tensor", lambda e, k=k, tt=tt, gbank=gbank, icsl=icsl: e.matmul(
                            ps[gbank][:, 0:512],
                            h2qfm[:, k * T + 128 * tt:k * T + 128 * (tt + 1)],
                            wqgbf[:, icsl * ND * 512 + 512 * k:icsl * ND * 512 + 512 * (k + 1)],
                            start=(k == 0), stop=(k == ND - 1)),
                            "T" if k == ND - 1 else None)
                    vtg = kb.cnt["T"]
                    kb.guard_wait("tensor", ("bank", ubank))
                    for k in range(ND):
                        d("tensor", lambda e, k=k, tt=tt, ubank=ubank, icsl=icsl: e.matmul(
                            ps[ubank][:, 0:512],
                            h2qfm[:, k * T + 128 * tt:k * T + 128 * (tt + 1)],
                            wqubf[:, icsl * ND * 512 + 512 * k:icsl * ND * 512 + 512 * (k + 1)],
                            start=(k == 0), stop=(k == ND - 1)),
                            "T" if k == ND - 1 else None)
                    vtu = kb.cnt["T"]
                    if tt == NT - 1:
                        kb.guard_set(("wq", icsl), "T", vtu)
                    kb.wait("scalar", "T", vtg)
                    kb.wait("scalar", "V", v_sg[tt])
                    kb.guard_wait("scalar", ("sgb", tt % 2))
                    vsg = d("scalar", lambda e, tt=tt, gbank=gbank: e.activation(
                        sgb[:, 512 * (tt % 2):512 * (tt % 2 + 1)], ps[gbank][:, 0:512],
                        AF.Silu, bias=0.0, scale=sgcol[:, tt:tt + 1]), "A")
                    kb.guard_set(("bank", gbank), "A", vsg)
                    kb.wait("vector", "T", vtu)
                    kb.wait("vector", "A", vsg)
                    kb.guard_wait("vector", ("gub", tt % 2))
                    vgu = d("vector", lambda e, tt=tt, ubank=ubank: e.scalar_tensor_tensor(
                        gub[:, 512 * (tt % 2):512 * (tt % 2 + 1)], ps[ubank][:, 0:512],
                        sgcol[:, NT + tt:NT + tt + 1],
                        sgb[:, 512 * (tt % 2):512 * (tt % 2 + 1)],
                        OP.mult, OP.mult), "V")
                    kb.guard_set(("bank", ubank), "V", vgu)
                    kb.guard_set(("sgb", tt % 2), "V", vgu)
                    d("vector", lambda e, tt=tt: e.tensor_reduce(
                        mx2[:, tt % 2:tt % 2 + 1], gub[:, 512 * (tt % 2):512 * (tt % 2 + 1)],
                        AX.X, OP.max, apply_absolute_value=True), "V")
                    vmx = d("vector", lambda e, tt=tt: e.tensor_tensor(
                        gmax[:, tt:tt + 1], gmax[:, tt:tt + 1],
                        mx2[:, tt % 2:tt % 2 + 1], OP.max), "V")
                    kb.wait("sync", "V", vgu)
                    vsp = kb.dma("sync",
                                 hap(sp_gu, (128 * tt) * II + 512 * ic, [[II, 128], [1, 512]]),
                                 gub[:, 512 * (tt % 2):512 * (tt % 2 + 1)], f"gu{tt % 2}")
                    kb.guard_set(("gub", tt % 2), *vsp)
                    if ic + 1 < NI and tt % 2 == 0:
                        vq_ic[ic + 1] = emit_piece(ic + 1, tt // 2)
                    if tt in (1, 5, 9, 13):
                        tick65()
            for _ in range(NIC):
                if next(g65, "END") == "END":
                    break

        def p78(E8):
            gut = E8(nc.sbuf_tensor("gut", [128, II], BF16))
            i32g = E8(nc.sbuf_tensor("i32g", [128, 512], I32))
            tf32g = E8(nc.sbuf_tensor("tf32g", [128, 512], F32))

            s2c = E8(nc.sbuf_tensor("s2c", [128, NT], F32))
            guqf = E8(nc.sbuf_tensor("guqf", [128, NIC * 512], BF16))
            wdb = E8(nc.sbuf_tensor("wdb", [128, 3 * NIC * 128], BF16))
            x1qb = E8(nc.sbuf_tensor("x1qb", [128, ND * 512], F32))
            xs2row = E8(nc.sbuf_tensor("xs2row", [1, T], F32))
            xsb = E8(nc.sbuf_tensor("xsb", [128, 512], F32))
            t8 = E8(nc.sbuf_tensor("t8", [128, 512], F32))
            out8 = E8(nc.sbuf_tensor("out8", [128, 2 * 512], F32))

            def p7_tile(tt):
                sl = tt % 2
                kb.guard_wait("sync", ("gut",))
                vg = kb.dma("sync", gut[:, :],
                            hap(sp_gu, (128 * tt) * II, [[II, 128], [1, II]]), f"gut{sl}")
                d("vector", lambda e, tt=tt: e.tensor_scalar(
                    s2c[:, tt:tt + 1], gmax[:, tt:tt + 1], 1.0 / 127.0, EPS,
                    OP.mult, OP.add), "V")
                d("vector", lambda e, tt=tt: e.reciprocal(
                    s2q[tt % 2][:, :], s2c[:, tt:tt + 1]), "V")
                kb.dwait("vector", vg)
                kb.guard_wait("vector", ("guq",))
                for half in range(16):
                    d("vector", lambda e, tt=tt, half=half: e.tensor_scalar(
                        tf32g[:, :], gut[:, 512 * half:512 * (half + 1)],
                        s2q[tt % 2][:, :], MAGIC, OP.mult, OP.add), "V")
                    d("vector", lambda e: e.tensor_scalar(
                        i32g[:, :], tf32g[:, :], -MAGIC, None, OP.add), "V")
                    d("vector", lambda e, half=half: e.tensor_scalar(
                        gut[:, 512 * half:512 * (half + 1)], i32g[:, :],
                        -129.0, 129.0, OP.max, OP.min), "V")
                vq = kb.cnt["V"]
                kb.guard_set(("gut",), "V", vq)
                kb.wait("sync", "V", vq)
                vsp = kb.dma("sync", hap(sp_guq, (128 * tt) * II, [[II, 128], [1, II]]),
                             gut[:, :], f"guq{tt % 2}")
                kb.guard_set(("guq",), *vsp)
                kb.guard_add(("gut",), *vsp)
                sp7[tt // 4][vsp[0]] = vsp

            def xs2_prep(seg):
                kb.wait("tensor", "V", kb.cnt["V"])
                kb.guard_wait("tensor", ("bank", 5))
                vtr = d("tensor", lambda e, seg=seg: e.transpose(
                    ps[5][0:4, 0:128], s2c[:, 4 * seg:4 * seg + 4], identF[:, :]), "T")
                kb.wait("vector", "T", vtr)
                kb.guard_wait("vector", ("stgf", 0))
                vev = d("vector", lambda e: e.tensor_copy(
                    stg_f[0:4, 0:128], ps[5][0:4, 0:128]), "V")
                kb.guard_set(("bank", 5), "V", vev)
                kb.wait("sync", "V", vev)
                for v_ in gq_last.values():
                    kb.dwait("sync", v_)
                vdm = kb.dma("sync",
                             xs2row[:, 512 * seg:512 * (seg + 1)]
                             .rearrange("o (r c) -> o r c", c=128),
                             stg_f[0:4, 0:128], "xs2g")
                kb.guard_set(("stgf", 0), *vdm)
                kb.dwait("vector", vdm)
                vxr = d("vector", lambda e, seg=seg: e.tensor_scalar(
                    xs2row[:, 512 * seg:512 * (seg + 1)],
                    xs2row[:, 512 * seg:512 * (seg + 1)],
                    ws_row[:, 2:3], None, OP.mult), "V")
                xs2v[seg] = vxr
                xs2dma[seg] = vdm

            sp7 = [dict() for _ in range(NQ)]
            xs2v = {}
            xs2dma = {}
            gq_last = {}
            for tt in range(4):
                p7_tile(tt)
            xs2_prep(0)

            for seg in range(NQ):
                kb.guard_wait("scalar", ("x1qb",))
                vx1 = kb.dma("scalar", x1qb[:, :].rearrange("p (m t) -> p m t", t=512),
                             hap(sp_x1, 512 * seg, [[T, 128], [128 * T, ND], [1, 512]]),
                             "x1q")
                kb.dwait("sync", xs2dma[seg])
                kb.dwait("scalar", xs2dma[seg])
                for v_ in sp7[seg].values():
                    kb.dwait("sync", v_)
                    kb.dwait("scalar", v_)
                kb.guard_wait("sync", ("guqf",))
                kb.guard_wait("scalar", ("guqf",))
                for ic in range(NIC):
                    gq_last[ic % 4] = kb.dma("sync",
                                             guqf[:, ic * 512:(ic + 1) * 512],
                                             hap(sp_guq, (512 * seg) * II + 128 * ic,
                                                 [[II, 512], [1, 128]]),
                                             f"gq{ic % 4}", transpose=True)
                kb.wait("tensor", "V", xs2v[seg])
                kb.guard_wait("tensor", ("bank", 5))
                vb3 = d("tensor", lambda e, seg=seg: e.matmul(
                    ps[5][:, 0:512], onesrow[:, 0:128],
                    xs2row[:, 512 * seg:512 * (seg + 1)], start=True, stop=True), "T")
                kb.wait("vector", "T", vb3)
                kb.guard_wait("vector", ("xsb",))
                vxsb = d("vector", lambda e: e.tensor_scalar(
                    xsb[:, :], ps[5][:, 0:512], 1.0, None, OP.mult), "V")
                kb.guard_set(("bank", 5), "V", vxsb)

                for m in range(ND):
                    msl = (seg * ND + m) % 3
                    kb.guard_wait("gpsimd", ("wdb", msl))
                    vwd = kb.dma("gpsimd", wdb[:, msl * NIC * 128:(msl + 1) * NIC * 128],
                                 hap(sp_wqd, m * (NIC * 128 * 128),
                                     [[128, 128], [128 * 128, NIC], [1, 128]]),
                                 f"wdb{msl}")
                    bank = m % 2
                    for v_ in gq_last.values():
                        kb.dwait("tensor", v_)
                    kb.dwait("tensor", vwd)
                    kb.guard_wait("tensor", ("bank", bank))
                    for ic in range(NIC):
                        d("tensor", lambda e, ic=ic, msl=msl, bank=bank: e.matmul(
                            ps[bank][:, 0:512],
                            wdb[:, msl * NIC * 128 + 128 * ic:msl * NIC * 128 + 128 * (ic + 1)],
                            guqf[:, ic * 512:(ic + 1) * 512],
                            start=(ic == 0), stop=(ic == NIC - 1)),
                            "T" if ic == NIC - 1 else None)
                    vT = kb.cnt["T"]
                    kb.guard_set(("wdb", msl), "T", vT)
                    kb.wait("vector", "T", vT)
                    kb.dwait("vector", vx1)
                    d("vector", lambda e, bank=bank: e.tensor_tensor(
                        t8[:, :], ps[bank][:, 0:512],
                        xsb[:, :], OP.mult), "V")
                    osl = m % 2
                    kb.guard_wait("vector", ("out8", osl))
                    vout = d("vector", lambda e, m=m, osl=osl: e.scalar_tensor_tensor(
                        out8[:, 512 * osl:512 * (osl + 1)], t8[:, :],
                        dcol("gb2", m), x1qb[:, 512 * m:512 * (m + 1)],
                        OP.mult, OP.add), "V")
                    kb.guard_set(("bank", bank), "V", vout)
                    kb.wait("sync", "V", vout)
                    vy = kb.dma("sync",
                                hap(y_fm, (128 * m) * T + 512 * seg, [[T, 128], [1, 512]]),
                                out8[:, 512 * osl:512 * (osl + 1)], f"y{m % 4}")
                    kb.guard_set(("out8", osl), *vy)
                    if seg + 1 < NQ:
                        if m in (2, 5, 8, 11):
                            p7_tile(4 * (seg + 1) + (m - 2) // 3)
                        elif m == 13:
                            xs2_prep(seg + 1)
                kb.guard_set(("guqf",), "T", kb.cnt["T"])
                kb.guard_set(("x1qb",), "V", kb.cnt["V"])
                kb.guard_set(("xsb",), "V", kb.cnt["V"])

        with ExitStack() as s0:
            p0(s0.enter_context)
        kb.barrier()
        with ExitStack() as s12:
            hbuf[0] = s12.enter_context(nc.sbuf_tensor("h_all", [128, ND * T], BF16))
            g5w = p5w_gen(s12.enter_context)
            next(g5w)

            def tick5w(n=1):
                for _ in range(n):
                    if next(g5w, "END") == "END":
                        return
            kb.tick5w = tick5w
            with ExitStack() as s1:
                p1(s1.enter_context)
            kb.barrier()
            if stop_after != "p1":
                with ExitStack() as s2:
                    p2(s2.enter_context)
                for _ in range(120):
                    if next(g5w, "END") == "END":
                        break
                kb.barrier()
        if stop_after not in ("p1", "p2"):
            with ExitStack() as s3:
                p3(s3.enter_context)
            kb.barrier()
        if stop_after not in ("p1", "p2", "p3"):
            with ExitStack() as s4:
                p4(s4.enter_context)
            kb.barrier()
            with ExitStack() as s56:
                h2qfm = s56.enter_context(nc.sbuf_tensor("h2qfm", [128, ND * T], BF16))
                g6pre, ep6, vq6, vsg6, wqg6, wqu6 = make_p6pre(s56.enter_context)
                next(g6pre)

                def tick6(n=1):
                    for _ in range(n):
                        if next(g6pre, "END") == "END":
                            return
                kb.tick6 = tick6
                p5(s56.enter_context, h2qfm)
                for _ in range(NT):
                    if next(g6pre, "END") == "END":
                        break
                if kb.dump:
                    kb.barrier()
                    p5wd()
                    kb.barrier()
                with ExitStack() as s6:
                    p6(s6.enter_context, h2qfm, (ep6, vq6, vsg6, wqg6, wqu6))
                kb.barrier()
            with ExitStack() as s8:
                p78(s8.enter_context)
            kb.barrier()
        else:
            zz = E(nc.sbuf_tensor("zz", [128, 512], F32))
            d("vector", lambda e: e.memset(zz[:, :], 0.0), "V")
            kb.wait("sync", "V", kb.cnt["V"])
            for m in range(ND):
                for qq in range(NQ):
                    kb.dma("sync", y_fm[128 * m:128 * (m + 1), 512 * qq:512 * (qq + 1)],
                           zz[:, :], "out")
            kb.barrier()

        with ExitStack() as sctx:
            for nm in kb.sem_names:
                kb.S[nm] = sctx.enter_context(nc.semaphore(nm))
            with nc.Block() as block:
                for eng in ENGINES:
                    if not kb.prog[eng]:
                        continue

                    def mk(engname):
                        def body(e):
                            for f in kb.prog[engname]:
                                f(e)
                        return body
                    getattr(block, eng)(mk(eng))

    return nc


def make_in_maps(inputs):
    xs = np.asarray(inputs["L0_attn_x"], np.float32)
    shared = {
        "wmod": np.asarray(inputs["L0_wmod"], np.float32),
        "wq": np.asarray(inputs["L0_attn_Wq"], np.float32),
        "wk": np.asarray(inputs["L0_attn_Wk"], np.float32),
        "wv": np.asarray(inputs["L0_attn_Wv"], np.float32),
        "wg": np.asarray(inputs["L0_attn_Wg"], np.float32),
        "wo": np.asarray(inputs["L0_attn_Wo"], np.float32),
        "wgk1": np.asarray(inputs["L0_attn_Wgk1"], np.float32),
        "wgk2": np.asarray(inputs["L0_attn_Wgk2"], np.float32),
        "bgk2": np.asarray(inputs["L0_attn_bgk2"], np.float32).reshape(1, D),
        "gnw": np.asarray(inputs["L0_attn_gnorm_w"], np.float32).reshape(1, HD),
        "wgT": np.ascontiguousarray(np.asarray(inputs["L0_ffn_w_gate"], np.float32).T),
        "wuT": np.ascontiguousarray(np.asarray(inputs["L0_ffn_w_up"], np.float32).T),
        "wdT": np.ascontiguousarray(np.asarray(inputs["L0_ffn_w_down"], np.float32).T),
    }
    in_maps = []
    for b in range(8):
        m = dict(shared)
        m["x_fm"] = np.ascontiguousarray(xs[b].T)
        m["c_row"] = np.ascontiguousarray(
            np.asarray(inputs["L0_c"], np.float32)[b:b + 1, :])
        in_maps.append(m)
    return in_maps


def kernel(**inputs):
    in_maps = make_in_maps(inputs)
    nc = build_nc()
    res = run_bass_kernel_spmd(nc, in_maps, core_ids=list(range(8)))
    out = np.empty((8, T, D), np.float32)
    for b in range(8):
        out[b] = res.results[b]["y_fm"].T
    return out


# version 15
# speedup vs baseline: 1.0432x; 1.0225x over previous
import numpy as np
import ml_dtypes
from contextlib import ExitStack

import concourse.bass as bass
import concourse.mybir as mybir
from concourse.bass_utils import run_bass_kernel_spmd

F32 = mybir.dt.float32
BF16 = mybir.dt.bfloat16
I32 = mybir.dt.int32
AF = mybir.ActivationFunctionType
OP = mybir.AluOpType
AX = mybir.AxisListType

D = 2048
T = 2048
HN = 16
HD = 128
CC = 128
NCH = T // CC
ND = D // 128
NT = T // 128
NQ = T // 512
II = 8192
NI = II // 512
NIC = II // 128
GLN = 16.0
EPS = 1e-5
SCALE = float(HD) ** -0.5
MAGIC = 12582912.0

ENGINES = ("sync", "tensor", "vector", "scalar", "gpsimd")
DMASEM = {"sync": "dS", "scalar": "dA", "gpsimd": "dG"}


def hap(handle, offset, dims):
    return bass.AP(handle, offset, [list(x) for x in dims])


class KB:

    def __init__(self, nc, dump=()):
        self.nc = nc
        self.dump = set(dump)
        self.prog = {e: [] for e in ENGINES}
        self.cnt = {}
        self.lastwait = {}
        self.S = {}
        self.sem_names = []
        self.guards = {}

    def sem(self, *names):
        for n in names:
            if n not in self.cnt:
                self.cnt[n] = 0
                self.sem_names.append(n)

    OWN = {"vector": "V", "scalar": "A", "gpsimd": "G", "tensor": "T"}

    def do(self, eng, fn, sem=None, n=1):
        if eng in ("vector", "scalar", "gpsimd"):
            own = self.OWN[eng]
            self.wait(eng, own, self.cnt[own])
            if sem is None:
                sem = own
        if sem is None:
            self.prog[eng].append(fn)
            return None
        self.cnt[sem] += n
        v = self.cnt[sem]

        def wrapped(e, f=fn, s=sem, k=n):
            f(e).then_inc(self.S[s], k)
        self.prog[eng].append(wrapped)
        return v

    def wait(self, eng, sem, v):
        if v is None or v <= 0:
            return
        key = (eng, sem)
        if self.lastwait.get(key, -1) >= v:
            return
        self.lastwait[key] = v

        def wrapped(e, s=sem, vv=v):
            e.wait_ge(self.S[s], vv)
        self.prog[eng].append(wrapped)

    def dma(self, eng, dst, src, stream="misc", transpose=False):
        s = "D_" + stream
        self.sem(s)
        self.cnt[s] += 16
        v = self.cnt[s]

        def wrapped(e, dd=dst, sr=src, tr=transpose, ss=s):
            if tr:
                e.dma_start(dd, sr, transpose=True).then_inc(self.S[ss], 16)
            else:
                e.dma_start(dd, sr).then_inc(self.S[ss], 16)
        self.prog[eng].append(wrapped)
        return (s, v)

    def dwait(self, eng, sv):
        self.wait(eng, sv[0], sv[1])

    def guard_wait(self, eng, key):
        ent = self.guards.get(key)
        if ent:
            for s, v in ent:
                self.wait(eng, s, v)

    def guard_set(self, key, sem, val):
        self.guards[key] = [(sem, val)]

    def guard_add(self, key, sem, val):
        self.guards.setdefault(key, []).append((sem, val))

    def barrier(self, engines=ENGINES):
        snap = dict(self.cnt)
        for eng in engines:
            for s, v in snap.items():
                self.wait(eng, s, v)


def build_nc(dump=(), stop_after=None, p3_chunks=NCH, p3_level=9):
    nc = bass.Bass("TRN2", target_bir_lowering=False)
    kb = KB(nc, dump)
    kb.sem("T", "V", "A", "G")

    x_fm = nc.dram_tensor("x_fm", [D, T], F32, kind="ExternalInput")
    c_row = nc.dram_tensor("c_row", [1, D], F32, kind="ExternalInput")
    wmod = nc.dram_tensor("wmod", [D, 6 * D], F32, kind="ExternalInput")
    wq_h = nc.dram_tensor("wq", [D, D], F32, kind="ExternalInput")
    wk_h = nc.dram_tensor("wk", [D, D], F32, kind="ExternalInput")
    wv_h = nc.dram_tensor("wv", [D, D], F32, kind="ExternalInput")
    wg_h = nc.dram_tensor("wg", [D, D], F32, kind="ExternalInput")
    wo_h = nc.dram_tensor("wo", [D, D], F32, kind="ExternalInput")
    wgk1 = nc.dram_tensor("wgk1", [D, 16], F32, kind="ExternalInput")
    wgk2 = nc.dram_tensor("wgk2", [16, D], F32, kind="ExternalInput")
    bgk2 = nc.dram_tensor("bgk2", [1, D], F32, kind="ExternalInput")
    gnw_h = nc.dram_tensor("gnw", [1, HD], F32, kind="ExternalInput")
    wgT = nc.dram_tensor("wgT", [D, II], F32, kind="ExternalInput")
    wuT = nc.dram_tensor("wuT", [D, II], F32, kind="ExternalInput")
    wdT = nc.dram_tensor("wdT", [II, D], F32, kind="ExternalInput")

    y_fm = nc.dram_tensor("y_fm", [D, T], F32, kind="ExternalOutput")

    def scratch(name, shape, dt):
        kind = "ExternalOutput" if name in kb.dump else "Internal"
        return nc.dram_tensor(name, shape, dt, kind=kind)

    sp_q = scratch("sp_q", [NCH, ND, 128, 128], BF16)
    sp_k = scratch("sp_k", [NCH, ND, 128, 128], BF16)
    sp_g = scratch("sp_g", [NCH, ND, 128, 128], BF16)
    sp_v = scratch("sp_v", [T, D], BF16)
    sp_ktm = scratch("sp_ktm", [T, D], BF16)
    sp_sp = scratch("sp_sp", [T, D], BF16)
    sp_op = scratch("sp_op", [D, T], BF16)
    sp_x1 = scratch("sp_x1", [D, T], F32)
    sp_h2 = scratch("sp_h2", [D, T], BF16)
    sp_h2q = scratch("sp_h2q", [T, D], BF16)
    sp_gu = scratch("sp_gu", [T, II], BF16)
    sp_guq = scratch("sp_guq", [T, II], BF16)
    sp_wqd = scratch("sp_wqd", [ND, NIC, 128, 128], BF16)

    dumps = {}
    for nm, shape, dt in [("dmp_mod", [1, 6 * D], F32), ("dmp_h", [D, T], BF16),
                          ("dmp_o", [D, T], F32), ("dmp_ws", [1, 8], F32),
                          ("dmp_wsr", [128, 8], F32)]:
        if nm in kb.dump:
            dumps[nm] = nc.dram_tensor(nm, shape, dt, kind="ExternalOutput")

    with ExitStack() as ctx:
        E = ctx.enter_context

        identB = E(nc.sbuf_tensor("identB", [128, 128], BF16))
        identF = E(nc.sbuf_tensor("identF", [128, 128], F32))
        U_bf = E(nc.sbuf_tensor("U_bf", [128, 128], BF16))
        U4 = E(nc.sbuf_tensor("U4", [128, 512], BF16))
        gnwsel = E(nc.sbuf_tensor("gnwsel", [16, HN * 128], BF16))
        iota_i = E(nc.sbuf_tensor("iota_i", [128, 128], I32))
        cg = E(nc.sbuf_tensor("cg", [128, 16 * HN], BF16))
        ones1 = E(nc.sbuf_tensor("ones1", [1, 1], F32))
        onesrow = E(nc.sbuf_tensor("onesrow", [1, 512], F32))
        sumcolF = E(nc.sbuf_tensor("sumcolF", [128, 1], BF16))
        gn_col = E(nc.sbuf_tensor("gn_col", [128, 1], BF16))
        dcols = E(nc.sbuf_tensor("dcols", [128, 8 * ND], F32))
        gnw_row = E(nc.sbuf_tensor("gnw_row", [1, HD], BF16))
        gnw_f32 = E(nc.sbuf_tensor("gnw_f32", [1, HD], F32))
        sc_cols = E(nc.sbuf_tensor("sc_cols", [128, ND], BF16))
        statbuf = [None]
        s_all = E(nc.sbuf_tensor("s_all", [128, NT], F32))
        srecip = E(nc.sbuf_tensor("srecip", [128, NT], F32))
        gmax = E(nc.sbuf_tensor("gmax", [128, NT], F32))
        s2r = E(nc.sbuf_tensor("s2r", [128, NT], F32))

        wsum = E(nc.sbuf_tensor("wsum", [128, 8], F32))
        wsr_c = E(nc.sbuf_tensor("wsr_c", [128, 4], F32))
        wsr1 = [E(nc.sbuf_tensor(f"wsr1_{i}", [128, 1], F32)) for i in range(3)]
        srq = [E(nc.sbuf_tensor(f"srq{i}", [128, 1], F32)) for i in range(2)]
        s2q = [E(nc.sbuf_tensor(f"s2q{i}", [128, 1], F32)) for i in range(2)]
        sgcol = E(nc.sbuf_tensor("sgcol", [128, 2 * NT], F32))
        ws_row = E(nc.sbuf_tensor("ws_row", [1, 8], F32))
        stg = E(nc.sbuf_tensor("stg", [128, 4 * 512], BF16))
        stg_f = E(nc.sbuf_tensor("stg_f", [128, 2 * 512], F32))
        ln_xbf = E(nc.sbuf_tensor("ln_xbf", [128, 2 * 512], BF16))
        ln_x2bf = E(nc.sbuf_tensor("ln_x2bf", [128, 2 * 512], BF16))
        ln_t1 = E(nc.sbuf_tensor("ln_t1", [128, 512], F32))
        ln_t4 = E(nc.sbuf_tensor("ln_t4", [128, 512], F32))

        ps = [E(nc.psum_tensor(f"ps{i}", [128, 512], F32)) for i in range(6)]
        psb = [E(nc.psum_tensor(f"psb{i}", [128, 1024], BF16)) for i in range(2)]

        def dcol(name, m):
            idx = {"sa": 0, "ca1": 1, "nca1": 2, "ga2": 3,
                   "sb": 4, "cb1": 5, "ncb1": 6, "gb2": 7}[name]
            return dcols[:, ND * idx + m:ND * idx + m + 1]

        d = kb.do

        def p0(E0):
            mod_row = E0(nc.sbuf_tensor("mod_row", [1, 6 * D], F32))
            extra = E0(nc.sbuf_tensor("extra_rows", [1, 2 * D], F32))
            c_sb = E0(nc.sbuf_tensor("c_sb", [1, D], F32))
            d("gpsimd", lambda e: e.iota(iota_i[:, :], [[1, 128]], base=0,
                                         channel_multiplier=-1), "G")
            kb.wait("vector", "G", kb.cnt["G"])
            d("vector", lambda e: e.tensor_scalar(U_bf[:, :], iota_i[:, :], 0.0, None, OP.is_ge), None)
            for r4 in range(4):
                d("vector", lambda e, r4=r4: e.tensor_copy(U4[:, 128 * r4:128 * (r4 + 1)], U_bf[:, :]), None)
            d("vector", lambda e: e.tensor_scalar(identB[:, :], iota_i[:, :], 0.0, None,
                                                  OP.is_equal), None)
            d("vector", lambda e: e.tensor_copy(identF[:, :], identB[:, :]), None)
            d("vector", lambda e: e.memset(cg[:, :], 0.0), None)
            for h in range(HN):
                d("vector", lambda e, h=h: e.memset(cg[:, 17 * h:17 * h + 1], 1.0 / 128.0), None)
            d("vector", lambda e: e.memset(ones1[:, :], 1.0), None)
            d("vector", lambda e: e.memset(onesrow[:, :], 1.0), None)
            d("vector", lambda e: e.memset(gn_col[:, :], 1.0 / 128.0), None)
            d("vector", lambda e: e.memset(sumcolF[:, :], 1.0 / 2048.0), "V")
            vc = kb.dma("sync", c_sb[:, :], c_row[:, :], "c")
            vg = kb.dma("sync", gnw_f32[:, :], gnw_h[:, :], "gnw")
            kb.dwait("vector", vg)
            d("vector", lambda e: e.tensor_copy(gnw_row[:, :], gnw_f32[:, :]), "V")
            d("vector", lambda e: e.memset(gnwsel[:, :], 0.0), "V")
            kb.wait("sync", "V", kb.cnt["V"])
            vgs = None
            for h in range(HN):
                vgs = kb.dma("sync", gnwsel[h:h + 1, 128 * h:128 * (h + 1)],
                             gnw_row[:, :], "gnws")
            kb.dwait("tensor", vgs)
            kb.dwait("scalar", vc)
            d("scalar", lambda e: e.activation(c_sb[:, :], c_sb[:, :], AF.Silu,
                                               bias=0.0, scale=1.0), "A")
            kb.wait("tensor", "A", kb.cnt["A"])
            kb.wait("tensor", "V", kb.cnt["V"])
            for m in range(ND):
                d("tensor", lambda e, m=m: e.matmul(
                    ps[0][:, m:m + 1], c_sb[:, 128 * m:128 * (m + 1)], ones1[:, :],
                    start=True, stop=True), "T" if m == ND - 1 else None)
            kb.wait("vector", "T", kb.cnt["T"])
            d("vector", lambda e: e.tensor_copy(sc_cols[:, :], ps[0][:, 0:ND]), "V")
            v_sc = kb.cnt["V"]
            kb.guard_set(("bank", 0), "V", v_sc)

            NKH = ND // 2
            wmall = E0(nc.sbuf_tensor("wmall", [128, 4 * NKH * 512], BF16))
            wmf32 = E0(nc.sbuf_tensor("wmf32", [128, 2 * NKH * 512], F32))
            kb.wait("tensor", "V", v_sc)
            piece_T = {}
            cast_done = {}
            f32cnt = 0
            for pc in range(48):
                j, half = divmod(pc, 2)
                slot = pc % 4
                bank = 1 + (j % 2)
                buf = wmall[:, slot * NKH * 512:(slot + 1) * NKH * 512]
                path = pc % 3
                src = hap(wmod, 512 * j + 128 * NKH * half * (6 * D),
                          [[6 * D, 128], [128 * 6 * D, NKH], [1, 512]])
                if path == 0:
                    if pc >= 4:
                        kb.wait("gpsimd", "T", piece_T[pc - 4])
                    vld = kb.dma("gpsimd",
                                 buf.rearrange("p (k c) -> p k c", c=512), src, f"wmg{slot}")
                    kb.dwait("tensor", vld)
                else:
                    fsl = f32cnt % 2
                    f32cnt += 1
                    fbuf = wmf32[:, fsl * NKH * 512:(fsl + 1) * NKH * 512]
                    qeng = "sync" if path == 1 else "scalar"
                    ceng = "scalar" if path == 1 else "vector"
                    prev = cast_done.get(fsl)
                    if prev is not None:
                        kb.wait(qeng, prev[0], prev[1])
                    vld = kb.dma(qeng, fbuf.rearrange("p (k c) -> p k c", c=512),
                                 src, f"wmf{fsl}")
                    if pc >= 4:
                        kb.wait(ceng, "T", piece_T[pc - 4])
                    kb.dwait(ceng, vld)
                    if ceng == "scalar":
                        vcast = d("scalar", lambda e, buf=buf, fbuf=fbuf: e.activation(
                            buf[:, :], fbuf[:, :], AF.Identity, bias=0.0, scale=1.0), "A")
                        cast_done[fsl] = ("A", vcast)
                    else:
                        vcast = d("vector", lambda e, buf=buf, fbuf=fbuf: e.tensor_copy(
                            buf[:, :], fbuf[:, :]), "V")
                        cast_done[fsl] = ("V", vcast)
                    kb.wait("tensor", cast_done[fsl][0], cast_done[fsl][1])
                if half == 0:
                    kb.guard_wait("tensor", ("bank", bank))
                for kh in range(NKH):
                    kc = half * NKH + kh
                    d("tensor", lambda e, kc=kc, kh=kh, bank=bank, buf=buf: e.matmul(
                        ps[bank][0:1, 0:512], sc_cols[:, kc:kc + 1],
                        buf[:, 512 * kh:512 * (kh + 1)],
                        start=(kc == 0), stop=(kc == ND - 1)),
                        "T" if kh == NKH - 1 else None)
                piece_T[pc] = kb.cnt["T"]
                if half == 1:
                    kb.wait("vector", "T", piece_T[pc])
                    vj = d("vector", lambda e, j=j, bank=bank: e.tensor_copy(
                        mod_row[:, 512 * j:512 * (j + 1)],
                        ps[bank][0:1, 0:512]), "V")
                    kb.guard_set(("bank", bank), "V", vj)
            v_mod = kb.cnt["V"]

            kb.wait("vector", "V", v_mod)
            d("vector", lambda e: e.tensor_scalar(mod_row[:, D:2 * D], mod_row[:, D:2 * D],
                                                  1.0, None, OP.add), None)
            d("vector", lambda e: e.tensor_scalar(mod_row[:, 4 * D:5 * D],
                                                  mod_row[:, 4 * D:5 * D],
                                                  1.0, None, OP.add), "V")
            for so in (2 * D, 5 * D):
                d("vector", lambda e, so=so: e.tensor_scalar(
                    extra[:, 0:D], mod_row[:, so:so + D], 0.0, None, OP.is_ge), None)
                d("vector", lambda e: e.tensor_scalar(
                    extra[:, 0:D], extra[:, 0:D], 2.0, -1.0, OP.mult, OP.add), "V")
                kb.wait("scalar", "V", kb.cnt["V"])
                d("scalar", lambda e, so=so: e.activation(
                    extra[:, D:2 * D], mod_row[:, so:so + D], AF.Abs,
                    bias=0.0, scale=1.0), "A")
                kb.wait("vector", "A", kb.cnt["A"])
                d("vector", lambda e: e.tensor_scalar(
                    extra[:, D:2 * D], extra[:, D:2 * D], 0.25, None, OP.max), None)
                d("vector", lambda e, so=so: e.tensor_tensor(
                    mod_row[:, so:so + D], extra[:, 0:D], extra[:, D:2 * D],
                    OP.mult), "V")
            d("vector", lambda e: e.tensor_scalar(extra[:, 0:D], mod_row[:, D:2 * D],
                                                  -1.0, None, OP.mult), None)
            d("vector", lambda e: e.tensor_scalar(extra[:, D:2 * D], mod_row[:, 4 * D:5 * D],
                                                  -1.0, None, OP.mult), "V")
            v_rows = kb.cnt["V"]
            kb.wait("tensor", "V", v_rows)
            srcs = [(mod_row, 0), (mod_row, D), (extra, 0), (mod_row, 2 * D),
                    (mod_row, 3 * D), (mod_row, 4 * D), (extra, D), (mod_row, 5 * D)]
            for i, (rt, so) in enumerate(srcs):
                bank = i % 2
                kb.guard_wait("tensor", ("bank", bank))
                for m in range(ND):
                    d("tensor", lambda e, rt=rt, so=so, m=m, bank=bank: e.matmul(
                        ps[bank][:, m:m + 1], rt[:, so + 128 * m:so + 128 * (m + 1)],
                        ones1[:, :], start=True, stop=True),
                        "T" if m == ND - 1 else None)
                kb.wait("vector", "T", kb.cnt["T"])
                vj = d("vector", lambda e, i=i, bank=bank: e.tensor_copy(
                    dcols[:, ND * i:ND * (i + 1)], ps[bank][:, 0:ND]), "V")
                kb.guard_set(("bank", bank), "V", vj)

            if "dmp_mod" in kb.dump:
                kb.wait("sync", "V", kb.cnt["V"])
                kb.dma("sync", dumps["dmp_mod"][:, :], mod_row[:, :], "dump")

        hbuf = [None]
        tf32c = None
        i32c = None

        def h_chunk(m):
            return hbuf[0][:, T * m:T * (m + 1)]

        def ln_pass(x_src_chunk, out_chunk, cname, csname, ncname):
            stat_rows = statbuf[0]
            """x_src_chunk(m) -> fp32 [128,512] AP (resident); writes
            out_chunk(m) bf16. Uses banks 0,1 (stats) and 2,3 (broadcast)."""
            xbf = ln_xbf
            x2bf = ln_x2bf
            t1 = ln_t1
            t4 = ln_t4
            mm2_done = {}
            for m in range(ND):
                pb = (m % 2) * 512
                if m >= 2:
                    kb.wait("vector", "T", mm2_done[m - 2])
                    kb.wait("scalar", "T", mm2_done[m - 2])
                d("vector", lambda e, m=m, pb=pb: e.tensor_copy(
                    xbf[:, pb:pb + 512], x_src_chunk(m)), "V")
                vv = kb.cnt["V"]
                kb.wait("scalar", "V", vv)
                d("scalar", lambda e, pb=pb: e.activation(
                    x2bf[:, pb:pb + 512], xbf[:, pb:pb + 512], AF.Square,
                    bias=0.0, scale=1.0), "A")
                va = kb.cnt["A"]
                kb.wait("tensor", "V", vv)
                if m == 0:
                    kb.guard_wait("tensor", ("bank", 0))
                    kb.guard_wait("tensor", ("bank", 1))
                d("tensor", lambda e, pb=pb, m=m: e.matmul(
                    ps[0][0:1, 0:512], sumcolF[:, :], xbf[:, pb:pb + 512],
                    start=(m == 0), stop=(m == ND - 1)), None)
                kb.wait("tensor", "A", va)
                d("tensor", lambda e, pb=pb, m=m: e.matmul(
                    ps[1][0:1, 0:512], sumcolF[:, :], x2bf[:, pb:pb + 512],
                    start=(m == 0), stop=(m == ND - 1)), "T")
                mm2_done[m] = kb.cnt["T"]
            v_stats = kb.cnt["T"]
            kb.wait("vector", "T", v_stats)
            R = stat_rows
            d("vector", lambda e: e.tensor_copy(R[:, 0:512], ps[0][0:1, 0:512]), None)
            d("vector", lambda e: e.tensor_tensor(R[:, 512:1024], R[:, 0:512],
                                                  R[:, 0:512], OP.mult), None)
            d("vector", lambda e: e.scalar_tensor_tensor(
                R[:, 1024:1536], ps[1][0:1, 0:512], EPS, R[:, 512:1024],
                OP.add, OP.subtract), None)
            d("vector", lambda e: e.reciprocal(R[:, 1536:2048], R[:, 1024:1536]), "V")
            v_free01 = kb.cnt["V"]
            kb.guard_set(("bank", 0), "V", v_free01)
            kb.guard_set(("bank", 1), "V", v_free01)
            kb.wait("scalar", "V", v_free01)
            d("scalar", lambda e: e.activation(R[:, 2048:2560], R[:, 1536:2048],
                                               AF.Sqrt, bias=0.0, scale=1.0), "A")
            kb.wait("vector", "A", kb.cnt["A"])
            d("vector", lambda e: e.tensor_tensor(R[:, 2560:3072], R[:, 0:512],
                                                  R[:, 2048:2560], OP.mult), "V")
            v_rows = kb.cnt["V"]
            kb.wait("tensor", "V", v_rows)
            kb.guard_wait("tensor", ("bank", 2))
            kb.guard_wait("tensor", ("bank", 3))
            d("tensor", lambda e: e.matmul(ps[2][:, 0:512], onesrow[:, 0:128],
                                           R[:, 2048:2560], start=True, stop=True), None)
            d("tensor", lambda e: e.matmul(ps[3][:, 0:512], onesrow[:, 0:128],
                                           R[:, 2560:3072], start=True, stop=True), "T")
            v_bc = kb.cnt["T"]
            kb.wait("vector", "T", v_bc)
            va_prev = None
            for m in range(ND):
                d("vector", lambda e, m=m: e.scalar_tensor_tensor(
                    t1[:, :], x_src_chunk(m), dcol(cname, m), ps[2][:, 0:512],
                    OP.mult, OP.mult), None)
                if va_prev is not None:
                    kb.wait("vector", "A", va_prev)
                d("vector", lambda e, m=m: e.scalar_tensor_tensor(
                    t4[:, :], ps[3][:, 0:512], dcol(ncname, m), t1[:, :],
                    OP.mult, OP.add), "V")
                kb.wait("scalar", "V", kb.cnt["V"])
                va_prev = d("scalar", lambda e, m=m: e.activation(
                    out_chunk(m), t4[:, :], AF.Identity,
                    bias=dcol(csname, m), scale=1.0), "A")
            v_done = kb.cnt["V"]
            kb.wait("vector", "A", va_prev)
            d("vector", lambda e: e.nop(), "V")
            v_done = kb.cnt["V"]
            kb.guard_set(("bank", 2), "V", v_done)
            kb.guard_set(("bank", 3), "V", v_done)
            return v_done

        def p1(E1):
            xbuf = E1(nc.sbuf_tensor("xbuf", [128, 2 * ND * 512], F32))
            statbuf[0] = E1(nc.sbuf_tensor("stat_rows1", [1, 6 * 512], F32))

            for qq in range(NQ):
                bq = (qq % 2) * ND * 512
                kb.guard_wait("sync", ("xbuf", qq % 2))
                vx = kb.dma("sync",
                            xbuf[:, bq:bq + ND * 512].rearrange("p (m t) -> p m t", t=512),
                            hap(x_fm, 512 * qq, [[T, 128], [128 * T, ND], [1, 512]]),
                            f"x{qq % 2}")
                kb.dwait("vector", vx)
                kb.dwait("scalar", vx)

                def src(m, bq=bq):
                    return xbuf[:, bq + 512 * m:bq + 512 * (m + 1)]

                def out(m, qq=qq):
                    return h_chunk(m)[:, 512 * qq:512 * (qq + 1)]

                v_done = ln_pass(src, out, "ca1", "sa", "nca1")
                kb.guard_set(("xbuf", qq % 2), "V", v_done)
                kb.tick5w(4)

            if "dmp_h" in kb.dump:
                kb.wait("sync", "V", kb.cnt["V"])
                for m in range(ND):
                    kb.dma("sync", dumps["dmp_h"][128 * m:128 * (m + 1), :], h_chunk(m), "dump")

        def p2(E2):
            wbuf = E2(nc.sbuf_tensor("wbuf", [128, 2 * ND * 512], BF16))
            hw1 = E2(nc.sbuf_tensor("hw1", [17, T], BF16))
            wgk2a = E2(nc.sbuf_tensor("wgk2a", [17, D], BF16))
            w1b = E2(nc.sbuf_tensor("w1b", [128, ND * 16], BF16))

            def stream_proj(w_hbm, kind, sp_dst, scale_q=1.0, silu_evac=False):
                for quarter in range(4):
                    bq = (quarter % 2) * ND * 512
                    kb.guard_wait("gpsimd", ("wbuf", quarter % 2))
                    vw = kb.dma("gpsimd",
                                wbuf[:, bq:bq + ND * 512].rearrange("p (k c) -> p k c", c=512),
                                hap(w_hbm, 512 * quarter, [[D, 128], [128 * D, ND], [1, 512]]),
                                f"w{quarter % 2}")
                    kb.dwait("tensor", vw)
                    outer = range(4) if kind == "fm" else range(NT)
                    for oidx in outer:
                        inner = range(NQ) if kind == "fm" else (0,)
                        for qq in inner:
                            it = (oidx * NQ + qq) if kind == "fm" else oidx
                            bank = it % 2
                            sslot = it % 4
                            kb.guard_wait("tensor", ("bank", bank))
                            for k in range(ND):
                                if kind == "fm":
                                    lhsT = wbuf[:, bq + 512 * k + 128 * oidx:
                                                bq + 512 * k + 128 * (oidx + 1)]
                                    rhs = h_chunk(k)[:, 512 * qq:512 * (qq + 1)]
                                else:
                                    lhsT = h_chunk(k)[:, 128 * oidx:128 * (oidx + 1)]
                                    rhs = wbuf[:, bq + 512 * k:bq + 512 * (k + 1)]
                                d("tensor", lambda e, lhsT=lhsT, rhs=rhs, k=k, bank=bank: e.matmul(
                                    ps[bank][:, 0:512], lhsT, rhs,
                                    start=(k == 0), stop=(k == ND - 1)),
                                    "T" if k == ND - 1 else None)
                            vT = kb.cnt["T"]
                            dst = stg[:, 512 * sslot:512 * (sslot + 1)]
                            if silu_evac:
                                kb.wait("scalar", "T", vT)
                                kb.guard_wait("scalar", ("stg", sslot))
                                d("scalar", lambda e, bank=bank, dst=dst: e.activation(
                                    dst, ps[bank][:, 0:512], AF.Silu,
                                    bias=0.0, scale=1.0), "A")
                                sem_e, vE = "A", kb.cnt["A"]
                            else:
                                kb.wait("vector", "T", vT)
                                kb.guard_wait("vector", ("stg", sslot))
                                d("vector", lambda e, bank=bank, dst=dst, sc=scale_q: e.tensor_scalar(
                                    dst, ps[bank][:, 0:512], sc, None, OP.mult), "V")
                                sem_e, vE = "V", kb.cnt["V"]
                            kb.guard_set(("bank", bank), sem_e, vE)
                            kb.wait("sync", sem_e, vE)
                            if kind == "fm":
                                m_glob = quarter * 4 + oidx
                                vout = kb.dma("sync",
                                              hap(sp_dst, (4 * qq) * (ND * 128 * 128) + m_glob * (128 * 128),
                                                  [[128, 128], [ND * 128 * 128, 4], [1, 128]]),
                                              dst.rearrange("p (c t) -> p c t", t=128),
                                              f"stg{sslot}")
                            else:
                                vout = kb.dma("sync",
                                              hap(sp_dst, (128 * oidx) * D + 512 * quarter,
                                                  [[D, 128], [1, 512]]),
                                              dst, f"stg{sslot}")
                            kb.guard_set(("stg", sslot), *vout)
                        if (it % 4) == 3:
                            kb.tick5w(1)
                        if getattr(kb, "tick2", None) is not None:
                            kb.tick2(2)
                    kb.guard_set(("wbuf", quarter % 2), "T", kb.cnt["T"])

            stream_proj(wq_h, "fm", sp_q, scale_q=SCALE)
            stream_proj(wk_h, "fm", sp_k)
            stream_proj(wg_h, "fm", sp_g, silu_evac=True)

            vw1 = kb.dma("gpsimd", w1b[:, :].rearrange("p (k c) -> p k c", c=16),
                         hap(wgk1, 0, [[16, 128], [128 * 16, ND], [1, 16]]), "gk1")
            vk2 = kb.dma("gpsimd", wgk2a[0:16, :], wgk2[:, :], "gk2")
            vb2 = kb.dma("gpsimd", wgk2a[16:17, :], bgk2[:, :], "gk3")
            for _sl in range(4):
                kb.guard_wait("vector", ("stg", _sl))
            d("vector", lambda e: e.memset(stg[0:1, 0:T], 1.0), "V")
            kb.wait("sync", "V", kb.cnt["V"])
            v_hw1ones = kb.dma("sync", hw1[16:17, :], stg[0:1, 0:T], "hw1o")
            for _sl in range(4):
                kb.guard_set(("stg", _sl), *v_hw1ones)
            kb.dwait("tensor", v_hw1ones)
            kb.dwait("tensor", vw1)
            for qq in range(NQ):
                bank = qq % 2
                kb.guard_wait("tensor", ("bank", bank))
                for k in range(ND):
                    d("tensor", lambda e, k=k, qq=qq, bank=bank: e.matmul(
                        ps[bank][0:16, 0:512], w1b[:, 16 * k:16 * (k + 1)],
                        h_chunk(k)[:, 512 * qq:512 * (qq + 1)],
                        start=(k == 0), stop=(k == ND - 1)),
                        "T" if k == ND - 1 else None)
                kb.wait("vector", "T", kb.cnt["T"])
                vj = d("vector", lambda e, qq=qq, bank=bank: e.tensor_copy(
                    hw1[0:16, 512 * qq:512 * (qq + 1)], ps[bank][0:16, 0:512]), "V")
                kb.guard_set(("bank", bank), "V", vj)
            v_hw1done = kb.cnt["V"]

            def gk2_gen():
                yield
                for it in range(64):
                    tt, j = divmod(it, 4)
                    bank = 2 + (it % 2)
                    fs = it % 2
                    if it == 0:
                        kb.dwait("tensor", vb2)
                        kb.dwait("tensor", vk2)
                        kb.wait("tensor", "V", v_hw1done)
                    kb.guard_wait("tensor", ("bank", bank))
                    d("tensor", lambda e, tt=tt, j=j, bank=bank: e.matmul(
                        ps[bank][:, 0:512], hw1[:, 128 * tt:128 * (tt + 1)],
                        wgk2a[:, 512 * j:512 * (j + 1)], start=True, stop=True), "T")
                    vT = kb.cnt["T"]
                    kb.wait("scalar", "T", vT)
                    kb.guard_wait("scalar", ("stgf", fs))
                    d("scalar", lambda e, bank=bank, fs=fs: e.activation(
                        stg_f[:, 512 * fs:512 * (fs + 1)], ps[bank][:, 0:512],
                        AF.Exp, bias=0.0, scale=-1.0), "A")
                    va1 = kb.cnt["A"]
                    kb.guard_set(("bank", bank), "A", va1)
                    kb.guard_wait("scalar", ("lnx", fs))
                    d("scalar", lambda e, fs=fs: e.activation(
                        ln_xbf[:, 512 * fs:512 * (fs + 1)],
                        stg_f[:, 512 * fs:512 * (fs + 1)],
                        AF.Ln, bias=1.0, scale=1.0), "A")
                    vA = kb.cnt["A"]
                    kb.guard_set(("stgf", fs), "A", vA)
                    kb.wait("sync", "A", vA)
                    vout = kb.dma("sync",
                                  hap(sp_sp, (128 * tt) * D + 512 * j, [[D, 128], [1, 512]]),
                                  ln_xbf[:, 512 * fs:512 * (fs + 1)], f"sp2{fs}")
                    kb.guard_set(("lnx", fs), *vout)
                    yield

            g2k = gk2_gen()
            next(g2k)

            def tick2(n=1):
                for _ in range(n):
                    if next(g2k, "END") == "END":
                        return
            kb.tick2 = tick2
            stream_proj(wv_h, "tm", sp_v)
            stream_proj(wk_h, "tm", sp_ktm)
            kb.tick2 = None
            for _ in range(64):
                if next(g2k, "END") == "END":
                    break

        def p3(E3):
            qc = E3(nc.sbuf_tensor("qc", [128, 2 * D], BF16))
            kcb = E3(nc.sbuf_tensor("kcb", [128, 2 * D], BF16))
            vcb = E3(nc.sbuf_tensor("vcb", [128, 2 * D], BF16))
            ktc = E3(nc.sbuf_tensor("ktc", [128, 2 * D], BF16))
            spc = E3(nc.sbuf_tensor("spc", [128, 2 * D], BF16))
            en_tm = E3(nc.sbuf_tensor("en_tm", [128, 2 * D], BF16))
            spcum = E3(nc.sbuf_tensor("spcum", [128, 2 * D], BF16))
            epos_b = E3(nc.sbuf_tensor("epos_b", [128, 2 * D], BF16))
            eneg_b = E3(nc.sbuf_tensor("eneg_b", [128, 2 * D], BF16))
            ecols_b = E3(nc.sbuf_tensor("ecols_b", [128, 2 * HN], F32))
            qg_b = E3(nc.sbuf_tensor("qg_b", [128, 2 * D], BF16))
            s_tmp = E3(nc.sbuf_tensor("s_tmp", [128, 512], F32))
            kh_b = E3(nc.sbuf_tensor("kh_b", [128, 2 * D], BF16))
            kt_b = E3(nc.sbuf_tensor("kt_b", [128, 2 * D], BF16))
            a_bf = E3(nc.sbuf_tensor("a_bf", [128, 2 * 512], BF16))
            s_f32 = E3(nc.sbuf_tensor("s_f32", [128, HN * 128], F32))
            s_bf = E3(nc.sbuf_tensor("s_bf", [128, HN * 128], BF16))
            o_seg = E3(nc.sbuf_tensor("o_seg", [128, HN * 512], F32))
            g_seg = E3(nc.sbuf_tensor("g_seg", [128, HN * 512], BF16))
            o2_bf = E3(nc.sbuf_tensor("o2_bf", [128, 2 * 512], BF16))
            fe_rows = E3(nc.sbuf_tensor("fe_rows", [16, 2 * 512], F32))
            f_bf = E3(nc.sbuf_tensor("f_bf", [16, 512], BF16))
            t_tmp = E3(nc.sbuf_tensor("t_tmp", [128, 2 * 512], F32))
            op_stg = E3(nc.sbuf_tensor("op_stg", [128, 2 * 512], BF16))

            CH_STRIDE = ND * 128 * 128
            LD = {}
            FR = {}

            def loads(cc):
                sl = cc % 2
                kb.guard_wait("sync", ("qbuf", sl))
                v_q = kb.dma("sync", qc[:, sl * D:(sl + 1) * D].rearrange("p (m t) -> p m t", t=128),
                             hap(sp_q, cc * CH_STRIDE, [[128, 128], [128 * 128, ND], [1, 128]]),
                             f"q{sl}")
                kb.guard_wait("sync", ("kbuf", sl))
                v_k = kb.dma("sync", kcb[:, sl * D:(sl + 1) * D].rearrange("p (m t) -> p m t", t=128),
                             hap(sp_k, cc * CH_STRIDE, [[128, 128], [128 * 128, ND], [1, 128]]),
                             f"kk{sl}")
                kb.guard_wait("sync", ("vbuf", sl))
                v_v = kb.dma("sync", vcb[:, sl * D:(sl + 1) * D],
                             hap(sp_v, (128 * cc) * D, [[D, 128], [1, D]]), f"v{sl}")
                kb.guard_wait("sync", ("ktbuf", sl))
                v_kt = kb.dma("sync", ktc[:, sl * D:(sl + 1) * D],
                              hap(sp_ktm, (128 * cc) * D, [[D, 128], [1, D]]), f"kt{sl}")
                kb.guard_wait("sync", ("spbuf", sl))
                v_sp = kb.dma("sync", spc[:, sl * D:(sl + 1) * D],
                              hap(sp_sp, (128 * cc) * D, [[D, 128], [1, D]]), f"sp{sl}")
                LD[cc] = dict(q=v_q, k=v_k, v=v_v, kt=v_kt, sp=v_sp)

            def load_gseg(seg):
                kb.guard_wait("sync", ("gseg",))
                for hh2 in range(HN):
                    v_g = kb.dma("sync",
                                 g_seg[:, 512 * hh2:512 * (hh2 + 1)]
                                 .rearrange("p (c t) -> p c t", t=128),
                                 hap(sp_g, (4 * seg) * CH_STRIDE + hh2 * (128 * 128),
                                     [[128, 128], [CH_STRIDE, 4], [1, 128]]),
                                 "gseg")
                kb.guards[("gseg_ld", seg)] = [v_g]

            def make_fillers(cc):
                sl = cc % 2
                FR[cc] = {}

                def u_piece(j4):
                    if j4 == 0:
                        kb.dwait("tensor", LD[cc]["sp"])
                    kb.guard_wait("tensor", ("bank", 5))
                    vT = d("tensor", lambda e, sl=sl, j4=j4: e.matmul(
                        ps[5][:, 0:512], U_bf[:, :],
                        spc[:, sl * D + 512 * j4:sl * D + 512 * (j4 + 1)],
                        start=True, stop=True), "T")
                    kb.wait("scalar", "T", vT)
                    if j4 == 0:
                        kb.guard_wait("scalar", ("entm", sl))
                        kb.guard_wait("scalar", ("spcum", sl))
                    d("scalar", lambda e, sl=sl, j4=j4: e.activation(
                        en_tm[:, sl * D + 512 * j4:sl * D + 512 * (j4 + 1)],
                        ps[5][:, 0:512], AF.Exp, bias=0.0, scale=1.0 / GLN), "A")
                    va = d("scalar", lambda e, sl=sl, j4=j4: e.activation(
                        spcum[:, sl * D + 512 * j4:sl * D + 512 * (j4 + 1)],
                        ps[5][:, 0:512], AF.Identity, bias=0.0, scale=1.0), "A")
                    kb.guard_set(("bank", 5), "A", va)
                    if j4 == 3:
                        kb.guard_set(("spbuf", sl), "T", kb.cnt["T"])
                        FR[cc]["spcum"] = va
                        FR[cc]["entm"] = va

                def tr_piece(g):
                    pb = psb[g]
                    kb.wait("tensor", "A", FR[cc]["spcum"])
                    kb.guard_wait("tensor", ("psb", g))
                    for h8 in range(8):
                        hh = g * 8 + h8
                        d("tensor", lambda e, sl=sl, hh=hh, h8=h8, pb=pb: e.transpose(
                            pb[:, 128 * h8:128 * (h8 + 1)],
                            spcum[:, sl * D + 128 * hh:sl * D + 128 * (hh + 1)],
                            identB[:, :]),
                            "T" if h8 == 7 else None)
                    vtr = kb.cnt["T"]
                    kb.wait("scalar", "T", vtr)
                    kb.guard_wait("scalar", ("eall", g, sl))
                    d("scalar", lambda e, g=g, pb=pb, sl=sl: e.activation(
                        epos_b[:, sl * D + 1024 * g:sl * D + 1024 * (g + 1)], pb[:, 0:1024],
                        AF.Exp, bias=0.0, scale=-1.0 / GLN), "A")
                    vexp = d("scalar", lambda e, g=g, pb=pb, sl=sl: e.activation(
                        eneg_b[:, sl * D + 1024 * g:sl * D + 1024 * (g + 1)], pb[:, 0:1024],
                        AF.Exp, bias=0.0, scale=1.0 / GLN), "A")
                    kb.guard_set(("psb", g), "A", vexp)
                    if g == 1:
                        kb.guard_set(("spcum", sl), "T", vtr)
                        FR[cc]["exps"] = vexp

                def ec_qg():
                    kb.wait("vector", "A", FR[cc]["exps"])
                    kb.guard_wait("vector", ("ecols", sl))
                    d("vector", lambda e, sl=sl: e.tensor_copy(
                        ecols_b[:, sl * HN:(sl + 1) * HN],
                        hap(epos_b, sl * D + 127, [[2 * D, 128], [128, HN]])), "V")
                    kb.dwait("vector", LD[cc]["q"])
                    kb.guard_wait("vector", ("qgall", sl))
                    d("vector", lambda e, sl=sl: e.tensor_tensor(
                        qg_b[:, sl * D:(sl + 1) * D], qc[:, sl * D:(sl + 1) * D],
                        epos_b[:, sl * D:(sl + 1) * D], OP.mult), "V")
                    kb.guard_set(("qbuf", sl), "V", kb.cnt["V"])

                def kh_f():
                    kb.dwait("vector", LD[cc]["k"])
                    kb.guard_wait("vector", ("khall", sl))
                    d("vector", lambda e, sl=sl: e.tensor_tensor(
                        kh_b[:, sl * D:(sl + 1) * D], kcb[:, sl * D:(sl + 1) * D],
                        eneg_b[:, sl * D:(sl + 1) * D], OP.mult), "V")
                    FR[cc]["gates"] = kb.cnt["V"]
                    kb.guard_set(("kbuf", sl), "V", FR[cc]["gates"])
                    kb.guard_set(("eall", 0, sl), "V", FR[cc]["gates"])
                    kb.guard_set(("eall", 1, sl), "V", FR[cc]["gates"])

                def kt_f():
                    kb.dwait("vector", LD[cc]["kt"])
                    kb.wait("vector", "A", FR[cc]["entm"])
                    kb.guard_wait("vector", ("ktall", sl))
                    v_ktall = d("vector", lambda e, sl=sl: e.tensor_tensor(
                        kt_b[:, sl * D:(sl + 1) * D], ktc[:, sl * D:(sl + 1) * D],
                        en_tm[:, sl * D:(sl + 1) * D], OP.mult), "V")
                    FR[cc]["ktall"] = v_ktall
                    kb.guard_set(("entm", sl), "V", v_ktall)
                    kb.guard_set(("ktbuf", sl), "V", v_ktall)

                return [lambda j4=j4: u_piece(j4) for j4 in range(4)] + \
                       [lambda: tr_piece(0), lambda: tr_piece(1),
                        ec_qg, kh_f, kt_f]

            def pass2(cc, fillers):
                sl = cc % 2
                fillers = list(fillers)

                def fill(n=1):
                    for _ in range(n):
                        if fillers:
                            fillers.pop(0)()

                def s_evac(hg):
                    vsmm = kb.cnt["T"]
                    kb.wait("vector", "T", vsmm)
                    for h4 in range(4):
                        kb.guard_wait("vector", ("sf32", hg * 4 + h4))
                    if cc > 0:
                        d("vector", lambda e, hg=hg: e.tensor_tensor(
                            s_tmp[:, :], ps[4][:, 0:512],
                            s_f32[:, 512 * hg:512 * (hg + 1)], OP.add), "V")
                        for h4 in range(4):
                            hh = hg * 4 + h4
                            d("vector", lambda e, hh=hh, h4=h4, sl=sl: e.tensor_scalar(
                                s_f32[:, 128 * hh:128 * (hh + 1)],
                                s_tmp[:, 128 * h4:128 * (h4 + 1)],
                                ecols_b[:, sl * HN + hh:sl * HN + hh + 1], None, OP.mult), "V")
                    else:
                        for h4 in range(4):
                            hh = hg * 4 + h4
                            d("vector", lambda e, hh=hh, h4=h4, sl=sl: e.tensor_scalar(
                                s_f32[:, 128 * hh:128 * (hh + 1)],
                                ps[4][:, 128 * h4:128 * (h4 + 1)],
                                ecols_b[:, sl * HN + hh:sl * HN + hh + 1], None, OP.mult), "V")
                    vsev = kb.cnt["V"]
                    kb.guard_set(("bank", 4), "V", vsev)
                    kb.wait("gpsimd", "V", vsev)
                    vcast = d("gpsimd", lambda e, hg=hg: e.tensor_copy(
                        s_bf[:, 512 * hg:512 * (hg + 1)],
                        s_f32[:, 512 * hg:512 * (hg + 1)]), "G")
                    for h4 in range(4):
                        kb.guard_set(("S", hg * 4 + h4), "G", vcast)
                        kb.guards[("sf32", hg * 4 + h4)] = [("G", vcast)]

                kb.wait("tensor", "V", FR[cc]["gates"])
                kb.wait("tensor", "V", FR[cc]["ktall"])
                kb.dwait("tensor", LD[cc]["v"])

                def a_mms(hg):
                    abank = hg % 2
                    kb.guard_wait("tensor", ("bank", abank))
                    for h4 in range(4):
                        hof = 128 * (hg * 4 + h4)
                        d("tensor", lambda e, hof=hof, abank=abank, h4=h4, sl=sl: e.matmul(
                            ps[abank][:, 128 * h4:128 * (h4 + 1)],
                            kh_b[:, sl * D + hof:sl * D + hof + 128],
                            qg_b[:, sl * D + hof:sl * D + hof + 128],
                            start=True, stop=True),
                            "T" if h4 == 3 else None)
                    return kb.cnt["T"]

                vamm_d = {0: a_mms(0), 1: a_mms(1)}
                prev_sev = None
                for hg in range(4):
                    abank = hg % 2
                    ob = 2 + hg % 2
                    vamm = vamm_d[hg]
                    kb.wait("vector", "T", vamm)
                    kb.guard_wait("vector", ("abf", abank))
                    vabf = d("vector", lambda e, hg=hg, abank=abank: e.tensor_tensor(
                        a_bf[:, 512 * (hg % 2):512 * (hg % 2 + 1)],
                        ps[abank][:, 0:512], U4[:, :], OP.mult), "V")
                    kb.guard_set(("bank", abank), "V", vabf)
                    if prev_sev is not None:
                        prev_sev()
                        prev_sev = None
                    fill()

                    kb.wait("tensor", "V", vabf)
                    kb.guard_wait("tensor", ("bank", ob))
                    for h4 in range(4):
                        hh = hg * 4 + h4
                        hof = 128 * hh
                        if cc > 0:
                            kb.guard_wait("tensor", ("S", hh))
                            d("tensor", lambda e, ob=ob, hof=hof, h4=h4, sl=sl: e.matmul(
                                ps[ob][:, 128 * h4:128 * (h4 + 1)], s_bf[:, hof:hof + 128],
                                qg_b[:, sl * D + hof:sl * D + hof + 128],
                                start=True, stop=False), None)
                        d("tensor", lambda e, ob=ob, sl=sl, hof=hof, hg=hg, h4=h4, cc=cc: e.matmul(
                            ps[ob][:, 128 * h4:128 * (h4 + 1)],
                            vcb[:, sl * D + hof:sl * D + hof + 128],
                            a_bf[:, 512 * (hg % 2) + 128 * h4:512 * (hg % 2) + 128 * (h4 + 1)],
                            start=(cc == 0), stop=True),
                            "T" if h4 == 3 else None)
                    vomm = kb.cnt["T"]
                    kb.guard_set(("abf", abank), "T", vomm)
                    kb.wait("scalar", "T", vomm)
                    for h4 in range(4):
                        kb.guard_wait("scalar", ("oseg", hg * 4 + h4, cc % 4))
                    voev = d("scalar", lambda e, ob=ob, hg=hg, cc=cc: e.activation(
                        hap(o_seg, 512 * (4 * hg) + 128 * (cc % 4),
                            [[HN * 512, 128], [512, 4], [1, 128]]),
                        ps[ob][:, 0:512], AF.Identity, bias=0.0, scale=1.0), "A")
                    kb.guard_set(("bank", ob), "A", voev)
                    for h4 in range(4):
                        kb.guards[("olast", hg * 4 + h4)] = [("A", voev)]
                    fill()

                    kb.guard_wait("tensor", ("bank", 4))
                    for h4 in range(4):
                        hof = 128 * (hg * 4 + h4)
                        d("tensor", lambda e, sl=sl, hof=hof, h4=h4: e.matmul(
                            ps[4][:, 128 * h4:128 * (h4 + 1)],
                            kt_b[:, sl * D + hof:sl * D + hof + 128],
                            vcb[:, sl * D + hof:sl * D + hof + 128],
                            start=True, stop=True),
                            "T" if h4 == 3 else None)
                    if hg + 2 < 4:
                        vamm_d[hg + 2] = a_mms(hg + 2)
                    prev_sev = (lambda hg=hg: s_evac(hg))
                prev_sev()
                fill(len(fillers))
                vT_end = kb.cnt["T"]
                kb.guard_set(("vbuf", sl), "T", vT_end)
                kb.guard_set(("ktall", sl), "T", vT_end)
                kb.guard_set(("qgall", sl), "T", vT_end)
                kb.guard_set(("khall", sl), "T", vT_end)
                kb.guard_set(("ecols", sl), "V", kb.cnt["V"])

            def tail(cc):
                sl = cc % 2
                seg = cc // 4
                kb.dwait("scalar", kb.guards[("gseg_ld", seg)][0])
                kb.dwait("vector", kb.guards[("gseg_ld", seg)][0])
                kb.guard_wait("tensor", ("bank", 4))
                for hh in range(HN):
                    hp = hh % 2
                    kb.wait("scalar", "A", kb.guards[("olast", hh)][0][1])
                    kb.guard_wait("scalar", ("o2", hp))
                    vsq = d("scalar", lambda e, hh=hh, hp=hp: e.activation(
                        o2_bf[:, 512 * hp:512 * (hp + 1)],
                        o_seg[:, 512 * hh:512 * (hh + 1)],
                        AF.Square, bias=0.0, scale=1.0), "A")
                    kb.wait("tensor", "A", vsq)
                    vmq = d("tensor", lambda e, hh=hh, hp=hp: e.matmul(
                        ps[4][0:16, 0:512], cg[:, 16 * hh:16 * (hh + 1)],
                        o2_bf[:, 512 * hp:512 * (hp + 1)],
                        start=(hh == 0), stop=(hh == HN - 1)), "T")
                    kb.guard_set(("o2", hp), "T", vmq)
                vmq = kb.cnt["T"]
                kb.wait("vector", "T", vmq)
                kb.guard_wait("vector", ("frows",))
                d("vector", lambda e: e.tensor_scalar(
                    fe_rows[:, 0:512], ps[4][0:16, 0:512], EPS, None, OP.add), "V")
                vrc = d("vector", lambda e: e.reciprocal(
                    fe_rows[:, 512:1024], fe_rows[:, 0:512]), "V")
                kb.guard_set(("bank", 4), "V", vrc)
                kb.wait("scalar", "V", vrc)
                kb.guard_wait("scalar", ("fbf",))
                vfr = d("scalar", lambda e: e.activation(
                    f_bf[:, :], fe_rows[:, 512:1024], AF.Sqrt,
                    bias=0.0, scale=1.0), "A")
                kb.guards[("frows",)] = [("A", vfr)]
                kb.wait("tensor", "A", vfr)
                for hh in range(HN):
                    hp = hh % 2
                    b2b = hp
                    kb.guard_wait("tensor", ("bank", b2b))
                    vb2m = d("tensor", lambda e, hh=hh, b2b=b2b: e.matmul(
                        ps[b2b][:, 0:512], gnwsel[:, 128 * hh:128 * (hh + 1)],
                        f_bf[:, :], start=True, stop=True), "T")
                    kb.guards[("fbf",)] = [("T", vb2m)]
                    kb.wait("vector", "T", vb2m)
                    kb.guard_wait("vector", ("ttmp", hp))
                    d("vector", lambda e, hh=hh, hp=hp, b2b=b2b: e.tensor_tensor(
                        t_tmp[:, 512 * hp:512 * (hp + 1)],
                        o_seg[:, 512 * hh:512 * (hh + 1)],
                        ps[b2b][:, 0:512], OP.mult), "V")
                    kb.guard_wait("vector", ("opstg", hp))
                    vop = d("vector", lambda e, hh=hh, hp=hp: e.tensor_tensor(
                        op_stg[:, 512 * hp:512 * (hp + 1)],
                        t_tmp[:, 512 * hp:512 * (hp + 1)],
                        g_seg[:, 512 * hh:512 * (hh + 1)], OP.mult), "V")
                    kb.guard_set(("bank", b2b), "V", vop)
                    kb.guard_set(("ttmp", hp), "V", vop)
                    for j in range(4):
                        kb.guard_set(("oseg", hh, j), "V", vop)
                    kb.wait("sync", "V", vop)
                    vsp = kb.dma("sync",
                                 hap(sp_op, (128 * hh) * T + 512 * seg,
                                     [[T, 128], [1, 512]]),
                                 op_stg[:, 512 * hp:512 * (hp + 1)], f"op{hp}")
                    kb.guard_set(("opstg", hp), *vsp)
                    if "dmp_o" in kb.dump:
                        vdo = kb.dma("sync",
                                     hap(dumps["dmp_o"], (128 * hh) * T + 512 * seg,
                                         [[T, 128], [1, 512]]),
                                     o_seg[:, 512 * hh:512 * (hh + 1)], f"do{hh}")
                        for j in range(4):
                            kb.guard_add(("oseg", hh, j), *vdo)
                kb.guard_set(("gseg",), "V", kb.cnt["V"])

            load_gseg(0)
            loads(0)
            for f in make_fillers(0):
                f()
            for cc in range(p3_chunks):
                nxt = cc + 1
                fillers = []
                if nxt < p3_chunks:
                    loads(nxt)
                    fillers = make_fillers(nxt)
                pass2(cc, fillers)
                if cc % 4 == 3:
                    tail(cc)
                if nxt < p3_chunks and nxt % 4 == 0:
                    load_gseg(nxt // 4)

        def p4(E4):
            wobuf = E4(nc.sbuf_tensor("wobuf", [128, ND * D], BF16))
            statbuf[0] = E4(nc.sbuf_tensor("stat_rows4", [1, 6 * 512], F32))
            opb = E4(nc.sbuf_tensor("opb", [128, 2 * ND * 512], BF16))
            xbuf4 = E4(nc.sbuf_tensor("xbuf4", [128, ND * 512], F32))
            x1q = E4(nc.sbuf_tensor("x1q", [128, ND * 512], F32))

            vwo = None
            for quarter in range(4):
                vwo = kb.dma("gpsimd",
                             wobuf[:, :].rearrange("p (k c) -> p k c", c=D)
                             [:, :, 512 * quarter:512 * (quarter + 1)],
                             hap(wo_h, 512 * quarter, [[D, 128], [128 * D, ND], [1, 512]]),
                             f"wo{quarter}")
                kb.dwait("tensor", vwo)

            for qq in range(NQ):
                osl = qq % 2
                kb.guard_wait("sync", ("opb", osl))
                vop = kb.dma("sync",
                             opb[:, osl * ND * 512:(osl + 1) * ND * 512]
                             .rearrange("p (k t) -> p k t", t=512),
                             hap(sp_op, 512 * qq, [[T, 128], [128 * T, ND], [1, 512]]),
                             f"ob{osl}")
                kb.guard_wait("sync", ("xbuf4",))
                vx = kb.dma("sync",
                            xbuf4[:, :].rearrange("p (m t) -> p m t", t=512),
                            hap(x_fm, 512 * qq, [[T, 128], [128 * T, ND], [1, 512]]),
                            "x4")
                kb.dwait("tensor", vop)
                kb.dwait("vector", vx)
                for m in range(ND):
                    bank = m % 2
                    kb.guard_wait("tensor", ("bank", bank))
                    for k in range(ND):
                        d("tensor", lambda e, m=m, k=k, bank=bank, osl=osl: e.matmul(
                            ps[bank][:, 0:512],
                            wobuf[:, k * D + 128 * m:k * D + 128 * (m + 1)],
                            opb[:, osl * ND * 512 + 512 * k:osl * ND * 512 + 512 * (k + 1)],
                            start=(k == 0), stop=(k == ND - 1)),
                            "T" if k == ND - 1 else None)
                    vT = kb.cnt["T"]
                    kb.wait("vector", "T", vT)
                    kb.guard_wait("vector", ("x1q", m % 4))
                    vx1 = d("vector", lambda e, m=m, bank=bank: e.scalar_tensor_tensor(
                        x1q[:, 512 * m:512 * (m + 1)], ps[bank][:, 0:512],
                        dcol("ga2", m), xbuf4[:, 512 * m:512 * (m + 1)],
                        OP.mult, OP.add), "V")
                    kb.guard_set(("bank", bank), "V", vx1)
                    kb.wait("sync", "V", vx1)
                    vx1s = kb.dma("sync",
                                  hap(sp_x1, (128 * m) * T + 512 * qq, [[T, 128], [1, 512]]),
                                  x1q[:, 512 * m:512 * (m + 1)], f"x1s{m % 4}")
                kb.guard_set(("opb", osl), "T", kb.cnt["T"])
                kb.guard_set(("xbuf4",), "V", kb.cnt["V"])
                for j in range(4):
                    kb.guard_set(("x1q", j), "D_x1s" + str(j), kb.cnt.get("D_x1s" + str(j), 0))

                def src(m):
                    return x1q[:, 512 * m:512 * (m + 1)]

                mm2_done = {}
                xbf, x2bf, t1, t4 = ln_xbf, ln_x2bf, ln_t1, ln_t4
                for m in range(ND):
                    pb = (m % 2) * 512
                    if m >= 2:
                        kb.wait("vector", "T", mm2_done[m - 2])
                        kb.wait("scalar", "T", mm2_done[m - 2])
                    d("vector", lambda e, m=m, pb=pb: e.tensor_copy(
                        xbf[:, pb:pb + 512], src(m)), "V")
                    vv = kb.cnt["V"]
                    kb.wait("scalar", "V", vv)
                    d("scalar", lambda e, pb=pb: e.activation(
                        x2bf[:, pb:pb + 512], xbf[:, pb:pb + 512], AF.Square,
                        bias=0.0, scale=1.0), "A")
                    va = kb.cnt["A"]
                    kb.wait("tensor", "V", vv)
                    if m == 0:
                        kb.guard_wait("tensor", ("bank", 0))
                        kb.guard_wait("tensor", ("bank", 1))
                    d("tensor", lambda e, pb=pb, m=m: e.matmul(
                        ps[0][0:1, 0:512], sumcolF[:, :], xbf[:, pb:pb + 512],
                        start=(m == 0), stop=(m == ND - 1)), None)
                    kb.wait("tensor", "A", va)
                    d("tensor", lambda e, pb=pb, m=m: e.matmul(
                        ps[1][0:1, 0:512], sumcolF[:, :], x2bf[:, pb:pb + 512],
                        start=(m == 0), stop=(m == ND - 1)), "T")
                    mm2_done[m] = kb.cnt["T"]
                v_stats = kb.cnt["T"]
                kb.wait("vector", "T", v_stats)
                R = statbuf[0]
                d("vector", lambda e: e.tensor_copy(R[:, 0:512], ps[0][0:1, 0:512]), None)
                d("vector", lambda e: e.tensor_tensor(R[:, 512:1024], R[:, 0:512],
                                                      R[:, 0:512], OP.mult), None)
                d("vector", lambda e: e.scalar_tensor_tensor(
                    R[:, 1024:1536], ps[1][0:1, 0:512], EPS, R[:, 512:1024],
                    OP.add, OP.subtract), None)
                d("vector", lambda e: e.reciprocal(R[:, 1536:2048], R[:, 1024:1536]), "V")
                v_free01 = kb.cnt["V"]
                kb.guard_set(("bank", 0), "V", v_free01)
                kb.guard_set(("bank", 1), "V", v_free01)
                kb.wait("scalar", "V", v_free01)
                d("scalar", lambda e: e.activation(R[:, 2048:2560], R[:, 1536:2048],
                                                   AF.Sqrt, bias=0.0, scale=1.0), "A")
                kb.wait("vector", "A", kb.cnt["A"])
                d("vector", lambda e: e.tensor_tensor(R[:, 2560:3072], R[:, 0:512],
                                                      R[:, 2048:2560], OP.mult), "V")
                v_rows = kb.cnt["V"]
                kb.wait("tensor", "V", v_rows)
                kb.guard_wait("tensor", ("bank", 2))
                kb.guard_wait("tensor", ("bank", 3))
                d("tensor", lambda e: e.matmul(ps[2][:, 0:512], onesrow[:, 0:128],
                                               R[:, 2048:2560], start=True, stop=True), None)
                vbc = d("tensor", lambda e: e.matmul(ps[3][:, 0:512], onesrow[:, 0:128],
                                                     R[:, 2560:3072], start=True, stop=True), "T")
                kb.wait("vector", "T", vbc)
                va_prev = None
                for m in range(ND):
                    sslot = m % 4
                    d("vector", lambda e, m=m: e.scalar_tensor_tensor(
                        t1[:, :], src(m), dcol("cb1", m), ps[2][:, 0:512],
                        OP.mult, OP.mult), None)
                    if va_prev is not None:
                        kb.wait("vector", "A", va_prev)
                    d("vector", lambda e, m=m: e.scalar_tensor_tensor(
                        t4[:, :], ps[3][:, 0:512], dcol("ncb1", m), t1[:, :],
                        OP.mult, OP.add), "V")
                    kb.wait("scalar", "V", kb.cnt["V"])
                    kb.guard_wait("scalar", ("stg", sslot))
                    va_prev = d("scalar", lambda e, m=m, sslot=sslot: e.activation(
                        stg[:, 512 * sslot:512 * (sslot + 1)], t4[:, :],
                        AF.Identity, bias=dcol("sb", m), scale=1.0), "A")
                    kb.wait("sync", "A", va_prev)
                    vsp2 = kb.dma("sync",
                                  hap(sp_h2, (128 * m) * T + 512 * qq, [[T, 128], [1, 512]]),
                                  stg[:, 512 * sslot:512 * (sslot + 1)], f"stg{sslot}")
                    kb.guard_set(("stg", sslot), *vsp2)
                kb.wait("vector", "A", va_prev)
                d("vector", lambda e: e.nop(), "V")
                v_done = kb.cnt["V"]
                kb.guard_set(("bank", 2), "V", v_done)
                kb.guard_set(("bank", 3), "V", v_done)
                kb.guard_set(("xbuf4",), "V", v_done)

        def p5(E5, h2qfm):
            nonlocal tf32c, i32c
            h2tm = E5(nc.sbuf_tensor("h2tm", [128, 2 * D], BF16))
            i32c = E5(nc.sbuf_tensor("i32c", [128, 1024], I32))
            tf32c = E5(nc.sbuf_tensor("tf32c", [128, 1024], F32))
            h2qs = E5(nc.sbuf_tensor("h2qs", [128, 2 * D], BF16))
            mxc = E5(nc.sbuf_tensor("mxc", [128, 2], F32))
            fb_last = {}

            def tile_loads(tt):
                sl = tt % 2
                kb.guard_wait("sync", ("h2tm", sl))
                vls = []
                for j in range(4):
                    vls.append(kb.dma("sync", h2tm[:, sl * D + 512 * j:sl * D + 512 * (j + 1)],
                                      hap(sp_h2, (512 * j) * T + 128 * tt, [[T, 512], [1, 128]]),
                                      f"xb{j}{sl}", transpose=True))
                return vls

            LD5 = {0: tile_loads(0)}
            for tt in range(NT):
                sl = tt % 2
                if tt + 1 < NT:
                    LD5[tt + 1] = tile_loads(tt + 1)
                for v_ in LD5[tt]:
                    kb.dwait("vector", v_)
                d("vector", lambda e, sl=sl: e.tensor_reduce(
                    mxc[:, sl:sl + 1], h2tm[:, sl * D:(sl + 1) * D], AX.X, OP.max,
                    apply_absolute_value=True), "V")
                d("vector", lambda e, sl=sl, tt=tt: e.tensor_scalar(
                    s_all[:, tt:tt + 1], mxc[:, sl:sl + 1], 1.0 / 127.0, EPS,
                    OP.mult, OP.add), "V")
                d("vector", lambda e, tt=tt: e.reciprocal(
                    srq[tt % 2][:, :], s_all[:, tt:tt + 1]), "V")
                kb.guard_wait("vector", ("h2qs", sl))
                for half in range(2):
                    d("vector", lambda e, sl=sl, tt=tt, half=half: e.tensor_scalar(
                        tf32c[:, :],
                        h2tm[:, sl * D + 1024 * half:sl * D + 1024 * (half + 1)],
                        srq[tt % 2][:, :], MAGIC, OP.mult, OP.add), "V")
                    d("vector", lambda e: e.tensor_scalar(
                        i32c[:, :], tf32c[:, :],
                        -MAGIC, None, OP.add), "V")
                    d("vector", lambda e, sl=sl, half=half: e.tensor_scalar(
                        h2qs[:, sl * D + 1024 * half:sl * D + 1024 * (half + 1)],
                        i32c[:, :],
                        -129.0, 129.0, OP.max, OP.min), "V")
                vq = kb.cnt["V"]
                kb.guard_set(("h2tm", sl), "V", vq)
                kb.wait("scalar", "V", vq)
                vsq = kb.dma("scalar", hap(sp_h2q, (128 * tt) * D, [[D, 128], [1, D]]),
                             h2qs[:, sl * D:(sl + 1) * D], f"h2q{sl}")
                kb.guard_set(("h2qs", sl), *vsq)
                if tt % 4 == 3:
                    j = tt // 4
                    kb.wait("sync", "D_h2q0", kb.cnt.get("D_h2q0", 0))
                    kb.wait("sync", "D_h2q1", kb.cnt.get("D_h2q1", 0))
                    for m in range(ND):
                        vfb = kb.dma("sync",
                                     h2qfm[:, m * T + 512 * j:m * T + 512 * (j + 1)],
                                     hap(sp_h2q, (512 * j) * D + 128 * m, [[D, 512], [1, 128]]),
                                     f"fbq{j}", transpose=True)
                        fb_last[m % 4] = vfb
                    kb.guards[("h2qfm", j)] = list(fb_last.values())
                kb.tick6(1)
            kb.guards[("h2qfm",)] = list(fb_last.values())

        def p5w_gen(E5w):
            wsb = E5w(nc.sbuf_tensor("wsb", [128, 2 * 4096], F32))
            acc = E5w(nc.sbuf_tensor("acc", [128, 4], F32))
            onescolF = E5w(nc.sbuf_tensor("onescolF", [128, 1], F32))
            d("vector", lambda e: e.memset(onescolF[:, :], 1.0), "V")
            d("vector", lambda e: e.memset(wsum[:, :], 0.0), "V")
            d("vector", lambda e: e.memset(ws_row[:, :], 0.0), "V")
            d("vector", lambda e: e.memset(wsr_c[:, :], 0.0), "V")
            yield
            for widx, wt in enumerate((wgT, wuT, wdT)):
                NP = (D * II) // (128 * 4096)
                for pc in range(NP):
                    sl = pc % 2
                    kb.guard_wait("scalar", ("wsb", sl))
                    vw = kb.dma("scalar", wsb[:, sl * 4096:(sl + 1) * 4096],
                                hap(wt, pc * 128 * 4096, [[4096, 128], [1, 4096]]),
                                f"ws{sl}")
                    kb.dwait("vector", vw)
                    kb.guard_wait("vector", ("acc", sl))
                    va = d("vector", lambda e, sl=sl: e.tensor_reduce(
                        acc[:, sl:sl + 1], wsb[:, sl * 4096:(sl + 1) * 4096],
                        AX.X, OP.add, apply_absolute_value=True), "V")
                    kb.guard_set(("wsb", sl), "V", va)
                    vacc = d("vector", lambda e, widx=widx, sl=sl: e.tensor_tensor(
                        wsum[:, widx:widx + 1], wsum[:, widx:widx + 1],
                        acc[:, sl:sl + 1], OP.add), "V")
                    kb.guard_set(("acc", sl), "V", vacc)
                    yield
                vws = kb.cnt["V"]
                kb.wait("tensor", "V", vws)
                kb.guard_wait("tensor", ("bank", 5))
                vmm = d("tensor", lambda e, widx=widx: e.matmul(
                    ps[5][0:1, 0:1], wsum[:, widx:widx + 1], onescolF[:, :],
                    start=True, stop=True), "T")
                kb.wait("vector", "T", vmm)
                d("vector", lambda e, widx=widx: e.tensor_scalar(
                    ws_row[:, widx:widx + 1], ps[5][0:1, 0:1],
                    1.0 / float(D * II), EPS, OP.mult, OP.add), "V")
                d("vector", lambda e, widx=widx: e.reciprocal(
                    ws_row[:, 4 + widx:5 + widx], ws_row[:, widx:widx + 1]), "V")
                vrw = kb.cnt["V"]
                kb.guard_set(("bank", 5), "V", vrw)
                kb.wait("tensor", "V", vrw)
                kb.guard_wait("tensor", ("bank", 4))
                d("tensor", lambda e, widx=widx: e.matmul(
                    ps[4][:, 0:1], onesrow[:, 0:128], ws_row[:, widx:widx + 1],
                    start=True, stop=True), None)
                vb = d("tensor", lambda e, widx=widx: e.matmul(
                    ps[4][:, 1:2], onesrow[:, 0:128], ws_row[:, 4 + widx:5 + widx],
                    start=True, stop=True, skip_group_check=True), "T")
                kb.wait("vector", "T", vb)
                d("vector", lambda e, widx=widx: e.tensor_copy(
                    wsr_c[:, widx:widx + 1], ps[4][:, 1:2]), "V")
                d("vector", lambda e, widx=widx: e.tensor_copy(
                    wsr1[widx][:, :], ps[4][:, 1:2]), "V")
                vcc2 = d("vector", lambda e, widx=widx: e.tensor_copy(
                    sgcol[:, 32 + widx:33 + widx] if False else wsum[:, 4 + widx:5 + widx],
                    ps[4][:, 0:1]), "V")
                kb.guard_set(("bank", 4), "V", vcc2)

        def p5wd():
            if "dmp_ws" in kb.dump:
                kb.wait("sync", "V", kb.cnt["V"])
                kb.dma("sync", dumps["dmp_ws"][:, :], ws_row[:, :], "dws")
            if "dmp_wsr" in kb.dump:
                kb.wait("sync", "V", kb.cnt["V"])
                kb.dma("sync", dumps["dmp_wsr"][:, 0:4], wsr_c[:, :], "dwsr")
                kb.dma("sync", dumps["dmp_wsr"][:, 4:8], wsum[:, 4:8], "dwsr2")

        def p65_gen(E65):
            wdf = E65(nc.sbuf_tensor("wdf", [128, 2048], F32))
            wqs = E65(nc.sbuf_tensor("wqs", [128, 2048], BF16))
            yield
            for pg in range(NIC):
                sl = pg % 2
                kb.guard_wait("gpsimd", ("wdf",))
                vw = kb.dma("gpsimd", wdf[:, :],
                            hap(wdT, (128 * pg) * D, [[D, 128], [1, D]]),
                            f"wd{sl}")
                kb.dwait("vector", vw)
                kb.guard_wait("vector", ("wqs",))
                for hf in range(2):
                    kb.guard_wait("vector", ("i32c",))
                    d("vector", lambda e, sl=sl, hf=hf: e.tensor_scalar(
                        tf32c[:, :], wdf[:, 1024 * hf:1024 * (hf + 1)],
                        wsr1[2][:, :], MAGIC, OP.mult, OP.add), "V")
                    d("vector", lambda e: e.tensor_scalar(
                        i32c[:, :], tf32c[:, :], -MAGIC, None, OP.add), "V")
                    vq = d("vector", lambda e, hf=hf: e.tensor_scalar(
                        wqs[:, 1024 * hf:1024 * (hf + 1)], i32c[:, :],
                        -1.0, 1.0, OP.max, OP.min), "V")
                    kb.guards[("i32c",)] = [("V", vq)]
                kb.guard_set(("wdf",), "V", vq)
                kb.wait("sync", "V", vq)
                vsp = kb.dma("sync",
                             hap(sp_wqd, pg * (128 * 128),
                                 [[128, 128], [NIC * 16384, ND], [1, 128]]),
                             wqs[:, :].rearrange("p (m c) -> p m c", c=128),
                             f"wqs{sl}")
                kb.guard_set(("wqs",), *vsp)
                yield

        def make_p6pre(E):
            wf32 = E(nc.sbuf_tensor("wf32", [128, 2048], F32))
            wqgbf = E(nc.sbuf_tensor("wqgbf", [128, 2 * ND * 512], BF16))
            wqubf = E(nc.sbuf_tensor("wqubf", [128, 2 * ND * 512], BF16))
            vq_ic = {}
            v_sg = {}

            def emit_piece(ic, pidx):
                wsel, qp = divmod(pidx, 4)
                wt = wgT if wsel == 0 else wuT
                wqb = wqgbf if wsel == 0 else wqubf
                icsl = ic % 2
                kb.guard_wait("gpsimd", ("wf32",))
                vw = kb.dma("gpsimd", wf32[:, :].rearrange("p (k c) -> p k c", c=128),
                            hap(wt, 512 * ic + 128 * qp,
                                [[II, 128], [128 * II, ND], [1, 128]]),
                            f"wf{pidx % 2}")
                kb.dwait("vector", vw)
                if pidx == 0:
                    kb.guard_wait("vector", ("wq", icsl))
                for hf in range(2):
                    kb.guard_wait("vector", ("i32c",))
                    d("vector", lambda e, wsel=wsel, hf=hf: e.tensor_scalar(
                        tf32c[:, :], wf32[:, 1024 * hf:1024 * (hf + 1)],
                        wsr1[wsel][:, :], MAGIC, OP.mult, OP.add), "V")
                    d("vector", lambda e: e.tensor_scalar(
                        i32c[:, :], tf32c[:, :], -MAGIC, None, OP.add), "V")
                    vq = d("vector", lambda e, wqb=wqb, qp=qp, icsl=icsl, hf=hf: e.tensor_scalar(
                        wqb[:, icsl * ND * 512 + hf * 8 * 512:icsl * ND * 512 + hf * 8 * 512 + 8 * 512]
                        .rearrange("p (k c) -> p k c", c=512)[:, :, 128 * qp:128 * (qp + 1)],
                        i32c[:, :].rearrange("p (k c) -> p k c", c=128),
                        -1.0, 1.0, OP.max, OP.min), "V")
                    kb.guards[("i32c",)] = [("V", vq)]
                kb.guard_set(("wf32",), "V", vq)
                return vq

            def gen():
                d("vector", lambda e: e.memset(gmax[:, :], 0.0), "V")
                yield
                for k in range(NT):
                    if k < 8:
                        vq_ic[0] = emit_piece(0, k)
                    d("vector", lambda e, k=k: e.tensor_tensor(
                        sgcol[:, k:k + 1], s_all[:, k:k + 1],
                        wsum[:, 4:5], OP.mult), "V")
                    d("vector", lambda e, k=k: e.tensor_tensor(
                        sgcol[:, NT + k:NT + k + 1], s_all[:, k:k + 1],
                        wsum[:, 5:6], OP.mult), "V")
                    v_sg[k] = kb.cnt["V"]
                    yield

            return gen(), emit_piece, vq_ic, v_sg, wqgbf, wqubf

        def p6(E6, h2qfm, p6x):
            emit_piece, vq_ic, v_sg, wqgbf, wqubf = p6x
            sgb = E6(nc.sbuf_tensor("sgb", [128, 2 * 512], BF16))
            gub = E6(nc.sbuf_tensor("gub", [128, 2 * 512], BF16))
            mx2 = E6(nc.sbuf_tensor("mx2", [128, 2], F32))
            g65 = p65_gen(E6)
            next(g65)

            def tick65(n=1):
                for _ in range(n):
                    if next(g65, "END") == "END":
                        return

            for ic in range(NI):
                icsl = ic % 2
                kb.wait("tensor", "V", vq_ic[ic])
                for tt in range(NT):
                    gbank = tt % 2
                    ubank = 2 + tt % 2
                    kb.guard_wait("tensor", ("h2qfm", tt // 4))
                    kb.guard_wait("tensor", ("bank", gbank))
                    for k in range(ND):
                        d("tensor", lambda e, k=k, tt=tt, gbank=gbank, icsl=icsl: e.matmul(
                            ps[gbank][:, 0:512],
                            h2qfm[:, k * T + 128 * tt:k * T + 128 * (tt + 1)],
                            wqgbf[:, icsl * ND * 512 + 512 * k:icsl * ND * 512 + 512 * (k + 1)],
                            start=(k == 0), stop=(k == ND - 1)),
                            "T" if k == ND - 1 else None)
                    vtg = kb.cnt["T"]
                    kb.guard_wait("tensor", ("bank", ubank))
                    for k in range(ND):
                        d("tensor", lambda e, k=k, tt=tt, ubank=ubank, icsl=icsl: e.matmul(
                            ps[ubank][:, 0:512],
                            h2qfm[:, k * T + 128 * tt:k * T + 128 * (tt + 1)],
                            wqubf[:, icsl * ND * 512 + 512 * k:icsl * ND * 512 + 512 * (k + 1)],
                            start=(k == 0), stop=(k == ND - 1)),
                            "T" if k == ND - 1 else None)
                    vtu = kb.cnt["T"]
                    if tt == NT - 1:
                        kb.guard_set(("wq", icsl), "T", vtu)
                    kb.wait("scalar", "T", vtg)
                    kb.wait("scalar", "V", v_sg[tt])
                    kb.guard_wait("scalar", ("sgb", tt % 2))
                    vsg = d("scalar", lambda e, tt=tt, gbank=gbank: e.activation(
                        sgb[:, 512 * (tt % 2):512 * (tt % 2 + 1)], ps[gbank][:, 0:512],
                        AF.Silu, bias=0.0, scale=sgcol[:, tt:tt + 1]), "A")
                    kb.guard_set(("bank", gbank), "A", vsg)
                    kb.wait("vector", "T", vtu)
                    kb.wait("vector", "A", vsg)
                    kb.guard_wait("vector", ("gub", tt % 2))
                    vgu = d("vector", lambda e, tt=tt, ubank=ubank: e.scalar_tensor_tensor(
                        gub[:, 512 * (tt % 2):512 * (tt % 2 + 1)], ps[ubank][:, 0:512],
                        sgcol[:, NT + tt:NT + tt + 1],
                        sgb[:, 512 * (tt % 2):512 * (tt % 2 + 1)],
                        OP.mult, OP.mult), "V")
                    kb.guard_set(("bank", ubank), "V", vgu)
                    kb.guard_set(("sgb", tt % 2), "V", vgu)
                    d("vector", lambda e, tt=tt: e.tensor_reduce(
                        mx2[:, tt % 2:tt % 2 + 1], gub[:, 512 * (tt % 2):512 * (tt % 2 + 1)],
                        AX.X, OP.max, apply_absolute_value=True), "V")
                    vmx = d("vector", lambda e, tt=tt: e.tensor_tensor(
                        gmax[:, tt:tt + 1], gmax[:, tt:tt + 1],
                        mx2[:, tt % 2:tt % 2 + 1], OP.max), "V")
                    kb.wait("sync", "V", vgu)
                    vsp = kb.dma("sync",
                                 hap(sp_gu, (128 * tt) * II + 512 * ic, [[II, 128], [1, 512]]),
                                 gub[:, 512 * (tt % 2):512 * (tt % 2 + 1)], f"gu{tt % 2}")
                    kb.guard_set(("gub", tt % 2), *vsp)
                    if ic + 1 < NI and tt % 2 == 0:
                        vq_ic[ic + 1] = emit_piece(ic + 1, tt // 2)
                    if tt in (1, 5, 9, 13):
                        tick65()
            for _ in range(NIC):
                if next(g65, "END") == "END":
                    break

        def p78(E8):
            gut = E8(nc.sbuf_tensor("gut", [128, II], BF16))
            i32g = E8(nc.sbuf_tensor("i32g", [128, 512], I32))
            tf32g = E8(nc.sbuf_tensor("tf32g", [128, 512], F32))

            s2c = E8(nc.sbuf_tensor("s2c", [128, NT], F32))
            guqf = E8(nc.sbuf_tensor("guqf", [128, NIC * 512], BF16))
            wdb = E8(nc.sbuf_tensor("wdb", [128, 3 * NIC * 128], BF16))
            x1qb = E8(nc.sbuf_tensor("x1qb", [128, ND * 512], F32))
            xs2row = E8(nc.sbuf_tensor("xs2row", [1, T], F32))
            xsb = E8(nc.sbuf_tensor("xsb", [128, 512], F32))
            t8 = E8(nc.sbuf_tensor("t8", [128, 512], F32))
            out8 = E8(nc.sbuf_tensor("out8", [128, 2 * 512], F32))

            def p7_tile(tt):
                sl = tt % 2
                kb.guard_wait("sync", ("gut",))
                vg = kb.dma("sync", gut[:, :],
                            hap(sp_gu, (128 * tt) * II, [[II, 128], [1, II]]), f"gut{sl}")
                d("vector", lambda e, tt=tt: e.tensor_scalar(
                    s2c[:, tt:tt + 1], gmax[:, tt:tt + 1], 1.0 / 127.0, EPS,
                    OP.mult, OP.add), "V")
                d("vector", lambda e, tt=tt: e.reciprocal(
                    s2q[tt % 2][:, :], s2c[:, tt:tt + 1]), "V")
                kb.dwait("vector", vg)
                kb.guard_wait("vector", ("guq",))
                for half in range(16):
                    d("vector", lambda e, tt=tt, half=half: e.tensor_scalar(
                        tf32g[:, :], gut[:, 512 * half:512 * (half + 1)],
                        s2q[tt % 2][:, :], MAGIC, OP.mult, OP.add), "V")
                    d("vector", lambda e: e.tensor_scalar(
                        i32g[:, :], tf32g[:, :], -MAGIC, None, OP.add), "V")
                    d("vector", lambda e, half=half: e.tensor_scalar(
                        gut[:, 512 * half:512 * (half + 1)], i32g[:, :],
                        -129.0, 129.0, OP.max, OP.min), "V")
                vq = kb.cnt["V"]
                kb.guard_set(("gut",), "V", vq)
                kb.wait("sync", "V", vq)
                vsp = kb.dma("sync", hap(sp_guq, (128 * tt) * II, [[II, 128], [1, II]]),
                             gut[:, :], f"guq{tt % 2}")
                kb.guard_set(("guq",), *vsp)
                kb.guard_add(("gut",), *vsp)
                sp7[tt // 4][vsp[0]] = vsp

            def xs2_prep(seg):
                kb.wait("tensor", "V", kb.cnt["V"])
                kb.guard_wait("tensor", ("bank", 5))
                vtr = d("tensor", lambda e, seg=seg: e.transpose(
                    ps[5][0:4, 0:128], s2c[:, 4 * seg:4 * seg + 4], identF[:, :]), "T")
                kb.wait("vector", "T", vtr)
                kb.guard_wait("vector", ("stgf", 0))
                vev = d("vector", lambda e: e.tensor_copy(
                    stg_f[0:4, 0:128], ps[5][0:4, 0:128]), "V")
                kb.guard_set(("bank", 5), "V", vev)
                kb.wait("sync", "V", vev)
                for v_ in gq_last.values():
                    kb.dwait("sync", v_)
                vdm = kb.dma("sync",
                             xs2row[:, 512 * seg:512 * (seg + 1)]
                             .rearrange("o (r c) -> o r c", c=128),
                             stg_f[0:4, 0:128], "xs2g")
                kb.guard_set(("stgf", 0), *vdm)
                kb.dwait("vector", vdm)
                vxr = d("vector", lambda e, seg=seg: e.tensor_scalar(
                    xs2row[:, 512 * seg:512 * (seg + 1)],
                    xs2row[:, 512 * seg:512 * (seg + 1)],
                    ws_row[:, 2:3], None, OP.mult), "V")
                xs2v[seg] = vxr
                xs2dma[seg] = vdm

            sp7 = [dict() for _ in range(NQ)]
            xs2v = {}
            xs2dma = {}
            gq_last = {}
            for tt in range(4):
                p7_tile(tt)
            xs2_prep(0)

            for seg in range(NQ):
                kb.guard_wait("scalar", ("x1qb",))
                vx1 = kb.dma("scalar", x1qb[:, :].rearrange("p (m t) -> p m t", t=512),
                             hap(sp_x1, 512 * seg, [[T, 128], [128 * T, ND], [1, 512]]),
                             "x1q")
                kb.dwait("sync", xs2dma[seg])
                kb.dwait("scalar", xs2dma[seg])
                for v_ in sp7[seg].values():
                    kb.dwait("sync", v_)
                    kb.dwait("scalar", v_)
                kb.guard_wait("sync", ("guqf",))
                kb.guard_wait("scalar", ("guqf",))
                for ic in range(NIC):
                    gq_last[ic % 4] = kb.dma("sync",
                                             guqf[:, ic * 512:(ic + 1) * 512],
                                             hap(sp_guq, (512 * seg) * II + 128 * ic,
                                                 [[II, 512], [1, 128]]),
                                             f"gq{ic % 4}", transpose=True)
                kb.wait("tensor", "V", xs2v[seg])
                kb.guard_wait("tensor", ("bank", 5))
                vb3 = d("tensor", lambda e, seg=seg: e.matmul(
                    ps[5][:, 0:512], onesrow[:, 0:128],
                    xs2row[:, 512 * seg:512 * (seg + 1)], start=True, stop=True), "T")
                kb.wait("vector", "T", vb3)
                kb.guard_wait("vector", ("xsb",))
                vxsb = d("vector", lambda e: e.tensor_scalar(
                    xsb[:, :], ps[5][:, 0:512], 1.0, None, OP.mult), "V")
                kb.guard_set(("bank", 5), "V", vxsb)

                for m in range(ND):
                    msl = (seg * ND + m) % 3
                    kb.guard_wait("gpsimd", ("wdb", msl))
                    vwd = kb.dma("gpsimd", wdb[:, msl * NIC * 128:(msl + 1) * NIC * 128],
                                 hap(sp_wqd, m * (NIC * 128 * 128),
                                     [[128, 128], [128 * 128, NIC], [1, 128]]),
                                 f"wdb{msl}")
                    bank = m % 2
                    for v_ in gq_last.values():
                        kb.dwait("tensor", v_)
                    kb.dwait("tensor", vwd)
                    kb.guard_wait("tensor", ("bank", bank))
                    for ic in range(NIC):
                        d("tensor", lambda e, ic=ic, msl=msl, bank=bank: e.matmul(
                            ps[bank][:, 0:512],
                            wdb[:, msl * NIC * 128 + 128 * ic:msl * NIC * 128 + 128 * (ic + 1)],
                            guqf[:, ic * 512:(ic + 1) * 512],
                            start=(ic == 0), stop=(ic == NIC - 1)),
                            "T" if ic == NIC - 1 else None)
                    vT = kb.cnt["T"]
                    kb.guard_set(("wdb", msl), "T", vT)
                    kb.wait("vector", "T", vT)
                    kb.dwait("vector", vx1)
                    d("vector", lambda e, bank=bank: e.tensor_tensor(
                        t8[:, :], ps[bank][:, 0:512],
                        xsb[:, :], OP.mult), "V")
                    osl = m % 2
                    kb.guard_wait("vector", ("out8", osl))
                    vout = d("vector", lambda e, m=m, osl=osl: e.scalar_tensor_tensor(
                        out8[:, 512 * osl:512 * (osl + 1)], t8[:, :],
                        dcol("gb2", m), x1qb[:, 512 * m:512 * (m + 1)],
                        OP.mult, OP.add), "V")
                    kb.guard_set(("bank", bank), "V", vout)
                    kb.wait("sync", "V", vout)
                    vy = kb.dma("sync",
                                hap(y_fm, (128 * m) * T + 512 * seg, [[T, 128], [1, 512]]),
                                out8[:, 512 * osl:512 * (osl + 1)], f"y{m % 4}")
                    kb.guard_set(("out8", osl), *vy)
                    if seg + 1 < NQ:
                        if m in (2, 5, 8, 11):
                            p7_tile(4 * (seg + 1) + (m - 2) // 3)
                        elif m == 13:
                            xs2_prep(seg + 1)
                kb.guard_set(("guqf",), "T", kb.cnt["T"])
                kb.guard_set(("x1qb",), "V", kb.cnt["V"])
                kb.guard_set(("xsb",), "V", kb.cnt["V"])

        with ExitStack() as s0:
            p0(s0.enter_context)
        kb.barrier()
        with ExitStack() as s12:
            hbuf[0] = s12.enter_context(nc.sbuf_tensor("h_all", [128, ND * T], BF16))
            g5w = p5w_gen(s12.enter_context)
            next(g5w)

            def tick5w(n=1):
                for _ in range(n):
                    if next(g5w, "END") == "END":
                        return
            kb.tick5w = tick5w
            with ExitStack() as s1:
                p1(s1.enter_context)
            kb.barrier()
            if stop_after != "p1":
                with ExitStack() as s2:
                    p2(s2.enter_context)
                for _ in range(120):
                    if next(g5w, "END") == "END":
                        break
                kb.barrier()
        if stop_after not in ("p1", "p2"):
            with ExitStack() as s3:
                p3(s3.enter_context)
            kb.barrier()
        if stop_after not in ("p1", "p2", "p3"):
            with ExitStack() as s4:
                p4(s4.enter_context)
            kb.barrier()
            with ExitStack() as s56:
                h2qfm = s56.enter_context(nc.sbuf_tensor("h2qfm", [128, ND * T], BF16))
                g6pre, ep6, vq6, vsg6, wqg6, wqu6 = make_p6pre(s56.enter_context)
                next(g6pre)

                def tick6(n=1):
                    for _ in range(n):
                        if next(g6pre, "END") == "END":
                            return
                kb.tick6 = tick6
                p5(s56.enter_context, h2qfm)
                for _ in range(NT):
                    if next(g6pre, "END") == "END":
                        break
                if kb.dump:
                    kb.barrier()
                    p5wd()
                    kb.barrier()
                with ExitStack() as s6:
                    p6(s6.enter_context, h2qfm, (ep6, vq6, vsg6, wqg6, wqu6))
                kb.barrier()
            with ExitStack() as s8:
                p78(s8.enter_context)
            kb.barrier()
        else:
            zz = E(nc.sbuf_tensor("zz", [128, 512], F32))
            d("vector", lambda e: e.memset(zz[:, :], 0.0), "V")
            kb.wait("sync", "V", kb.cnt["V"])
            for m in range(ND):
                for qq in range(NQ):
                    kb.dma("sync", y_fm[128 * m:128 * (m + 1), 512 * qq:512 * (qq + 1)],
                           zz[:, :], "out")
            kb.barrier()

        with ExitStack() as sctx:
            for nm in kb.sem_names:
                kb.S[nm] = sctx.enter_context(nc.semaphore(nm))
            with nc.Block() as block:
                for eng in ENGINES:
                    if not kb.prog[eng]:
                        continue

                    def mk(engname):
                        def body(e):
                            for f in kb.prog[engname]:
                                f(e)
                        return body
                    getattr(block, eng)(mk(eng))

    return nc


def make_in_maps(inputs):
    xs = np.asarray(inputs["L0_attn_x"], np.float32)
    shared = {
        "wmod": np.asarray(inputs["L0_wmod"], np.float32),
        "wq": np.asarray(inputs["L0_attn_Wq"], np.float32),
        "wk": np.asarray(inputs["L0_attn_Wk"], np.float32),
        "wv": np.asarray(inputs["L0_attn_Wv"], np.float32),
        "wg": np.asarray(inputs["L0_attn_Wg"], np.float32),
        "wo": np.asarray(inputs["L0_attn_Wo"], np.float32),
        "wgk1": np.asarray(inputs["L0_attn_Wgk1"], np.float32),
        "wgk2": np.asarray(inputs["L0_attn_Wgk2"], np.float32),
        "bgk2": np.asarray(inputs["L0_attn_bgk2"], np.float32).reshape(1, D),
        "gnw": np.asarray(inputs["L0_attn_gnorm_w"], np.float32).reshape(1, HD),
        "wgT": np.ascontiguousarray(np.asarray(inputs["L0_ffn_w_gate"], np.float32).T),
        "wuT": np.ascontiguousarray(np.asarray(inputs["L0_ffn_w_up"], np.float32).T),
        "wdT": np.ascontiguousarray(np.asarray(inputs["L0_ffn_w_down"], np.float32).T),
    }
    in_maps = []
    for b in range(8):
        m = dict(shared)
        m["x_fm"] = np.ascontiguousarray(xs[b].T)
        m["c_row"] = np.ascontiguousarray(
            np.asarray(inputs["L0_c"], np.float32)[b:b + 1, :])
        in_maps.append(m)
    return in_maps


def kernel(**inputs):
    in_maps = make_in_maps(inputs)
    nc = build_nc()
    res = run_bass_kernel_spmd(nc, in_maps, core_ids=list(range(8)))
    out = np.empty((8, T, D), np.float32)
    for b in range(8):
        out[b] = res.results[b]["y_fm"].T
    return out
